# version 19
# baseline (speedup 1.0000x reference)
import numpy as np
from contextlib import ExitStack

import concourse.bass as bass
import concourse.mybir as mybir
from concourse.bass_utils import run_bass_kernel_spmd

F32 = mybir.dt.float32
BF16 = mybir.dt.bfloat16
AF = mybir.ActivationFunctionType
OP = mybir.AluOpType
AX = mybir.AxisListType

M32 = float(np.float32(1.5 * 2 ** 23))
M64 = float(np.float32(1.5 * 2 ** 52))
EPS = 1e-5

L, DM, DI, NST, HID = 2048, 1024, 2048, 16, 4096
QM, QI, QH = DM // 128, DI // 128, HID // 128
BL = 2
TA, TB = 512, 256
NTA, NTB = L // TA, L // TB


class Buf:
    __slots__ = ("w", "r")

    def __init__(self):
        self.w = {}
        self.r = {}


def _merge(d, tok):
    sk, v = tok
    if d.get(sk, 0) < v:
        d[sk] = v


NQPOOL = 32


class Bld:
    def __init__(self, nc, es):
        self.nc = nc
        self.sem = {}
        self.cnt = {}
        for name in ("v", "s", "t"):
            self.sem[name] = es.enter_context(nc.semaphore("sem_" + name))
            self.cnt[name] = 0
        for i in range(NQPOOL):
            self.sem[f"q{i}"] = es.enter_context(nc.semaphore(f"sem_q{i}"))
            self.cnt[f"q{i}"] = 0
        self.qi = 0
        self.eng = dict(v=nc.vector, s=nc.scalar, t=nc.tensor, y=nc.sync)
        self.wm = {e: {} for e in ("v", "s", "t", "y")}

    def _wait(self, ekey, bufs_r, bufs_w):
        need = {}
        for b in bufs_r:
            for sk, v in b.w.items():
                if v > need.get(sk, 0):
                    need[sk] = v
        for b in bufs_w:
            for d in (b.w, b.r):
                for sk, v in d.items():
                    if v > need.get(sk, 0):
                        need[sk] = v
        eng = self.eng[ekey]
        wm = self.wm[ekey]
        for sk, val in need.items():
            if wm.get(sk, 0) < val:
                eng.wait_ge(self.sem[sk], val)
                wm[sk] = val

    def op(self, ekey, fn, reads=(), writes=()):
        self._wait(ekey, reads, writes)
        instr = fn(self.eng[ekey])
        self.cnt[ekey] += 1
        instr.then_inc(self.sem[ekey], 1)
        tok = (ekey, self.cnt[ekey])
        for b in writes:
            _merge(b.w, tok)
        for b in reads:
            _merge(b.r, tok)
        return tok

    def dma(self, fn, reads=(), writes=()):
        sk = f"q{self.qi}"
        self.qi = (self.qi + 1) % NQPOOL
        prior = Buf()
        prior.w[sk] = self.cnt[sk]
        self._wait("y", list(reads) + [prior], writes)
        fn(self.eng["y"]).then_inc(self.sem[sk], 16)
        self.cnt[sk] += 16
        tok = (sk, self.cnt[sk])
        for b in writes:
            _merge(b.w, tok)
        for b in reads:
            _merge(b.r, tok)
        return tok

    def barrier(self):
        fake = Buf()
        for k in self.cnt:
            fake.w[k] = self.cnt[k]
        for e in ("v", "s", "t", "y"):
            self._wait(e, [fake], [])


class GAcc:

    def __init__(self, bld, psum_ap, psum_buf, n):
        self.b, self.ap, self.buf, self.n = bld, psum_ap, psum_buf, n
        self.i = 0
        self.reads = []

    def add(self, lt, rh, reads):
        reads = list(reads)
        first, last = self.i == 0, self.i == self.n - 1
        self.b._wait("t", reads, [self.buf] if first else [])
        ins = self.b.eng["t"].matmul(self.ap, lt, rh, start=first, stop=last)
        self.i += 1
        self.b.cnt["t"] += 1
        ins.then_inc(self.b.sem["t"], 1)
        tok = ("t", self.b.cnt["t"])
        for r in reads:
            _merge(r.r, tok)
        if last:
            _merge(self.buf.w, tok)
            return tok
        return None


def _host_prep(inputs):
    g = {}
    x = np.asarray(inputs["L0_attn_x"], dtype=np.float32)
    g["xT"] = np.ascontiguousarray(x.transpose(0, 2, 1))

    def tiles(wT, nk, nm):
        K, M = wT.shape
        a = wT.reshape(nk, 128, nm, 128)
        return np.ascontiguousarray(a.transpose(2, 0, 1, 3))

    w_in_T = np.ascontiguousarray(np.asarray(inputs["L0_attn_in_proj_w"], np.float32).T)
    w_out_T = np.ascontiguousarray(np.asarray(inputs["L0_attn_out_proj_w"], np.float32).T)
    w_xp_T = np.ascontiguousarray(np.asarray(inputs["L0_attn_x_proj_w"], np.float32).T)
    w1_T = np.ascontiguousarray(np.asarray(inputs["L0_ffn_w1"], np.float32).T)
    w2_T = np.ascontiguousarray(np.asarray(inputs["L0_ffn_w2"], np.float32).T)

    g["w_in"] = tiles(w_in_T, QM, QM * 4)
    g["w_out"] = tiles(w_out_T, QI, QM)
    g["w1"] = tiles(w1_T, QM, QH)
    g["w2"] = tiles(w2_T, QH, QM)
    g["w_xp"] = np.ascontiguousarray(w_xp_T.reshape(QI, 128, 33).transpose(1, 0, 2))

    def chunkvec(v, q):
        return np.ascontiguousarray(np.asarray(v, np.float32).reshape(q, 128).T)

    g["conv_w"] = np.ascontiguousarray(
        np.asarray(inputs["L0_attn_conv_w"], np.float32).reshape(DI, 4)
        .reshape(QI, 128, 4).transpose(1, 0, 2).reshape(128, QI * 4))
    g["conv_b"] = chunkvec(inputs["L0_attn_conv_b"], QI)
    g["dt_w"] = chunkvec(np.asarray(inputs["L0_attn_dt_proj_w"], np.float32).reshape(DI), QI)
    g["dt_b"] = chunkvec(inputs["L0_attn_dt_proj_b"], QI)
    g["a_log"] = np.ascontiguousarray(
        np.asarray(inputs["L0_attn_A_log"], np.float32)
        .reshape(QI, 128, NST).transpose(1, 0, 2).reshape(128, QI * NST))
    g["dvec"] = chunkvec(inputs["L0_attn_D"], QI)
    g["b1"] = chunkvec(inputs["L0_ffn_b1"], QH)
    g["b2"] = chunkvec(inputs["L0_ffn_b2"], QM)
    g["gamma"] = chunkvec(inputs["L0_ffn_gamma"], QM)
    g["beta"] = chunkvec(inputs["L0_ffn_beta"], QM)
    g["ones_row"] = np.ones((1, 128), np.float32)
    g["konst"] = np.array([[EPS, -0.5, 1.0 + EPS, EPS * EPS]], np.float32)
    sel = np.zeros((33, 33 * 128), np.float32)
    for r in range(33):
        sel[r, r * 128:(r + 1) * 128] = 1.0
    g["selmat"] = sel
    g["ones_col"] = np.ones((128, 1), np.float32)
    g["eye"] = np.eye(128, dtype=np.float32)
    return g


def build_nc(dbg=False):
    nc = bass.Bass("TRN2", target_bir_lowering=False)
    es = ExitStack()
    b = Bld(nc, es)

    dr = {}
    for nm, sh in [("x", (BL, DM, L)), ("w_in", (QM * 4, QM, 128, 128)),
                   ("w_out", (QM, QI, 128, 128)), ("w1", (QH, QM, 128, 128)),
                   ("w2", (QM, QH, 128, 128)), ("w_xp", (128, QI, 33)),
                   ("conv_w", (128, QI * 4)), ("conv_b", (128, QI)),
                   ("dt_w", (128, QI)), ("dt_b", (128, QI)),
                   ("a_log", (128, QI * NST)), ("dvec", (128, QI)),
                   ("b1", (128, QH)), ("b2", (128, QM)),
                   ("gamma", (128, QM)), ("beta", (128, QM)),
                   ("ones_row", (1, 128)), ("ones_col", (128, 1)),
                   ("eye", (128, 128)), ("konst", (1, 4)),
                   ("selmat", (33, 33 * 128))]:
        dr[nm] = nc.dram_tensor(nm, list(sh), F32, kind="ExternalInput").ap()
    x_d = dr["x"]
    out_d = nc.dram_tensor("out", [BL, DM, L], F32, kind="ExternalOutput").ap()

    wbf_in_d = nc.dram_tensor("wbf_in", [QM * 4, 128, QM * 128], BF16, kind="Internal").ap()
    wbf_out_d = nc.dram_tensor("wbf_out", [QM, 128, QI * 128], BF16, kind="Internal").ap()
    wq1_d = nc.dram_tensor("wq1", [QH, 128, QM * 128], BF16, kind="Internal").ap()
    wq2_d = nc.dram_tensor("wq2", [QM, 128, QH * 128], BF16, kind="Internal").ap()
    x1_d = nc.dram_tensor("x1h", [BL, DM, L], BF16, kind="ExternalOutput").ap()
    xc_d = nc.dram_tensor("xch", [BL, DI, L], BF16, kind="ExternalOutput").ap()
    sz_d = nc.dram_tensor("szh", [BL, DI, L], BF16, kind="ExternalOutput").ap()

    def tap(name, sb_ap, shape, reads, dt=F32):
        if not dbg:
            return
        d = nc.dram_tensor("dbg_" + name, list(shape), dt, kind="ExternalOutput").ap()
        b.dma(lambda y: y.dma_start(out=d, in_=sb_ap), reads=list(reads), writes=[Buf()])

    def sb(name, shape, dt=F32):
        t = es.enter_context(nc.sbuf_tensor(name, list(shape), dt))
        return t, Buf()

    cvw, cvw_b = sb("cvw", (128, QI * 4))
    cvb, cvb_b = sb("cvb", (128, QI))
    dtw, dtw_b = sb("dtw", (128, QI))
    dtb, dtb_b = sb("dtb", (128, QI))
    alog, alog_b = sb("alog", (128, QI * NST))
    Ares, Ares_b = sb("Ares", (128, QI * NST))
    dvv, dvv_b = sb("dvv", (128, QI))
    b1v, b1v_b = sb("b1v", (128, QH))
    b2v, b2v_b = sb("b2v", (128, QM))
    gam, gam_b = sb("gam", (128, QM))
    bet, bet_b = sb("bet", (128, QM))
    nbet, nbet_b = sb("nbet", (128, QM))
    orow, orow_b = sb("orow", (1, 128))
    konst, konst_b = sb("konst_sb", (1, 4))
    selbf, selbf_b = sb("selbf", (33, 33 * 128), BF16)
    ocol, ocol_b = sb("ocol", (128, 1))
    eyef, eyef_b = sb("eyef", (128, 128))
    orow_bf, orow_bf_b = sb("orow_bf", (1, 128), BF16)
    ocol_bf, ocol_bf_b = sb("ocol_bf", (128, 1), BF16)
    eye_bf, eye_bf_b = sb("eye_bf", (128, 128), BF16)
    xpw_bf, xpw_bf_b = sb("xpw_bf", (128, QI, 33), BF16)
    ublend, ublend_b = sb("ublend", (128, 1))
    m32c, m32c_b = sb("m32c", (128, 1))
    nm32c, nm32c_b = sb("nm32c", (128, 1))
    m64c, m64c_b = sb("m64c", (128, 1))
    nm64c, nm64c_b = sb("nm64c", (128, 1))
    ub_scr, ub_scr_b = sb("ub_scr", (128, 1))
    sw1_vec, sw1_vec_b = sb("sw1_vec", (128, 1))
    sw1i_vec, sw1i_vec_b = sb("sw1i_vec", (128, 1))
    sw2_vec, sw2_vec_b = sb("sw2_vec", (128, 1))
    sw2i_vec, sw2i_vec_b = sb("sw2i_vec", (128, 1))
    states, states_b = sb("states", (128, BL * QI * NST))
    tails, tails_b = sb("tails", (128, BL * QI * 3), BF16)

    p0 = ExitStack()

    def sb0(name, shape, dt=F32):
        t = p0.enter_context(nc.sbuf_tensor(name, list(shape), dt))
        return t, Buf()

    for nm, (tt, bb) in [("conv_w", (cvw, cvw_b)), ("conv_b", (cvb, cvb_b)),
                         ("dt_w", (dtw, dtw_b)), ("dt_b", (dtb, dtb_b)),
                         ("a_log", (alog, alog_b)), ("dvec", (dvv, dvv_b)),
                         ("b1", (b1v, b1v_b)), ("b2", (b2v, b2v_b)),
                         ("gamma", (gam, gam_b)), ("beta", (bet, bet_b)),
                         ("ones_row", (orow, orow_b)), ("ones_col", (ocol, ocol_b)),
                         ("eye", (eyef, eyef_b)), ("konst", (konst, konst_b))]:
        b.dma(lambda y, d=dr[nm], t=tt: y.dma_start(out=t[:], in_=d), writes=[bb])

    b.op("v", lambda v: v.tensor_copy(orow_bf[:], orow[:]), reads=[orow_b], writes=[orow_bf_b])
    b.op("v", lambda v: v.tensor_copy(ocol_bf[:], ocol[:]), reads=[ocol_b], writes=[ocol_bf_b])
    b.op("v", lambda v: v.tensor_copy(eye_bf[:], eyef[:]), reads=[eyef_b], writes=[eye_bf_b])
    self_f, self_f_b = sb0("self_f", (33, 33 * 128))
    b.dma(lambda y: y.dma_start(out=self_f[:], in_=dr["selmat"]), writes=[self_f_b])
    b.op("v", lambda v: v.tensor_copy(selbf[:], self_f[:]), reads=[self_f_b], writes=[selbf_b])

    ascr, ascr_b = sb0("ascr", (128, QI * NST))
    b.op("s", lambda s: s.activation(ascr[:], alog[:], AF.Exp), reads=[alog_b], writes=[ascr_b])
    b.op("v", lambda v: v.tensor_scalar(Ares[:], ascr[:], -1.0, None, OP.mult),
         reads=[ascr_b], writes=[Ares_b])

    b.op("v", lambda v: v.tensor_scalar(nbet[:], bet[:], -1.0, None, OP.mult),
         reads=[bet_b], writes=[nbet_b])
    for cv, (ct, cb_) in [(M32, (m32c, m32c_b)), (-M32, (nm32c, nm32c_b)),
                          (M64, (m64c, m64c_b)), (-M64, (nm64c, nm64c_b))]:
        b.op("v", lambda v, cv=cv, ct=ct: v.tensor_scalar(
            ct[:], ocol[:], cv, None, OP.mult), reads=[ocol_b], writes=[cb_])
    b.op("v", lambda v: v.tensor_scalar(ub_scr[:], ocol[:], M64, None, OP.add),
         reads=[ocol_b], writes=[ub_scr_b])
    b.op("v", lambda v: v.tensor_scalar(ublend[:], ub_scr[:], -M64, None, OP.add),
         reads=[ub_scr_b], writes=[ublend_b])

    xpw_f, xpw_f_b = sb0("xpw_f", (128, QI * 33))
    b.dma(lambda y: y.dma_start(out=xpw_f[:], in_=dr["w_xp"].rearrange("p c m -> p (c m)")),
          writes=[xpw_f_b])
    b.op("s", lambda s: s.copy(xpw_bf[:].rearrange("p c m -> p (c m)"), xpw_f[:]),
         reads=[xpw_f_b], writes=[xpw_bf_b])

    def cast_stream(src_d, dst_d, nj, nk, tag):
        fsl = [sb0(f"{tag}_f{i}", (128, nk * 128)) for i in range(3)]
        bsl = [sb0(f"{tag}_b{i}", (128, nk * 128), BF16) for i in range(3)]
        for j in range(nj):
            (ft, fb), (bt, bb2) = fsl[j % 3], bsl[j % 3]
            b.dma(lambda y, j=j, ft=ft: y.dma_start(
                out=ft[:].rearrange("p (k m) -> p k m", m=128),
                in_=src_d[j].rearrange("k p m -> p k m")), writes=[fb])
            b.op("s", lambda s, ft=ft, bt=bt: s.copy(bt[:], ft[:]), reads=[fb], writes=[bb2])
            b.dma(lambda y, j=j, bt=bt: y.dma_start(out=dst_d[j], in_=bt[:]),
                  reads=[bb2])

    cast_stream(dr["w_in"], wbf_in_d, QM * 4, QM, "ci")
    cast_stream(dr["w_out"], wbf_out_d, QM, QI, "co")

    def quant_weights(src_d, dst_d, nj, nk, sw_pair, swi_pair, tag):
        sw_vec, sw_vec_b = sw_pair
        swi_vec, swi_vec_b = swi_pair
        n_el = float(nj * nk * 128 * 128)
        npg = 3 if nk <= 8 else 2
        fsl = [sb0(f"{tag}qf{i}", (128, nk * 128)) for i in range(npg)]
        red, red_b = sb0(f"{tag}red", (128, nj))
        for j in range(nj):
            ft, fb = fsl[j % npg]
            b.dma(lambda y, j=j, ft=ft: y.dma_start(
                out=ft[:].rearrange("p (k m) -> p k m", m=128),
                in_=src_d[j].rearrange("k p m -> p k m")), writes=[fb])
            b.op("v", lambda v, ft=ft, j=j: v.tensor_reduce(
                red[:, j:j + 1], ft[:], AX.X, OP.add, apply_absolute_value=True),
                reads=[fb], writes=[red_b])
        redt, redt_b = sb0(f"{tag}redt", (128, 1))
        red_bf, red_bf_b = sb0(f"{tag}redbf", (128, 1), BF16)
        b.op("v", lambda v: v.tensor_reduce(redt[:], red[:], AX.X, OP.add),
             reads=[red_b], writes=[redt_b])
        b.op("v", lambda v: v.tensor_copy(red_bf[:], redt[:]), reads=[redt_b], writes=[red_bf_b])
        pml = p0.enter_context(nc.psum_tensor(f"{tag}pm", [128, 512], F32))
        pml_b = Buf()
        GAcc(b, pml[0:1, 0:1], pml_b, 1).add(ocol_bf[:], red_bf[:], [red_bf_b, ocol_bf_b])
        sc1, sc1_b = sb0(f"{tag}sc1", (1, 2))
        b.op("v", lambda v: v.tensor_scalar(sc1[:, 0:1], pml[0:1, 0:1], 1.0 / n_el, 1e-5,
                                            OP.mult, OP.max), reads=[pml_b], writes=[sc1_b])
        b.op("v", lambda v: v.reciprocal(sc1[:, 1:2], sc1[:, 0:1]), reads=[sc1_b], writes=[sc1_b])
        sc_bf, sc_bf_b = sb0(f"{tag}scbf", (1, 2), BF16)
        b.op("v", lambda v: v.tensor_copy(sc_bf[:], sc1[:]), reads=[sc1_b], writes=[sc_bf_b])
        for col, (vec, vec_b) in [(0, (swi_vec, swi_vec_b)), (1, (sw_vec, sw_vec_b))]:
            GAcc(b, pml[:, 4 + col:5 + col], pml_b, 1).add(
                orow_bf[:], sc_bf[:, col:col + 1], [sc_bf_b, orow_bf_b])
            b.op("v", lambda v, col=col, vec=vec: v.tensor_copy(
                vec[:], pml[:, 4 + col:5 + col]), reads=[pml_b], writes=[vec_b])
        zs, zs_b = sb0(f"{tag}zs", (128, nk * 128))
        q1, q1_b = sb0(f"{tag}q1", (128, nk * 128), BF16)
        q2, q2_b = sb0(f"{tag}q2", (128, nk * 128), BF16)
        qd, qd_b = sb0(f"{tag}qd", (128, nk * 128), BF16)
        qb, qb_b = sb0(f"{tag}qb", (128, nk * 128), BF16)
        qc, qc_b = sb0(f"{tag}qc", (128, nk * 128), BF16)
        for j in range(nj):
            ft, fb = fsl[j % 2]
            b.dma(lambda y, j=j, ft=ft: y.dma_start(
                out=ft[:].rearrange("p (k m) -> p k m", m=128),
                in_=src_d[j].rearrange("k p m -> p k m")), writes=[fb])
            b.op("v", lambda v, ft=ft: v.tensor_scalar(zs[:], ft[:], sw_vec[:], None, OP.mult),
                 reads=[fb, sw_vec_b], writes=[zs_b])
            b.op("v", lambda v: v.tensor_scalar(q1[:], zs[:], M32, -M32, OP.add, OP.add),
                 reads=[zs_b], writes=[q1_b])
            b.op("v", lambda v: v.tensor_scalar(q2[:], zs[:], M64, -M64, OP.add, OP.add),
                 reads=[zs_b], writes=[q2_b])
            b.op("v", lambda v: v.tensor_tensor(qd[:], q2[:], q1[:], OP.subtract),
                 reads=[q1_b, q2_b], writes=[qd_b])
            b.op("v", lambda v: v.scalar_tensor_tensor(qb[:], qd[:], ublend[:], q1[:],
                                                       OP.mult, OP.add),
                 reads=[qd_b, q1_b, ublend_b], writes=[qb_b])
            b.op("v", lambda v: v.tensor_scalar(qc[:], qb[:], -1.0, 1.0, OP.max, OP.min),
                 reads=[qb_b], writes=[qc_b])
            b.dma(lambda y, j=j: y.dma_start(out=dst_d[j], in_=qc[:]), reads=[qc_b])

    quant_weights(dr["w1"], wq1_d, QH, QM, (sw1_vec, sw1_vec_b), (sw1i_vec, sw1i_vec_b), "a")
    quant_weights(dr["w2"], wq2_d, QM, QH, (sw2_vec, sw2_vec_b), (sw2i_vec, sw2i_vec_b), "b")

    b.op("v", lambda v: v.memset(states[:], 0.0), writes=[states_b])
    b.op("v", lambda v: v.memset(tails[:], 0.0), writes=[tails_b])

    b.barrier()
    p0.close()

    def make_ln(T, pstat1, pstat2, row_sets):
        state = {"i": 0}

        def ln_begin(nq):
            rows, rows_b, lbf, lbf_b, sqp = row_sets[state["i"] % len(row_sets)]
            ap1, buf1 = pstat1
            ap2, buf2 = pstat2
            state["g1"] = GAcc(b, ap1, buf1, nq)
            state["g2"] = GAcc(b, ap2, buf2, nq)
            state["qn"] = 0

        def ln_feed(ap, rds):
            rows, rows_b, lbf, lbf_b, sqp = row_sets[state["i"] % len(row_sets)]
            state["g1"].add(ocol_bf[:], ap, list(rds) + [ocol_bf_b])
            sq = sqp[state["qn"] % 2]
            state["qn"] += 1
            b.op("s", lambda s, ap=ap, sq=sq: s.activation(sq[0][:], ap, AF.Square),
                 reads=rds, writes=[sq[1]])
            state["g2"].add(ocol_bf[:], sq[0][:], [sq[1], ocol_bf_b])

        def ln_finish(nq, fuse_double):
            rows, rows_b, lbf, lbf_b, sqp = row_sets[state["i"] % len(row_sets)]
            R = lambda i: rows[:, i * T:(i + 1) * T]
            H = lambda i: lbf[:, i * T:(i + 1) * T]
            ap1, buf1 = pstat1
            ap2, buf2 = pstat2
            RW = [rows_b]
            b.op("v", lambda v: v.tensor_copy(R(0), ap1), reads=[buf1], writes=RW)
            b.op("v", lambda v: v.tensor_copy(R(1), ap2), reads=[buf2], writes=RW)
            nfeat = float(nq * 128)
            b.op("v", lambda v: v.tensor_scalar(R(2), R(0), 1.0 / nfeat, None, OP.mult),
                 reads=RW, writes=RW)
            b.op("v", lambda v: v.tensor_tensor(R(4), R(2), R(2), OP.mult),
                 reads=RW, writes=RW)
            b.op("v", lambda v: v.scalar_tensor_tensor(R(5), R(1), 1.0 / nfeat, R(4),
                                                       OP.mult, OP.subtract),
                 reads=RW, writes=RW)
            sc1 = (1.0 + EPS) if fuse_double else 1.0
            sc2 = (EPS * EPS) if fuse_double else EPS
            b.op("v", lambda v: v.tensor_scalar(R(1), R(5), sc1, sc2, OP.mult, OP.add),
                 reads=RW, writes=RW)
            b.op("s", lambda s: s.activation(R(3), R(1), AF.Ln), reads=RW, writes=RW)
            b.op("s", lambda s: s.activation(R(0), R(3), AF.Exp, scale=konst[0:1, 1:2]),
                 reads=RW + [konst_b], writes=RW)
            b.op("s", lambda s: s.copy(H(0), R(0)), reads=RW, writes=[lbf_b])
            b.op("v", lambda v: v.tensor_tensor(H(1), R(0), H(0), OP.subtract),
                 reads=RW + [lbf_b], writes=[lbf_b])
            b.op("v", lambda v: v.tensor_tensor(R(3), R(2), R(0), OP.mult),
                 reads=RW, writes=RW)
            b.op("s", lambda s: s.copy(H(2), R(3)), reads=RW, writes=[lbf_b])
            b.op("v", lambda v: v.tensor_tensor(H(3), R(3), H(2), OP.subtract),
                 reads=RW + [lbf_b], writes=[lbf_b])
            state["cur"] = (lbf, lbf_b)

        def ln_stats(src, nq, fuse_double):
            ln_begin(nq)
            for q in range(nq):
                ap, rds = src(q)
                ln_feed(ap, rds)
            ln_finish(nq, fuse_double)

        def bcast(psum_ap, psum_buf, off):
            lbf, lbf_b = state["cur"]
            H = lambda i: lbf[:, i * T:(i + 1) * T]
            g = GAcc(b, psum_ap, psum_buf, 2)
            g.add(orow_bf[:], H(off), [lbf_b, orow_bf_b])
            g.add(orow_bf[:], H(off + 1), [lbf_b, orow_bf_b])

        def next_site():
            state["i"] += 1

        return ln_stats, bcast, next_site, ln_begin, ln_feed, ln_finish

    pA1 = ExitStack()

    def sb1(name, shape, dt=F32):
        t = pA1.enter_context(nc.sbuf_tensor(name, list(shape), dt))
        return t, Buf()

    def ps1(name, shape, dt=F32):
        t = pA1.enter_context(nc.psum_tensor(name, list(shape), dt))
        return t, Buf()

    x0s = [sb1(f"x0_{i}", (128, QM, TA)) for i in range(2)]
    u1s = [sb1(f"u1_{i}", (128, QM, TA), BF16) for i in range(2)]
    xssm = [sb1(f"xssm{i}", (128, TA + 3), BF16) for i in range(2)]
    cvs = [sb1(f"cvs{i}", (128, TA)) for i in range(4)]
    xcts = [sb1(f"xct{i}", (128, QI, TA), BF16) for i in range(2)]
    szts = [sb1(f"szt{i}", (128, QI, TA), BF16) for i in range(2)]
    sqs1 = [sb1(f"sqs1_{i}", (128, TA), BF16) for i in range(2)]
    cbs1 = [sb1(f"cbs1_{i}", (128, TA), BF16) for i in range(2)]
    rows1 = [sb1(f"rows1_{i}", (1, 6 * TA)) for i in range(2)]
    lbf1 = [sb1(f"lbf1_{i}", (1, 4 * TA), BF16) for i in range(2)]
    t1q = [sb1(f"t1q{i}", (128, TA)) for i in range(2)]
    winb = [sb1(f"winb{i}", (128, QM * 128), BF16) for i in range(3)]
    pin = [ps1(f"pin{i}", (128, TA)) for i in range(3)]
    pst1, pst1_b = ps1("pst1", (1, 2 * TA))
    pln1, pln1_b = ps1("pln1", (128, 2 * TA))

    ln_1, bcast_1, next_1, _, _, _ = make_ln(
        TA, (pst1[0:1, 0:TA], pst1_b), (pst1[0:1, TA:2 * TA], pst1_b),
        [(rows1[0][0], rows1[0][1], lbf1[0][0], lbf1[0][1], sqs1),
         (rows1[1][0], rows1[1][1], lbf1[1][0], lbf1[1][1], sqs1)])

    tails_bb = [Buf(), Buf()]
    unitsA = [(bb, t) for t in range(NTA) for bb in range(BL)]
    for ui, (bi, ti) in enumerate(unitsA):
        par = ui % 2
        t0 = ti * TA
        x0, x0_b = x0s[par]
        u1t, u1_b = u1s[par]
        xct, xct_b = xcts[par]
        szt, szt_b = szts[par]
        tl_b = tails_bb[bi]
        b.dma(lambda y, bi=bi, t0=t0, x0=x0: y.dma_start(
            out=x0[:], in_=x_d[bi, :, t0:t0 + TA].rearrange("(q p) t -> p q t", p=128)),
            writes=[x0_b])

        def src_x0(q, x0=x0, x0_b=x0_b):
            cb = cbs1[q % 2]
            b.op("s", lambda s, q=q, cb=cb, x0=x0: s.copy(cb[0][:], x0[:, q, :]),
                 reads=[x0_b], writes=[cb[1]])
            return cb[0][:], [cb[1]]

        ln_1(src_x0, QM, False)
        bcast_1(pln1[:, 0:TA], pln1_b, 0)
        bcast_1(pln1[:, TA:2 * TA], pln1_b, 2)
        next_1()
        for q in range(QM):
            tq = t1q[q % 2]
            b.op("v", lambda v, q=q, tq=tq, x0=x0: v.tensor_tensor(
                tq[0][:], x0[:, q, :], pln1[:, 0:TA], OP.mult),
                reads=[x0_b, pln1_b], writes=[tq[1]])
            b.op("v", lambda v, q=q, tq=tq, u1t=u1t: v.tensor_tensor(
                u1t[:, q, :], tq[0][:], pln1[:, TA:2 * TA], OP.subtract),
                reads=[tq[1], pln1_b], writes=[u1_b])

        for j in range(QM * 4):
            wb = winb[j % 3]
            b.dma(lambda y, j=j, wb=wb: y.dma_start(
                out=wb[0][:], in_=wbf_in_d[j]), writes=[wb[1]])
            pp = pin[j % 3]
            g = GAcc(b, pp[0][:], pp[1], QM)
            for k in range(QM):
                g.add(wb[0][:, k * 128:(k + 1) * 128], u1t[:, k, :], [wb[1], u1_b])
            if j < QI:
                c = j
                xs = xssm[c % 2]
                tix = bi * QI * 3 + c * 3
                b.op("v", lambda v, xs=xs, pp=pp: v.tensor_copy(
                    xs[0][:, 3:3 + TA], pp[0][:]), reads=[pp[1]], writes=[xs[1]])
                b.op("v", lambda v, xs=xs, tix=tix: v.tensor_copy(
                    xs[0][:, 0:3], tails[:, tix:tix + 3]), reads=[tl_b], writes=[xs[1]])
                b.op("v", lambda v, xs=xs, tix=tix: v.tensor_copy(
                    tails[:, tix:tix + 3], xs[0][:, TA:TA + 3]),
                    reads=[xs[1]], writes=[tl_b])
                ca, c2 = cvs[c % 2], cvs[2 + c % 2]
                b.op("v", lambda v, xs=xs, ca=ca, c=c: v.tensor_scalar(
                    ca[0][:], xs[0][:, 0:TA], cvw[:, 4 * c:4 * c + 1], None, OP.mult),
                    reads=[xs[1], cvw_b], writes=[ca[1]])
                b.op("v", lambda v, xs=xs, c2=c2, c=c: v.tensor_scalar(
                    c2[0][:], xs[0][:, 2:2 + TA], cvw[:, 4 * c + 2:4 * c + 3], None, OP.mult),
                    reads=[xs[1], cvw_b], writes=[c2[1]])
                b.op("v", lambda v, xs=xs, ca=ca, c=c: v.scalar_tensor_tensor(
                    ca[0][:], xs[0][:, 1:1 + TA], cvw[:, 4 * c + 1:4 * c + 2], ca[0][:],
                    OP.mult, OP.add), reads=[xs[1], ca[1], cvw_b], writes=[ca[1]])
                b.op("v", lambda v, xs=xs, c2=c2, c=c: v.scalar_tensor_tensor(
                    c2[0][:], xs[0][:, 3:3 + TA], cvw[:, 4 * c + 3:4 * c + 4], c2[0][:],
                    OP.mult, OP.add), reads=[xs[1], c2[1], cvw_b], writes=[c2[1]])
                b.op("v", lambda v, ca=ca, c2=c2: v.tensor_tensor(
                    ca[0][:], ca[0][:], c2[0][:], OP.add),
                    reads=[ca[1], c2[1]], writes=[ca[1]])
                b.op("s", lambda s, ca=ca, c=c, xct=xct: s.activation(
                    xct[:, c, :], ca[0][:], AF.Silu, bias=cvb[:, c:c + 1]),
                    reads=[ca[1], cvb_b], writes=[xct_b])
            else:
                c = j - QI
                b.op("s", lambda s, c=c, pp=pp, szt=szt: s.activation(
                    szt[:, c, :], pp[0][:], AF.Silu), reads=[pp[1]], writes=[szt_b])
        b.dma(lambda y, bi=bi, t0=t0, xct=xct: y.dma_start(
            out=xc_d[bi, :, t0:t0 + TA].rearrange("(c p) t -> p c t", p=128),
            in_=xct[:]), reads=[xct_b])
        b.dma(lambda y, bi=bi, t0=t0, szt=szt: y.dma_start(
            out=sz_d[bi, :, t0:t0 + TA].rearrange("(c p) t -> p c t", p=128),
            in_=szt[:]), reads=[szt_b])

    b.barrier()
    pA1.close()

    pA2 = ExitStack()

    def sb2(name, shape, dt=F32):
        t = pA2.enter_context(nc.sbuf_tensor(name, list(shape), dt))
        return t, Buf()

    def ps2(name, shape, dt=F32):
        t = pA2.enter_context(nc.psum_tensor(name, list(shape), dt))
        return t, Buf()

    x0s2 = [sb2("x0b_0", (128, QM, TA))]
    xcs2 = [sb2(f"xc2_{i}", (128, QI, TA), BF16) for i in range(2)]
    szs2 = [sb2("sz2_0", (128, QI, TA), BF16)]
    xpbf, xpbf_b = sb2("xpbf", (33, TA), BF16)
    rsb, rsb_b = sb2("rsb", (128, TA), BF16)
    dts = [sb2(f"dts{i}", (128, TA), BF16) for i in range(2)]
    scrp = [sb2(f"scrp{i}", (128, TA)) for i in range(4)]
    spE = [sb2(f"spE{i}", (128, TA), BF16) for i in range(2)]
    spR = [sb2(f"spR{i}", (128, TA), BF16) for i in range(2)]
    spR2 = [sb2(f"spR2{i}", (128, TA), BF16) for i in range(2)]
    usc = [sb2(f"usc{i}", (128, TA), BF16) for i in range(2)]
    Bt, Bt_b = sb2("Bt", (128, NST, TA), BF16)
    Ct, Ct_b = sb2("Ct", (128, NST, TA), BF16)
    dAs = [sb2(f"dAs{i}", (128, 2, TA)) for i in range(2)]
    dbx = [sb2(f"dbx{i}", (128, 2, TA), BF16) for i in range(3)]
    hsl = [sb2(f"hsl{i}", (128, 2, TA), BF16) for i in range(3)]
    hcs = [sb2(f"hcs{i}", (128, 2, TA), BF16) for i in range(2)]
    ysc = [sb2(f"ysc{i}", (128, TA)) for i in range(2)]
    ygt, ygt_b = sb2("ygt", (128, QI, TA), BF16)
    mixb, mixb_b = sb2("mixb", (128, QM, TA), BF16)
    sqs2 = [sb2(f"sqs2_{i}", (128, TA), BF16) for i in range(2)]
    rows2 = [sb2("rows2_0", (1, 6 * TA))]
    lbf2 = [sb2("lbf2_0", (1, 4 * TA), BF16)]
    t1q2 = [scrp[0], scrp[1]]
    t2q2 = [scrp[2], scrp[3]]
    x1o = [spR[0], spR[1]]
    woutb = [sb2("woutb0", (128, QI * 128), BF16)]
    pout = [ps2(f"pout{i}", (128, TA)) for i in range(2)]
    pY, pY_b = ps2("pY", (128, TA))
    pst2, pst2_b = ps2("pst2", (1, TA))
    pln2, pln2_b = ps2("pln2", (128, 2 * TA))
    pmisc, pmisc_b = ps2("pmisc", (128, TA))
    pmisc2, pmisc2_b = ps2("pmisc2", (128, TA))

    ln_2, bcast_2, next_2, ln2_begin, ln2_feed, ln2_finish = make_ln(
        TA, (pst2[0:1, 0:TA], pst2_b), (pmisc2[0:1, 0:TA], pmisc2_b),
        [(rows2[0][0], rows2[0][1], lbf2[0][0], lbf2[0][1], sqs2)])

    states_bb = [Buf(), Buf()]
    for ui, (bi, ti) in enumerate(unitsA):
        par = ui % 2
        t0 = ti * TA
        x0, x0_b = x0s2[0]
        xct, xct_b = xcs2[par]
        sluz, sluz_b = szs2[0]
        st_b = states_bb[bi]
        b.dma(lambda y, bi=bi, t0=t0, x0=x0: y.dma_start(
            out=x0[:], in_=x_d[bi, :, t0:t0 + TA].rearrange("(q p) t -> p q t", p=128)),
            writes=[x0_b])
        b.dma(lambda y, bi=bi, t0=t0, xct=xct: y.dma_start(
            out=xct[:], in_=xc_d[bi, :, t0:t0 + TA].rearrange("(c p) t -> p c t", p=128)),
            writes=[xct_b])
        b.dma(lambda y, bi=bi, t0=t0, sluz=sluz: y.dma_start(
            out=sluz[:], in_=sz_d[bi, :, t0:t0 + TA].rearrange("(c p) t -> p c t", p=128)),
            writes=[sluz_b])
        g = GAcc(b, pmisc[0:33, :], pmisc_b, QI)
        for c in range(QI):
            g.add(xpw_bf[:, c, :], xct[:, c, :], [xpw_bf_b, xct_b])
        b.op("v", lambda v: v.tensor_copy(xpbf[:], pmisc[0:33, :]),
             reads=[pmisc_b], writes=[xpbf_b])
        GAcc(b, pmisc[:, :], pmisc_b, 1).add(selbf[:, 0:128], xpbf[:], [xpbf_b, selbf_b])
        b.op("s", lambda s: s.copy(rsb[:], pmisc[:, :]), reads=[pmisc_b], writes=[rsb_b])
        bc_banks = [(pmisc, pmisc_b), (pmisc2, pmisc2_b)]
        for n in range(NST):
            bk, bk_b = bc_banks[0]
            GAcc(b, bk[:, :], bk_b, 1).add(
                selbf[:, (1 + n) * 128:(2 + n) * 128], xpbf[:], [xpbf_b, selbf_b])
            b.op("s", lambda s, n=n, bk=bk: s.copy(Bt[:, n, :], bk[:, :]),
                 reads=[bk_b], writes=[Bt_b])
            bk, bk_b = bc_banks[1]
            GAcc(b, bk[:, :], bk_b, 1).add(
                selbf[:, (17 + n) * 128:(18 + n) * 128], xpbf[:], [xpbf_b, selbf_b])
            b.op("s", lambda s, n=n, bk=bk: s.copy(Ct[:, n, :], bk[:, :]),
                 reads=[bk_b], writes=[Ct_b])

        def softplus_pre(c):
            pre, ab = scrp[c % 2], scrp[2 + c % 2]
            er, rl, ln_ = spE[c % 2], spR[c % 2], spR2[c % 2]
            b.op("v", lambda v, c=c, pre=pre: v.tensor_scalar(
                pre[0][:], rsb[:], dtw[:, c:c + 1], dtb[:, c:c + 1], OP.mult, OP.add),
                reads=[rsb_b, dtw_b, dtb_b], writes=[pre[1]])
            b.op("s", lambda s, pre=pre, rl=rl: s.activation(rl[0][:], pre[0][:], AF.Relu),
                 reads=[pre[1]], writes=[rl[1]])
            b.op("s", lambda s, pre=pre, ab=ab: s.activation(ab[0][:], pre[0][:], AF.Abs),
                 reads=[pre[1]], writes=[ab[1]])
            b.op("s", lambda s, ab=ab, er=er: s.activation(
                er[0][:], ab[0][:], AF.Exp, scale=-1.0), reads=[ab[1]], writes=[er[1]])
            b.op("s", lambda s, er=er, ln_=ln_: s.activation(
                ln_[0][:], er[0][:], AF.Ln, bias=1.0), reads=[er[1]], writes=[ln_[1]])

        def softplus_tail(c, xct=xct, xct_b=xct_b):
            dtc = dts[c % 2]
            rl, ln_ = spR[c % 2], spR2[c % 2]
            b.op("v", lambda v, dtc=dtc, rl=rl, ln_=ln_: v.tensor_tensor(
                dtc[0][:], rl[0][:], ln_[0][:], OP.add),
                reads=[rl[1], ln_[1]], writes=[dtc[1]])
            uc = usc[c % 2]
            b.op("v", lambda v, c=c, dtc=dtc, uc=uc: v.tensor_tensor(
                uc[0][:], dtc[0][:], xct[:, c, :], OP.mult),
                reads=[dtc[1], xct_b], writes=[uc[1]])

        softplus_pre(0)
        softplus_tail(0)
        NGG = NST // 2
        for c in range(QI):
            dtc = dts[c % 2]
            uc = usc[c % 2]
            gy = GAcc(b, pY[:], pY_b, NST)
            for k in range(NGG + 2):
                if k == 1 and c + 1 < QI:
                    softplus_pre(c + 1)
                if k == 6 and c + 1 < QI:
                    softplus_tail(c + 1)
                if k < NGG:
                    gg = k
                    da, dx = dAs[gg % 2], dbx[gg % 3]
                    for i in range(2):
                        n = 2 * gg + i
                        b.op("s", lambda s, c=c, n=n, i=i, da=da, dtc=dtc: s.activation(
                            da[0][:, i, :], dtc[0][:], AF.Exp,
                            scale=Ares[:, c * NST + n:c * NST + n + 1]),
                            reads=[dtc[1], Ares_b], writes=[da[1]])
                    b.op("v", lambda v, gg=gg, dx=dx, uc=uc: v.tensor_tensor(
                        dx[0][:], uc[0][:].unsqueeze(1).broadcast_to([128, 2, TA]),
                        Bt[:, 2 * gg:2 * gg + 2, :], OP.mult),
                        reads=[uc[1], Bt_b], writes=[dx[1]])
                if 1 <= k <= NGG:
                    gg = k - 1
                    da, dx, hh = dAs[gg % 2], dbx[gg % 3], hsl[gg % 3]
                    for i in range(2):
                        n = 2 * gg + i
                        sidx = bi * QI * NST + c * NST + n
                        if ti == 0:
                            init = 0.0
                            rds = [da[1], dx[1]]
                        else:
                            init = states[:, sidx:sidx + 1]
                            rds = [da[1], dx[1], st_b]
                        b.op("v", lambda v, i=i, da=da, dx=dx, hh=hh, init=init:
                             v.tensor_tensor_scan(hh[0][:, i, :], da[0][:, i, :],
                                                  dx[0][:, i, :], init, OP.mult, OP.add),
                             reads=rds, writes=[hh[1]])
                if k >= 2:
                    gg = k - 2
                    hh, hc = hsl[gg % 3], hcs[gg % 2]
                    b.op("v", lambda v, gg=gg, hh=hh, hc=hc: v.tensor_tensor(
                        hc[0][:], hh[0][:], Ct[:, 2 * gg:2 * gg + 2, :], OP.mult),
                        reads=[hh[1], Ct_b], writes=[hc[1]])
                    s0 = bi * QI * NST + c * NST + 2 * gg
                    b.op("v", lambda v, hh=hh, s0=s0: v.tensor_copy(
                        states[:, s0:s0 + 2],
                        hh[0][:, :, TA - 1:TA].rearrange("p n o -> p (n o)")),
                        reads=[hh[1]], writes=[st_b])
                    for i in range(2):
                        gy.add(eye_bf[:], hc[0][:, i, :], [hc[1], eye_bf_b])
            ys = ysc[c % 2]
            b.op("v", lambda v, c=c, ys=ys, xct=xct: v.scalar_tensor_tensor(
                ys[0][:], xct[:, c, :], dvv[:, c:c + 1], pY[:], OP.mult, OP.add),
                reads=[xct_b, dvv_b, pY_b], writes=[ys[1]])
            b.op("v", lambda v, c=c, ys=ys, sluz=sluz: v.tensor_tensor(
                ygt[:, c, :], ys[0][:], sluz[:, c, :], OP.mult),
                reads=[ys[1], sluz_b], writes=[ygt_b])

        ln2_begin(QM)
        for j in range(QM):
            wb = woutb[0]
            b.dma(lambda y, j=j, wb=wb: y.dma_start(
                out=wb[0][:], in_=wbf_out_d[j]), writes=[wb[1]])
            pp = pout[j % 2]
            g = GAcc(b, pp[0][:], pp[1], QI)
            for c in range(QI):
                g.add(wb[0][:, c * 128:(c + 1) * 128], ygt[:, c, :], [wb[1], ygt_b])
            b.op("s", lambda s, j=j, pp=pp: s.copy(mixb[:, j, :], pp[0][:]),
                 reads=[pp[1]], writes=[mixb_b])
            ln2_feed(mixb[:, j, :], [mixb_b])
        ln2_finish(QM, False)
        bcast_2(pln2[:, 0:TA], pln2_b, 0)
        bcast_2(pln2[:, TA:2 * TA], pln2_b, 2)
        next_2()
        for q in range(QM):
            tq, tq2, xo = t1q2[q % 2], t2q2[q % 2], x1o[q % 2]
            b.op("v", lambda v, q=q, tq=tq: v.tensor_tensor(
                tq[0][:], mixb[:, q, :], pln2[:, 0:TA], OP.mult),
                reads=[mixb_b, pln2_b], writes=[tq[1]])
            b.op("v", lambda v, q=q, tq=tq, tq2=tq2, x0=x0: v.tensor_tensor(
                tq2[0][:], tq[0][:], x0[:, q, :], OP.add),
                reads=[tq[1], x0_b], writes=[tq2[1]])
            b.op("v", lambda v, q=q, tq2=tq2, xo=xo: v.tensor_tensor(
                xo[0][:], tq2[0][:], pln2[:, TA:2 * TA], OP.subtract),
                reads=[tq2[1], pln2_b], writes=[xo[1]])
            b.dma(lambda y, bi=bi, t0=t0, q=q, xo=xo: y.dma_start(
                out=x1_d[bi, 128 * q:128 * (q + 1), t0:t0 + TA], in_=xo[0][:]),
                reads=[xo[1]])

    b.barrier()
    pA2.close()

    pB = ExitStack()

    def sbB(name, shape, dt=F32):
        t = pB.enter_context(nc.sbuf_tensor(name, list(shape), dt))
        return t, Buf()

    def psB(name, shape, dt=F32):
        t = pB.enter_context(nc.psum_tensor(name, list(shape), dt))
        return t, Buf()

    x1s = [sbB(f"x1s{i}", (128, QM, TB), BF16) for i in range(4)]
    qtmp, qtmp_b = sbB("qtmp", (128, 2 * TB))
    vqs = [sbB(f"vq{i}", (128, QM, TB), BF16) for i in range(2)]
    gts = [sbB(f"gt{i}", (128, QH, TB), BF16) for i in range(2)]
    fts, fts_b = sbB("fts", (128, QM, 2 * TB), BF16)
    fbA = [sbB("fbA0", (128, 2 * TB), BF16)]

    def qset(tag):
        return (sbB("zsc" + tag, (128, 2 * TB)),
                sbB("qw1" + tag, (128, 2 * TB), BF16),
                sbB("qw2" + tag, (128, 2 * TB), BF16),
                sbB("qdd" + tag, (128, 2 * TB), BF16))

    qsetA, qsetB2 = qset("A"), qset("B")
    m1A = sbB("m1A", (128, TB), BF16)
    m1B = sbB("m1B", (128, TB), BF16)
    colmA = sbB("colmA", (128, 10))
    colmB = sbB("colmB", (128, 10))
    colbfA = sbB("colbfA", (128, 8), BF16)
    colbfB = sbB("colbfB", (128, 8), BF16)
    rows4A = sbB("rows4A", (4, TB), BF16)
    rows4B = sbB("rows4B", (4, TB), BF16)
    alf = [sbB(f"alf{i}", (128, TB)) for i in range(2)]
    gsc = [sbB(f"gsc{i}", (128, TB)) for i in range(2)]
    gsf = [sbB(f"gsf{i}", (128, TB)) for i in range(2)]
    btf, btf_b = sbB("btf", (128, TB))
    t1qB = [sbB(f"t1qB{i}", (128, TB)) for i in range(2)]
    ftq = [sbB(f"ftq{i}", (128, TB)) for i in range(2)]
    x2o = [sbB(f"x2o{i}", (128, TB)) for i in range(2)]
    rowsB1 = sbB("rowsB1", (1, 6 * TB))
    lbfB1 = sbB("lbfB1", (1, 4 * TB), BF16)
    rowsBF = sbB("rowsBF", (1, 6 * TB))
    lbfF = sbB("lbfF", (1, 4 * TB), BF16)
    wq1r, wq1r_b = sbB("wq1r", (128, QM * QH * 128), BF16)
    wq2s = [sbB(f"wq2s{i}", (128, QH * 128), BF16) for i in range(2)]

    pm1a = psB("pm1a", (128, TB))
    pm1b = psB("pm1b", (128, TB))
    pm2a = psB("pm2a", (128, TB))
    pm2b = psB("pm2b", (128, TB))
    pbcA = psB("pbcA", (128, 2 * TB))
    pbcB = psB("pbcB", (128, 2 * TB))
    ptrA = psB("ptrA", (128, 2 * TB))
    ptrB = psB("ptrB", (128, 2 * TB))
    stA = (ptrA[0], ptrA[1])

    for j in range(QH):
        b.dma(lambda y, j=j: y.dma_start(
            out=wq1r[:, j * QM * 128:(j + 1) * QM * 128], in_=wq1_d[j]),
            writes=[wq1r_b])

    def ln_finishB(stat_ap, stat_buf, rowsp, lbfp, fused):
        rows, rows_b = rowsp
        lbf, lbf_b = lbfp
        R = lambda i: rows[:, i * TB:(i + 1) * TB]
        H = lambda i: lbf[:, i * TB:(i + 1) * TB]
        RW = [rows_b]
        nfeat = float(QM * 128)
        b.op("v", lambda v: v.tensor_copy(rows[:, 0:2 * TB], stat_ap),
             reads=[stat_buf], writes=RW)
        b.op("v", lambda v: v.tensor_scalar(R(2), R(0), 1.0 / nfeat, None, OP.mult),
             reads=RW, writes=RW)
        b.op("v", lambda v: v.tensor_tensor(R(4), R(2), R(2), OP.mult),
             reads=RW, writes=RW)
        b.op("v", lambda v: v.scalar_tensor_tensor(R(5), R(1), 1.0 / nfeat, R(4),
                                                   OP.mult, OP.subtract),
             reads=RW, writes=RW)
        if fused:
            b.op("v", lambda v: v.tensor_scalar(R(1), R(5), 1.0 + EPS, EPS * EPS,
                                                OP.mult, OP.add),
                 reads=RW, writes=RW)
        else:
            b.op("v", lambda v: v.tensor_scalar(R(1), R(5), EPS, None, OP.add),
                 reads=RW, writes=RW)
        b.op("v", lambda v: v.reciprocal(R(4), R(1)), reads=RW, writes=RW)
        b.op("s", lambda s: s.activation(R(0), R(4), AF.Sqrt),
             reads=RW, writes=RW)
        b.op("s", lambda s: s.copy(H(0), R(0)), reads=RW, writes=[lbf_b])
        b.op("v", lambda v: v.tensor_tensor(H(1), R(0), H(0), OP.subtract),
             reads=RW + [lbf_b], writes=[lbf_b])
        b.op("v", lambda v: v.tensor_tensor(R(3), R(2), R(0), OP.mult),
             reads=RW, writes=RW)
        b.op("s", lambda s: s.copy(H(2), R(3)), reads=RW, writes=[lbf_b])
        b.op("v", lambda v: v.tensor_tensor(H(3), R(3), H(2), OP.subtract),
             reads=RW + [lbf_b], writes=[lbf_b])

    def bcastB2(lbfp, off, psum_ap, psum_buf):
        lbf, lbf_b = lbfp
        g = GAcc(b, psum_ap, psum_buf, 2)
        g.add(orow_bf[:], lbf[:, off * TB:(off + 1) * TB], [lbf_b, orow_bf_b])
        g.add(orow_bf[:], lbf[:, (off + 1) * TB:(off + 2) * TB], [lbf_b, orow_bf_b])

    def absmaxB(src_ap, src_rds, m1p, colmp, cbfp, r4p, ptrp, swi_pair):
        m1, m1_b = m1p
        colm, colm_b = colmp
        cbf, cbf_b = cbfp
        r4, r4_b = r4p
        ptr_t, ptr_b2 = ptrp
        swi, swi_b = swi_pair
        b.op("v", lambda v: v.tensor_reduce(
            m1[:], src_ap.rearrange("p q t -> p t q"), AX.X, OP.max,
            apply_absolute_value=True), reads=list(src_rds), writes=[m1_b])
        yield
        for g_ in range(TB // 128):
            GAcc(b, ptr_t[:, g_ * 128:(g_ + 1) * 128], ptr_b2, 1).add(
                m1[:, g_ * 128:(g_ + 1) * 128], eye_bf[:], [m1_b, eye_bf_b])
            b.op("v", lambda v, g_=g_: v.tensor_reduce(
                colm[:, g_:g_ + 1], ptr_t[:, g_ * 128:(g_ + 1) * 128], AX.X, OP.max),
                reads=[ptr_b2], writes=[colm_b])
        yield
        b.op("v", lambda v: v.tensor_scalar(colm[:, 2:4], colm[:, 0:2], 1e-5, None,
                                            OP.max), reads=[colm_b], writes=[colm_b])
        b.op("v", lambda v: v.reciprocal(colm[:, 4:6], colm[:, 2:4]),
             reads=[colm_b], writes=[colm_b])
        b.op("v", lambda v: v.tensor_scalar(colm[:, 6:8], colm[:, 4:6], 127.0, None,
                                            OP.mult), reads=[colm_b], writes=[colm_b])
        b.op("v", lambda v: v.tensor_scalar(colm[:, 8:10], colm[:, 2:4], swi[:],
                                            1.0 / 127.0, OP.mult, OP.mult),
             reads=[colm_b, swi_b], writes=[colm_b])
        cb3 = cbf[:].rearrange("p (g k) -> p g k", k=4)
        yield
        b.op("v", lambda v: v.tensor_copy(cb3[:, :, 0:1], colm[:, 6:8].unsqueeze(2)),
             reads=[colm_b], writes=[cbf_b])
        b.op("v", lambda v: v.tensor_tensor(cb3[:, :, 1:2], colm[:, 6:8].unsqueeze(2),
                                            cb3[:, :, 0:1], OP.subtract),
             reads=[colm_b, cbf_b], writes=[cbf_b])
        b.op("v", lambda v: v.tensor_copy(cb3[:, :, 2:3], colm[:, 8:10].unsqueeze(2)),
             reads=[colm_b], writes=[cbf_b])
        b.op("v", lambda v: v.tensor_tensor(cb3[:, :, 3:4], colm[:, 8:10].unsqueeze(2),
                                            cb3[:, :, 2:3], OP.subtract),
             reads=[colm_b, cbf_b], writes=[cbf_b])
        for g_ in range(TB // 128):
            GAcc(b, ptr_t[0:4, 256 + g_ * 128:256 + (g_ + 1) * 128], ptr_b2, 1).add(
                cbf[:, g_ * 4:(g_ + 1) * 4], eye_bf[:], [cbf_b, eye_bf_b])
        b.op("v", lambda v: v.tensor_copy(r4[:, 0:TB], ptr_t[0:4, 256:256 + TB]),
             reads=[ptr_b2], writes=[r4_b])

    def selb4(r4p, off, psum_ap, psum_buf):
        r4, r4_b = r4p
        g = GAcc(b, psum_ap, psum_buf, 2)
        g.add(selbf[0:4, off * 128:(off + 1) * 128], r4[:], [r4_b, selbf_b])
        g.add(selbf[0:4, (off + 1) * 128:(off + 2) * 128], r4[:], [r4_b, selbf_b])

    def quantB(src3, src_rds, s_ap, s_buf, out3, out_buf, qs, smagic=False):
        (zs, zs_b), (q1, q1_b), (q2, q2_b), (qd, qd_b) = qs
        z3 = zs[:].rearrange("p (q t) -> p q t", t=TB)
        q13 = q1[:].rearrange("p (q t) -> p q t", t=TB)
        qd3 = qd[:].rearrange("p (q t) -> p q t", t=TB)
        b.op("v", lambda v: v.tensor_tensor(
            z3, src3, s_ap.unsqueeze(1).broadcast_to([128, 2, TB]), OP.mult),
            reads=list(src_rds) + [s_buf], writes=[zs_b])
        if smagic:
            b.op("s", lambda s: s.activation(qtmp[:], zs[:], AF.Identity,
                                             bias=m32c[:]),
                 reads=[zs_b, m32c_b], writes=[qtmp_b])
            b.op("s", lambda s: s.activation(q1[:], qtmp[:], AF.Identity,
                                             bias=nm32c[:]),
                 reads=[qtmp_b, nm32c_b], writes=[q1_b])
            b.op("s", lambda s: s.activation(qtmp[:], zs[:], AF.Identity,
                                             bias=m64c[:]),
                 reads=[zs_b, m64c_b], writes=[qtmp_b])
            b.op("s", lambda s: s.activation(q2[:], qtmp[:], AF.Identity,
                                             bias=nm64c[:]),
                 reads=[qtmp_b, nm64c_b], writes=[q2_b])
        else:
            b.op("v", lambda v: v.tensor_scalar(q1[:], zs[:], M32, -M32, OP.add,
                                                OP.add), reads=[zs_b], writes=[q1_b])
            b.op("v", lambda v: v.tensor_scalar(q2[:], zs[:], M64, -M64, OP.add,
                                                OP.add), reads=[zs_b], writes=[q2_b])
        b.op("v", lambda v: v.tensor_tensor(qd[:], q2[:], q1[:], OP.subtract),
             reads=[q1_b, q2_b], writes=[qd_b])
        b.op("v", lambda v: v.scalar_tensor_tensor(out3, qd3, ublend[:], q13,
                                                   OP.mult, OP.add),
             reads=[qd_b, q1_b, ublend_b], writes=[out_buf])

    unitsB = [(bb, t) for t in range(NTB) for bb in range(BL)]
    NU = len(unitsB)

    def dma_x1(u):
        bi, ti = unitsB[u]
        t0 = ti * TB
        x1, x1_b = x1s[u % 4]
        b.dma(lambda y, bi=bi, t0=t0, x1=x1: y.dma_start(
            out=x1[:], in_=x1_d[bi, :, t0:t0 + TB].rearrange("(q p) t -> p q t", p=128)),
            writes=[x1_b])

    def gen_P1(u):
        x1, x1_b = x1s[u % 4]
        vq, vq_b = vqs[u % 2]
        al, al_b = alf[u % 2]
        g1 = GAcc(b, stA[0][0:1, 0:2 * TB], stA[1], QM)
        for q in range(QM):
            fb, fb_b = fbA[0]
            b.op("s", lambda s, q=q, fb=fb: s.copy(fb[:, 0:TB], x1[:, q, :]),
                 reads=[x1_b], writes=[fb_b])
            b.op("s", lambda s, q=q, fb=fb: s.activation(
                fb[:, TB:2 * TB], x1[:, q, :], AF.Square),
                reads=[x1_b], writes=[fb_b])
            g1.add(ocol_bf[:], fb[:], [fb_b, ocol_bf_b])
            if q % 2 == 1:
                yield
        ln_finishB(stA[0][0:1, 0:2 * TB], stA[1], rowsB1, lbfB1, True)
        yield
        bcastB2(lbfB1, 0, pbcA[0][:, 0:TB], pbcA[1])
        bcastB2(lbfB1, 2, pbcA[0][:, TB:2 * TB], pbcA[1])
        yield
        zsA, zsA_b = qsetA[0]
        for q in range(QM):
            tq, tq_b = t1qB[q % 2]
            t2 = zsA[:, (q % 2) * TB:(q % 2) * TB + TB]
            b.op("v", lambda v, q=q, tq=tq: v.tensor_tensor(
                tq[:], x1[:, q, :], pbcA[0][:, 0:TB], OP.mult),
                reads=[x1_b, pbcA[1]], writes=[tq_b])
            b.op("v", lambda v, q=q, tq=tq, t2=t2: v.tensor_tensor(
                t2, tq[:], pbcA[0][:, TB:2 * TB], OP.subtract),
                reads=[tq_b, pbcA[1]], writes=[zsA_b])
            b.op("s", lambda s, q=q, t2=t2: s.activation(
                vq[:, q, :], t2, AF.Identity,
                scale=gam[:, q:q + 1], bias=bet[:, q:q + 1]),
                reads=[zsA_b, gam_b, bet_b], writes=[vq_b])
            if q % 2 == 1:
                yield
        if dbg and u == 0:
            tap("v", vq[:].rearrange("p q t -> p (q t)"), (128, QM * TB), [vq_b], BF16)
        yield from absmaxB(vq[:], [vq_b], m1A, colmA, colbfA, rows4A, ptrA,
                           (sw1i_vec, sw1i_vec_b))
        yield
        selb4(rows4A, 0, pbcA[0][:, 0:TB], pbcA[1])
        selb4(rows4A, 2, pbcA[0][:, TB:2 * TB], pbcA[1])
        yield
        for h in range(4):
            quantB(vq[:, 2 * h:2 * h + 2, :], [vq_b], pbcA[0][:, 0:TB], pbcA[1],
                   vq[:, 2 * h:2 * h + 2, :], vq_b, qsetA)
            yield
        b.op("v", lambda v, al=al: v.tensor_copy(al[:], pbcA[0][:, TB:2 * TB]),
             reads=[pbcA[1]], writes=[al_b])

    def gen_P2(u):
        vq, vq_b = vqs[u % 2]
        al, al_b = alf[u % 2]
        gt, gt_b = gts[u % 2]
        b.op("v", lambda v: v.tensor_tensor(
            vq[:], vq[:], al[:].unsqueeze(1).broadcast_to([128, QM, TB]), OP.mult),
            reads=[al_b], writes=[vq_b])
        for j in range(QH):
            pp, pp_b = pm1a if j % 2 == 0 else pm1b
            g = GAcc(b, pp[:], pp_b, QM)
            for k in range(QM):
                g.add(wq1r[:, (j * QM + k) * 128:(j * QM + k + 1) * 128],
                      vq[:, k, :], [wq1r_b, vq_b])
            gs, gs_b = gsc[j % 2]
            b.op("s", lambda s, pp=pp, gs=gs: s.copy(gs[:], pp[:]),
                 reads=[pp_b], writes=[gs_b])
            b.op("s", lambda s, j=j, gs=gs, gt=gt: s.activation(
                gt[:, j, :], gs[:], AF.Gelu, bias=b1v[:, j:j + 1]),
                reads=[gs_b, b1v_b], writes=[gt_b])
            yield
        if dbg and u == 0:
            tap("g", gt[:].rearrange("p q t -> p (q t)"), (128, QH * TB), [gt_b], BF16)

    def gen_P3(u):
        bi, ti = unitsB[u]
        t0 = ti * TB
        x1, x1_b = x1s[u % 4]
        gt, gt_b = gts[u % 2]
        ws0, ws0_b = wq2s[0]
        b.dma(lambda y, ws0=ws0: y.dma_start(out=ws0[:], in_=wq2_d[0]),
              writes=[ws0_b])
        yield from absmaxB(gt[:], [gt_b], m1B, colmB, colbfB, rows4B, ptrB,
                           (sw2i_vec, sw2i_vec_b))
        yield
        selb4(rows4B, 0, pbcB[0][:, 0:TB], pbcB[1])
        selb4(rows4B, 2, pbcB[0][:, TB:2 * TB], pbcB[1])
        b.op("v", lambda v: v.tensor_copy(btf[:], pbcB[0][:, TB:2 * TB]),
             reads=[pbcB[1]], writes=[btf_b])
        yield
        for h in range(16):
            quantB(gt[:, 2 * h:2 * h + 2, :], [gt_b], pbcB[0][:, 0:TB], pbcB[1],
                   gt[:, 2 * h:2 * h + 2, :], gt_b, qsetB2)
            yield
        gf = GAcc(b, ptrB[0][0:1, 0:2 * TB], ptrB[1], QM)
        for j in range(QM):
            if j + 1 < QM:
                wsn, wsn_b = wq2s[(j + 1) % 2]
                b.dma(lambda y, j=j, wsn=wsn: y.dma_start(
                    out=wsn[:], in_=wq2_d[j + 1]), writes=[wsn_b])
            ws, ws_b = wq2s[j % 2]
            pp, pp_b = pm2a if j % 2 == 0 else pm2b
            g = GAcc(b, pp[:], pp_b, QH)
            for k in range(QH):
                g.add(ws[:, k * 128:(k + 1) * 128], gt[:, k, :], [ws_b, gt_b])
            gs, gs_b = gsf[j % 2]
            b.op("v", lambda v, pp=pp, gs=gs: v.tensor_tensor(
                gs[:], pp[:], btf[:], OP.mult),
                reads=[pp_b, btf_b], writes=[gs_b])
            b.op("s", lambda s, j=j, gs=gs: s.activation(
                fts[:, j, 0:TB], gs[:], AF.Identity, bias=b2v[:, j:j + 1]),
                reads=[gs_b, b2v_b], writes=[fts_b])
            b.op("s", lambda s, j=j: s.activation(
                fts[:, j, TB:2 * TB], fts[:, j, 0:TB], AF.Square),
                reads=[fts_b], writes=[fts_b])
            gf.add(ocol_bf[:], fts[:, j, :], [fts_b, ocol_bf_b])
            yield
        ln_finishB(ptrB[0][0:1, 0:2 * TB], ptrB[1], rowsBF, lbfF, False)
        yield
        bcastB2(lbfF, 0, pbcB[0][:, 0:TB], pbcB[1])
        bcastB2(lbfF, 2, pbcB[0][:, TB:2 * TB], pbcB[1])
        yield
        for q in range(QM):
            tq, tq_b = ftq[q % 2]
            t2, t2_b = x2o[q % 2]
            b.op("v", lambda v, q=q, tq=tq: v.tensor_tensor(
                tq[:], fts[:, q, 0:TB], pbcB[0][:, 0:TB], OP.mult),
                reads=[fts_b, pbcB[1]], writes=[tq_b])
            b.op("v", lambda v, q=q, tq=tq, t2=t2: v.tensor_tensor(
                t2[:], tq[:], x1[:, q, :], OP.add),
                reads=[tq_b, x1_b], writes=[t2_b])
            b.op("v", lambda v, q=q, tq=tq, t2=t2: v.tensor_tensor(
                tq[:], t2[:], pbcB[0][:, TB:2 * TB], OP.subtract),
                reads=[t2_b, pbcB[1]], writes=[tq_b])
            b.dma(lambda y, bi=bi, t0=t0, q=q, tq=tq: y.dma_start(
                out=out_d[bi, 128 * q:128 * (q + 1), t0:t0 + TB], in_=tq[:]),
                reads=[tq_b])
            if q % 2 == 1:
                yield

    def step(g, n=1):
        if g is None:
            return None
        try:
            for _ in range(n):
                next(g)
            return g
        except StopIteration:
            return None

    dma_x1(0)
    dma_x1(1)
    for c in range(NU + 2):
        p2 = gen_P2(c - 1) if 0 <= c - 1 < NU else None
        p3 = gen_P3(c - 2) if 0 <= c - 2 < NU else None
        p1 = gen_P1(c) if c < NU else None
        p3 = step(p3)
        p2 = step(p2, 12)
        p3 = step(p3, 3)
        p1_head = 10
        while p3 is not None or p2 is not None or p1 is not None:
            if p3 is not None:
                p3 = step(p3)
            if p2 is not None:
                p2 = step(p2)
            if p1 is not None and (p1_head > 0 or p2 is None):
                p1 = step(p1)
                p1_head -= 1
        if c + 2 < NU:
            dma_x1(c + 2)

    b.barrier()
    pB.close()
    es.close()
    return nc


_NC_CACHE = {}


def kernel(**inputs):
    g = _host_prep(inputs)
    if "nc" not in _NC_CACHE:
        _NC_CACHE["nc"] = build_nc()
    nc = _NC_CACHE["nc"]
    base = {k: v for k, v in g.items() if k != "xT"}
    in_maps = []
    for c in range(8):
        m = dict(base)
        m["x"] = np.ascontiguousarray(g["xT"][2 * c:2 * c + 2])
        in_maps.append(m)
    res = run_bass_kernel_spmd(nc, in_maps, core_ids=list(range(8)))
    outs = [np.asarray(res.results[c]["out"]) for c in range(8)]
    full = np.concatenate(outs, axis=0)
    return np.ascontiguousarray(full.transpose(0, 2, 1)).astype(np.float32)


# version 20
# speedup vs baseline: 1.0017x; 1.0017x over previous
import numpy as np
from contextlib import ExitStack

import concourse.bass as bass
import concourse.mybir as mybir
from concourse.bass_utils import run_bass_kernel_spmd

F32 = mybir.dt.float32
BF16 = mybir.dt.bfloat16
AF = mybir.ActivationFunctionType
OP = mybir.AluOpType
AX = mybir.AxisListType

M32 = float(np.float32(1.5 * 2 ** 23))
M64 = float(np.float32(1.5 * 2 ** 52))
EPS = 1e-5

L, DM, DI, NST, HID = 2048, 1024, 2048, 16, 4096
QM, QI, QH = DM // 128, DI // 128, HID // 128
BL = 2
TA, TB = 512, 256
NTA, NTB = L // TA, L // TB


class Buf:
    __slots__ = ("w", "r")

    def __init__(self):
        self.w = {}
        self.r = {}


def _merge(d, tok):
    sk, v = tok
    if d.get(sk, 0) < v:
        d[sk] = v


NQPOOL = 32


class Bld:
    def __init__(self, nc, es):
        self.nc = nc
        self.sem = {}
        self.cnt = {}
        for name in ("v", "s", "t"):
            self.sem[name] = es.enter_context(nc.semaphore("sem_" + name))
            self.cnt[name] = 0
        for i in range(NQPOOL):
            self.sem[f"q{i}"] = es.enter_context(nc.semaphore(f"sem_q{i}"))
            self.cnt[f"q{i}"] = 0
        self.qi = 0
        self.eng = dict(v=nc.vector, s=nc.scalar, t=nc.tensor, y=nc.sync)
        self.wm = {e: {} for e in ("v", "s", "t", "y")}

    def _wait(self, ekey, bufs_r, bufs_w):
        need = {}
        for b in bufs_r:
            for sk, v in b.w.items():
                if v > need.get(sk, 0):
                    need[sk] = v
        for b in bufs_w:
            for d in (b.w, b.r):
                for sk, v in d.items():
                    if v > need.get(sk, 0):
                        need[sk] = v
        eng = self.eng[ekey]
        wm = self.wm[ekey]
        for sk, val in need.items():
            if wm.get(sk, 0) < val:
                eng.wait_ge(self.sem[sk], val)
                wm[sk] = val

    def op(self, ekey, fn, reads=(), writes=()):
        self._wait(ekey, reads, writes)
        instr = fn(self.eng[ekey])
        self.cnt[ekey] += 1
        instr.then_inc(self.sem[ekey], 1)
        tok = (ekey, self.cnt[ekey])
        for b in writes:
            _merge(b.w, tok)
        for b in reads:
            _merge(b.r, tok)
        return tok

    def dma(self, fn, reads=(), writes=()):
        sk = f"q{self.qi}"
        self.qi = (self.qi + 1) % NQPOOL
        prior = Buf()
        prior.w[sk] = self.cnt[sk]
        self._wait("y", list(reads) + [prior], writes)
        fn(self.eng["y"]).then_inc(self.sem[sk], 16)
        self.cnt[sk] += 16
        tok = (sk, self.cnt[sk])
        for b in writes:
            _merge(b.w, tok)
        for b in reads:
            _merge(b.r, tok)
        return tok

    def barrier(self):
        fake = Buf()
        for k in self.cnt:
            fake.w[k] = self.cnt[k]
        for e in ("v", "s", "t", "y"):
            self._wait(e, [fake], [])


class GAcc:

    def __init__(self, bld, psum_ap, psum_buf, n):
        self.b, self.ap, self.buf, self.n = bld, psum_ap, psum_buf, n
        self.i = 0
        self.reads = []

    def add(self, lt, rh, reads):
        reads = list(reads)
        first, last = self.i == 0, self.i == self.n - 1
        self.b._wait("t", reads, [self.buf] if first else [])
        ins = self.b.eng["t"].matmul(self.ap, lt, rh, start=first, stop=last)
        self.i += 1
        self.b.cnt["t"] += 1
        ins.then_inc(self.b.sem["t"], 1)
        tok = ("t", self.b.cnt["t"])
        for r in reads:
            _merge(r.r, tok)
        if last:
            _merge(self.buf.w, tok)
            return tok
        return None


def _host_prep(inputs):
    g = {}
    x = np.asarray(inputs["L0_attn_x"], dtype=np.float32)
    g["xT"] = np.ascontiguousarray(x.transpose(0, 2, 1))

    def tiles(wT, nk, nm):
        K, M = wT.shape
        a = wT.reshape(nk, 128, nm, 128)
        return np.ascontiguousarray(a.transpose(2, 0, 1, 3))

    w_in_T = np.ascontiguousarray(np.asarray(inputs["L0_attn_in_proj_w"], np.float32).T)
    w_out_T = np.ascontiguousarray(np.asarray(inputs["L0_attn_out_proj_w"], np.float32).T)
    w_xp_T = np.ascontiguousarray(np.asarray(inputs["L0_attn_x_proj_w"], np.float32).T)
    w1_T = np.ascontiguousarray(np.asarray(inputs["L0_ffn_w1"], np.float32).T)
    w2_T = np.ascontiguousarray(np.asarray(inputs["L0_ffn_w2"], np.float32).T)

    g["w_in"] = tiles(w_in_T, QM, QM * 4)
    g["w_out"] = tiles(w_out_T, QI, QM)
    g["w1"] = tiles(w1_T, QM, QH)
    g["w2"] = tiles(w2_T, QH, QM)
    g["w_xp"] = np.ascontiguousarray(w_xp_T.reshape(QI, 128, 33).transpose(1, 0, 2))

    def chunkvec(v, q):
        return np.ascontiguousarray(np.asarray(v, np.float32).reshape(q, 128).T)

    g["conv_w"] = np.ascontiguousarray(
        np.asarray(inputs["L0_attn_conv_w"], np.float32).reshape(DI, 4)
        .reshape(QI, 128, 4).transpose(1, 0, 2).reshape(128, QI * 4))
    g["conv_b"] = chunkvec(inputs["L0_attn_conv_b"], QI)
    g["dt_w"] = chunkvec(np.asarray(inputs["L0_attn_dt_proj_w"], np.float32).reshape(DI), QI)
    g["dt_b"] = chunkvec(inputs["L0_attn_dt_proj_b"], QI)
    g["a_log"] = np.ascontiguousarray(
        np.asarray(inputs["L0_attn_A_log"], np.float32)
        .reshape(QI, 128, NST).transpose(1, 0, 2).reshape(128, QI * NST))
    g["dvec"] = chunkvec(inputs["L0_attn_D"], QI)
    g["b1"] = chunkvec(inputs["L0_ffn_b1"], QH)
    g["b2"] = chunkvec(inputs["L0_ffn_b2"], QM)
    g["gamma"] = chunkvec(inputs["L0_ffn_gamma"], QM)
    g["beta"] = chunkvec(inputs["L0_ffn_beta"], QM)
    g["ones_row"] = np.ones((1, 128), np.float32)
    g["konst"] = np.array([[EPS, -0.5, 1.0 + EPS, EPS * EPS]], np.float32)
    sel = np.zeros((33, 33 * 128), np.float32)
    for r in range(33):
        sel[r, r * 128:(r + 1) * 128] = 1.0
    g["selmat"] = sel
    g["ones_col"] = np.ones((128, 1), np.float32)
    g["eye"] = np.eye(128, dtype=np.float32)
    return g


def build_nc(dbg=False):
    nc = bass.Bass("TRN2", target_bir_lowering=False)
    es = ExitStack()
    b = Bld(nc, es)

    dr = {}
    for nm, sh in [("x", (BL, DM, L)), ("w_in", (QM * 4, QM, 128, 128)),
                   ("w_out", (QM, QI, 128, 128)), ("w1", (QH, QM, 128, 128)),
                   ("w2", (QM, QH, 128, 128)), ("w_xp", (128, QI, 33)),
                   ("conv_w", (128, QI * 4)), ("conv_b", (128, QI)),
                   ("dt_w", (128, QI)), ("dt_b", (128, QI)),
                   ("a_log", (128, QI * NST)), ("dvec", (128, QI)),
                   ("b1", (128, QH)), ("b2", (128, QM)),
                   ("gamma", (128, QM)), ("beta", (128, QM)),
                   ("ones_row", (1, 128)), ("ones_col", (128, 1)),
                   ("eye", (128, 128)), ("konst", (1, 4)),
                   ("selmat", (33, 33 * 128))]:
        dr[nm] = nc.dram_tensor(nm, list(sh), F32, kind="ExternalInput").ap()
    x_d = dr["x"]
    out_d = nc.dram_tensor("out", [BL, DM, L], F32, kind="ExternalOutput").ap()

    wbf_in_d = nc.dram_tensor("wbf_in", [QM * 4, 128, QM * 128], BF16, kind="Internal").ap()
    wbf_out_d = nc.dram_tensor("wbf_out", [QM, 128, QI * 128], BF16, kind="Internal").ap()
    wq1_d = nc.dram_tensor("wq1", [QH, 128, QM * 128], BF16, kind="Internal").ap()
    wq2_d = nc.dram_tensor("wq2", [QM, 128, QH * 128], BF16, kind="Internal").ap()
    x1_d = nc.dram_tensor("x1h", [BL, DM, L], BF16, kind="ExternalOutput").ap()
    xc_d = nc.dram_tensor("xch", [BL, DI, L], BF16, kind="ExternalOutput").ap()
    sz_d = nc.dram_tensor("szh", [BL, DI, L], BF16, kind="ExternalOutput").ap()

    def tap(name, sb_ap, shape, reads, dt=F32):
        if not dbg:
            return
        d = nc.dram_tensor("dbg_" + name, list(shape), dt, kind="ExternalOutput").ap()
        b.dma(lambda y: y.dma_start(out=d, in_=sb_ap), reads=list(reads), writes=[Buf()])

    def sb(name, shape, dt=F32):
        t = es.enter_context(nc.sbuf_tensor(name, list(shape), dt))
        return t, Buf()

    cvw, cvw_b = sb("cvw", (128, QI * 4))
    cvb, cvb_b = sb("cvb", (128, QI))
    dtw, dtw_b = sb("dtw", (128, QI))
    dtb, dtb_b = sb("dtb", (128, QI))
    alog, alog_b = sb("alog", (128, QI * NST))
    Ares, Ares_b = sb("Ares", (128, QI * NST))
    dvv, dvv_b = sb("dvv", (128, QI))
    b1v, b1v_b = sb("b1v", (128, QH))
    b2v, b2v_b = sb("b2v", (128, QM))
    gam, gam_b = sb("gam", (128, QM))
    bet, bet_b = sb("bet", (128, QM))
    nbet, nbet_b = sb("nbet", (128, QM))
    orow, orow_b = sb("orow", (1, 128))
    konst, konst_b = sb("konst_sb", (1, 4))
    selbf, selbf_b = sb("selbf", (33, 33 * 128), BF16)
    ocol, ocol_b = sb("ocol", (128, 1))
    eyef, eyef_b = sb("eyef", (128, 128))
    orow_bf, orow_bf_b = sb("orow_bf", (1, 128), BF16)
    ocol_bf, ocol_bf_b = sb("ocol_bf", (128, 1), BF16)
    eye_bf, eye_bf_b = sb("eye_bf", (128, 128), BF16)
    xpw_bf, xpw_bf_b = sb("xpw_bf", (128, QI, 33), BF16)
    ublend, ublend_b = sb("ublend", (128, 1))
    m32c, m32c_b = sb("m32c", (128, 1))
    nm32c, nm32c_b = sb("nm32c", (128, 1))
    m64c, m64c_b = sb("m64c", (128, 1))
    nm64c, nm64c_b = sb("nm64c", (128, 1))
    ub_scr, ub_scr_b = sb("ub_scr", (128, 1))
    sw1_vec, sw1_vec_b = sb("sw1_vec", (128, 1))
    sw1i_vec, sw1i_vec_b = sb("sw1i_vec", (128, 1))
    sw2_vec, sw2_vec_b = sb("sw2_vec", (128, 1))
    sw2i_vec, sw2i_vec_b = sb("sw2i_vec", (128, 1))
    states, states_b = sb("states", (128, BL * QI * NST))
    tails, tails_b = sb("tails", (128, BL * QI * 3), BF16)

    p0 = ExitStack()

    def sb0(name, shape, dt=F32):
        t = p0.enter_context(nc.sbuf_tensor(name, list(shape), dt))
        return t, Buf()

    for nm, (tt, bb) in [("conv_w", (cvw, cvw_b)), ("conv_b", (cvb, cvb_b)),
                         ("dt_w", (dtw, dtw_b)), ("dt_b", (dtb, dtb_b)),
                         ("a_log", (alog, alog_b)), ("dvec", (dvv, dvv_b)),
                         ("b1", (b1v, b1v_b)), ("b2", (b2v, b2v_b)),
                         ("gamma", (gam, gam_b)), ("beta", (bet, bet_b)),
                         ("ones_row", (orow, orow_b)), ("ones_col", (ocol, ocol_b)),
                         ("eye", (eyef, eyef_b)), ("konst", (konst, konst_b))]:
        b.dma(lambda y, d=dr[nm], t=tt: y.dma_start(out=t[:], in_=d), writes=[bb])

    b.op("v", lambda v: v.tensor_copy(orow_bf[:], orow[:]), reads=[orow_b], writes=[orow_bf_b])
    b.op("v", lambda v: v.tensor_copy(ocol_bf[:], ocol[:]), reads=[ocol_b], writes=[ocol_bf_b])
    b.op("v", lambda v: v.tensor_copy(eye_bf[:], eyef[:]), reads=[eyef_b], writes=[eye_bf_b])
    self_f, self_f_b = sb0("self_f", (33, 33 * 128))
    b.dma(lambda y: y.dma_start(out=self_f[:], in_=dr["selmat"]), writes=[self_f_b])
    b.op("v", lambda v: v.tensor_copy(selbf[:], self_f[:]), reads=[self_f_b], writes=[selbf_b])

    ascr, ascr_b = sb0("ascr", (128, QI * NST))
    b.op("s", lambda s: s.activation(ascr[:], alog[:], AF.Exp), reads=[alog_b], writes=[ascr_b])
    b.op("v", lambda v: v.tensor_scalar(Ares[:], ascr[:], -1.0, None, OP.mult),
         reads=[ascr_b], writes=[Ares_b])

    b.op("v", lambda v: v.tensor_scalar(nbet[:], bet[:], -1.0, None, OP.mult),
         reads=[bet_b], writes=[nbet_b])
    for cv, (ct, cb_) in [(M32, (m32c, m32c_b)), (-M32, (nm32c, nm32c_b)),
                          (M64, (m64c, m64c_b)), (-M64, (nm64c, nm64c_b))]:
        b.op("v", lambda v, cv=cv, ct=ct: v.tensor_scalar(
            ct[:], ocol[:], cv, None, OP.mult), reads=[ocol_b], writes=[cb_])
    b.op("v", lambda v: v.tensor_scalar(ub_scr[:], ocol[:], M64, None, OP.add),
         reads=[ocol_b], writes=[ub_scr_b])
    b.op("v", lambda v: v.tensor_scalar(ublend[:], ub_scr[:], -M64, None, OP.add),
         reads=[ub_scr_b], writes=[ublend_b])

    xpw_f, xpw_f_b = sb0("xpw_f", (128, QI * 33))
    b.dma(lambda y: y.dma_start(out=xpw_f[:], in_=dr["w_xp"].rearrange("p c m -> p (c m)")),
          writes=[xpw_f_b])
    b.op("s", lambda s: s.copy(xpw_bf[:].rearrange("p c m -> p (c m)"), xpw_f[:]),
         reads=[xpw_f_b], writes=[xpw_bf_b])

    def cast_stream(src_d, dst_d, nj, nk, tag):
        fsl = [sb0(f"{tag}_f{i}", (128, nk * 128)) for i in range(3)]
        bsl = [sb0(f"{tag}_b{i}", (128, nk * 128), BF16) for i in range(3)]
        for j in range(nj):
            (ft, fb), (bt, bb2) = fsl[j % 3], bsl[j % 3]
            b.dma(lambda y, j=j, ft=ft: y.dma_start(
                out=ft[:].rearrange("p (k m) -> p k m", m=128),
                in_=src_d[j].rearrange("k p m -> p k m")), writes=[fb])
            b.op("s", lambda s, ft=ft, bt=bt: s.copy(bt[:], ft[:]), reads=[fb], writes=[bb2])
            b.dma(lambda y, j=j, bt=bt: y.dma_start(out=dst_d[j], in_=bt[:]),
                  reads=[bb2])

    cast_stream(dr["w_in"], wbf_in_d, QM * 4, QM, "ci")
    cast_stream(dr["w_out"], wbf_out_d, QM, QI, "co")

    def quant_weights(src_d, dst_d, nj, nk, sw_pair, swi_pair, tag):
        sw_vec, sw_vec_b = sw_pair
        swi_vec, swi_vec_b = swi_pair
        n_el = float(nj * nk * 128 * 128)
        npg = 3 if nk <= 8 else 2
        fsl = [sb0(f"{tag}qf{i}", (128, nk * 128)) for i in range(npg)]
        red, red_b = sb0(f"{tag}red", (128, nj))
        for j in range(nj):
            ft, fb = fsl[j % npg]
            b.dma(lambda y, j=j, ft=ft: y.dma_start(
                out=ft[:].rearrange("p (k m) -> p k m", m=128),
                in_=src_d[j].rearrange("k p m -> p k m")), writes=[fb])
            b.op("v", lambda v, ft=ft, j=j: v.tensor_reduce(
                red[:, j:j + 1], ft[:], AX.X, OP.add, apply_absolute_value=True),
                reads=[fb], writes=[red_b])
        redt, redt_b = sb0(f"{tag}redt", (128, 1))
        red_bf, red_bf_b = sb0(f"{tag}redbf", (128, 1), BF16)
        b.op("v", lambda v: v.tensor_reduce(redt[:], red[:], AX.X, OP.add),
             reads=[red_b], writes=[redt_b])
        b.op("v", lambda v: v.tensor_copy(red_bf[:], redt[:]), reads=[redt_b], writes=[red_bf_b])
        pml = p0.enter_context(nc.psum_tensor(f"{tag}pm", [128, 512], F32))
        pml_b = Buf()
        GAcc(b, pml[0:1, 0:1], pml_b, 1).add(ocol_bf[:], red_bf[:], [red_bf_b, ocol_bf_b])
        sc1, sc1_b = sb0(f"{tag}sc1", (1, 2))
        b.op("v", lambda v: v.tensor_scalar(sc1[:, 0:1], pml[0:1, 0:1], 1.0 / n_el, 1e-5,
                                            OP.mult, OP.max), reads=[pml_b], writes=[sc1_b])
        b.op("v", lambda v: v.reciprocal(sc1[:, 1:2], sc1[:, 0:1]), reads=[sc1_b], writes=[sc1_b])
        sc_bf, sc_bf_b = sb0(f"{tag}scbf", (1, 2), BF16)
        b.op("v", lambda v: v.tensor_copy(sc_bf[:], sc1[:]), reads=[sc1_b], writes=[sc_bf_b])
        for col, (vec, vec_b) in [(0, (swi_vec, swi_vec_b)), (1, (sw_vec, sw_vec_b))]:
            GAcc(b, pml[:, 4 + col:5 + col], pml_b, 1).add(
                orow_bf[:], sc_bf[:, col:col + 1], [sc_bf_b, orow_bf_b])
            b.op("v", lambda v, col=col, vec=vec: v.tensor_copy(
                vec[:], pml[:, 4 + col:5 + col]), reads=[pml_b], writes=[vec_b])
        zs, zs_b = sb0(f"{tag}zs", (128, nk * 128))
        q1, q1_b = sb0(f"{tag}q1", (128, nk * 128), BF16)
        q2, q2_b = sb0(f"{tag}q2", (128, nk * 128), BF16)
        qd, qd_b = sb0(f"{tag}qd", (128, nk * 128), BF16)
        qb, qb_b = sb0(f"{tag}qb", (128, nk * 128), BF16)
        qc, qc_b = sb0(f"{tag}qc", (128, nk * 128), BF16)
        for j in range(nj):
            ft, fb = fsl[j % 2]
            b.dma(lambda y, j=j, ft=ft: y.dma_start(
                out=ft[:].rearrange("p (k m) -> p k m", m=128),
                in_=src_d[j].rearrange("k p m -> p k m")), writes=[fb])
            b.op("v", lambda v, ft=ft: v.tensor_scalar(zs[:], ft[:], sw_vec[:], None, OP.mult),
                 reads=[fb, sw_vec_b], writes=[zs_b])
            b.op("v", lambda v: v.tensor_scalar(q1[:], zs[:], M32, -M32, OP.add, OP.add),
                 reads=[zs_b], writes=[q1_b])
            b.op("v", lambda v: v.tensor_scalar(q2[:], zs[:], M64, -M64, OP.add, OP.add),
                 reads=[zs_b], writes=[q2_b])
            b.op("v", lambda v: v.tensor_tensor(qd[:], q2[:], q1[:], OP.subtract),
                 reads=[q1_b, q2_b], writes=[qd_b])
            b.op("v", lambda v: v.scalar_tensor_tensor(qb[:], qd[:], ublend[:], q1[:],
                                                       OP.mult, OP.add),
                 reads=[qd_b, q1_b, ublend_b], writes=[qb_b])
            b.op("v", lambda v: v.tensor_scalar(qc[:], qb[:], -1.0, 1.0, OP.max, OP.min),
                 reads=[qb_b], writes=[qc_b])
            b.dma(lambda y, j=j: y.dma_start(out=dst_d[j], in_=qc[:]), reads=[qc_b])

    quant_weights(dr["w1"], wq1_d, QH, QM, (sw1_vec, sw1_vec_b), (sw1i_vec, sw1i_vec_b), "a")
    quant_weights(dr["w2"], wq2_d, QM, QH, (sw2_vec, sw2_vec_b), (sw2i_vec, sw2i_vec_b), "b")

    b.op("v", lambda v: v.memset(states[:], 0.0), writes=[states_b])
    b.op("v", lambda v: v.memset(tails[:], 0.0), writes=[tails_b])

    b.barrier()
    p0.close()

    def make_ln(T, pstat1, pstat2, row_sets):
        state = {"i": 0}

        def ln_begin(nq):
            rows, rows_b, lbf, lbf_b, sqp = row_sets[state["i"] % len(row_sets)]
            ap1, buf1 = pstat1
            ap2, buf2 = pstat2
            state["g1"] = GAcc(b, ap1, buf1, nq)
            state["g2"] = GAcc(b, ap2, buf2, nq)
            state["qn"] = 0

        def ln_feed(ap, rds):
            rows, rows_b, lbf, lbf_b, sqp = row_sets[state["i"] % len(row_sets)]
            state["g1"].add(ocol_bf[:], ap, list(rds) + [ocol_bf_b])
            sq = sqp[state["qn"] % 2]
            state["qn"] += 1
            b.op("s", lambda s, ap=ap, sq=sq: s.activation(sq[0][:], ap, AF.Square),
                 reads=rds, writes=[sq[1]])
            state["g2"].add(ocol_bf[:], sq[0][:], [sq[1], ocol_bf_b])

        def ln_finish(nq, fuse_double):
            rows, rows_b, lbf, lbf_b, sqp = row_sets[state["i"] % len(row_sets)]
            R = lambda i: rows[:, i * T:(i + 1) * T]
            H = lambda i: lbf[:, i * T:(i + 1) * T]
            ap1, buf1 = pstat1
            ap2, buf2 = pstat2
            RW = [rows_b]
            b.op("v", lambda v: v.tensor_copy(R(0), ap1), reads=[buf1], writes=RW)
            b.op("v", lambda v: v.tensor_copy(R(1), ap2), reads=[buf2], writes=RW)
            nfeat = float(nq * 128)
            b.op("v", lambda v: v.tensor_scalar(R(2), R(0), 1.0 / nfeat, None, OP.mult),
                 reads=RW, writes=RW)
            b.op("v", lambda v: v.tensor_tensor(R(4), R(2), R(2), OP.mult),
                 reads=RW, writes=RW)
            b.op("v", lambda v: v.scalar_tensor_tensor(R(5), R(1), 1.0 / nfeat, R(4),
                                                       OP.mult, OP.subtract),
                 reads=RW, writes=RW)
            sc1 = (1.0 + EPS) if fuse_double else 1.0
            sc2 = (EPS * EPS) if fuse_double else EPS
            b.op("v", lambda v: v.tensor_scalar(R(1), R(5), sc1, sc2, OP.mult, OP.add),
                 reads=RW, writes=RW)
            b.op("s", lambda s: s.activation(R(3), R(1), AF.Ln), reads=RW, writes=RW)
            b.op("s", lambda s: s.activation(R(0), R(3), AF.Exp, scale=konst[0:1, 1:2]),
                 reads=RW + [konst_b], writes=RW)
            b.op("s", lambda s: s.copy(H(0), R(0)), reads=RW, writes=[lbf_b])
            b.op("v", lambda v: v.tensor_tensor(H(1), R(0), H(0), OP.subtract),
                 reads=RW + [lbf_b], writes=[lbf_b])
            b.op("v", lambda v: v.tensor_tensor(R(3), R(2), R(0), OP.mult),
                 reads=RW, writes=RW)
            b.op("s", lambda s: s.copy(H(2), R(3)), reads=RW, writes=[lbf_b])
            b.op("v", lambda v: v.tensor_tensor(H(3), R(3), H(2), OP.subtract),
                 reads=RW + [lbf_b], writes=[lbf_b])
            state["cur"] = (lbf, lbf_b)

        def ln_stats(src, nq, fuse_double):
            ln_begin(nq)
            for q in range(nq):
                ap, rds = src(q)
                ln_feed(ap, rds)
            ln_finish(nq, fuse_double)

        def bcast(psum_ap, psum_buf, off):
            lbf, lbf_b = state["cur"]
            H = lambda i: lbf[:, i * T:(i + 1) * T]
            g = GAcc(b, psum_ap, psum_buf, 2)
            g.add(orow_bf[:], H(off), [lbf_b, orow_bf_b])
            g.add(orow_bf[:], H(off + 1), [lbf_b, orow_bf_b])

        def next_site():
            state["i"] += 1

        return ln_stats, bcast, next_site, ln_begin, ln_feed, ln_finish

    pA1 = ExitStack()

    def sb1(name, shape, dt=F32):
        t = pA1.enter_context(nc.sbuf_tensor(name, list(shape), dt))
        return t, Buf()

    def ps1(name, shape, dt=F32):
        t = pA1.enter_context(nc.psum_tensor(name, list(shape), dt))
        return t, Buf()

    x0s = [sb1(f"x0_{i}", (128, QM, TA)) for i in range(2)]
    u1s = [sb1(f"u1_{i}", (128, QM, TA), BF16) for i in range(2)]
    xssm = [sb1(f"xssm{i}", (128, TA + 3), BF16) for i in range(2)]
    cvs = [sb1(f"cvs{i}", (128, TA)) for i in range(4)]
    xcts = [sb1(f"xct{i}", (128, QI, TA), BF16) for i in range(2)]
    szts = [sb1(f"szt{i}", (128, QI, TA), BF16) for i in range(2)]
    sqs1 = [sb1(f"sqs1_{i}", (128, TA), BF16) for i in range(2)]
    cbs1 = [sb1(f"cbs1_{i}", (128, TA), BF16) for i in range(2)]
    rows1 = [sb1(f"rows1_{i}", (1, 6 * TA)) for i in range(2)]
    lbf1 = [sb1(f"lbf1_{i}", (1, 4 * TA), BF16) for i in range(2)]
    t1q = [sb1(f"t1q{i}", (128, TA)) for i in range(2)]
    winb = [sb1(f"winb{i}", (128, QM * 128), BF16) for i in range(3)]
    pin = [ps1(f"pin{i}", (128, TA)) for i in range(3)]
    pst1, pst1_b = ps1("pst1", (1, 2 * TA))
    pln1, pln1_b = ps1("pln1", (128, 2 * TA))

    ln_1, bcast_1, next_1, _, _, _ = make_ln(
        TA, (pst1[0:1, 0:TA], pst1_b), (pst1[0:1, TA:2 * TA], pst1_b),
        [(rows1[0][0], rows1[0][1], lbf1[0][0], lbf1[0][1], sqs1),
         (rows1[1][0], rows1[1][1], lbf1[1][0], lbf1[1][1], sqs1)])

    tails_bb = [Buf(), Buf()]
    unitsA = [(bb, t) for t in range(NTA) for bb in range(BL)]
    for ui, (bi, ti) in enumerate(unitsA):
        par = ui % 2
        t0 = ti * TA
        x0, x0_b = x0s[par]
        u1t, u1_b = u1s[par]
        xct, xct_b = xcts[par]
        szt, szt_b = szts[par]
        tl_b = tails_bb[bi]
        b.dma(lambda y, bi=bi, t0=t0, x0=x0: y.dma_start(
            out=x0[:], in_=x_d[bi, :, t0:t0 + TA].rearrange("(q p) t -> p q t", p=128)),
            writes=[x0_b])

        def src_x0(q, x0=x0, x0_b=x0_b):
            cb = cbs1[q % 2]
            b.op("s", lambda s, q=q, cb=cb, x0=x0: s.copy(cb[0][:], x0[:, q, :]),
                 reads=[x0_b], writes=[cb[1]])
            return cb[0][:], [cb[1]]

        ln_1(src_x0, QM, False)
        bcast_1(pln1[:, 0:TA], pln1_b, 0)
        bcast_1(pln1[:, TA:2 * TA], pln1_b, 2)
        next_1()
        for q in range(QM):
            tq = t1q[q % 2]
            b.op("v", lambda v, q=q, tq=tq, x0=x0: v.tensor_tensor(
                tq[0][:], x0[:, q, :], pln1[:, 0:TA], OP.mult),
                reads=[x0_b, pln1_b], writes=[tq[1]])
            b.op("v", lambda v, q=q, tq=tq, u1t=u1t: v.tensor_tensor(
                u1t[:, q, :], tq[0][:], pln1[:, TA:2 * TA], OP.subtract),
                reads=[tq[1], pln1_b], writes=[u1_b])

        for j in range(QM * 4):
            wb = winb[j % 3]
            b.dma(lambda y, j=j, wb=wb: y.dma_start(
                out=wb[0][:], in_=wbf_in_d[j]), writes=[wb[1]])
            pp = pin[j % 3]
            g = GAcc(b, pp[0][:], pp[1], QM)
            for k in range(QM):
                g.add(wb[0][:, k * 128:(k + 1) * 128], u1t[:, k, :], [wb[1], u1_b])
            if j < QI:
                c = j
                xs = xssm[c % 2]
                tix = bi * QI * 3 + c * 3
                b.op("v", lambda v, xs=xs, pp=pp: v.tensor_copy(
                    xs[0][:, 3:3 + TA], pp[0][:]), reads=[pp[1]], writes=[xs[1]])
                b.op("v", lambda v, xs=xs, tix=tix: v.tensor_copy(
                    xs[0][:, 0:3], tails[:, tix:tix + 3]), reads=[tl_b], writes=[xs[1]])
                b.op("v", lambda v, xs=xs, tix=tix: v.tensor_copy(
                    tails[:, tix:tix + 3], xs[0][:, TA:TA + 3]),
                    reads=[xs[1]], writes=[tl_b])
                ca, c2 = cvs[c % 2], cvs[2 + c % 2]
                b.op("v", lambda v, xs=xs, ca=ca, c=c: v.tensor_scalar(
                    ca[0][:], xs[0][:, 0:TA], cvw[:, 4 * c:4 * c + 1], None, OP.mult),
                    reads=[xs[1], cvw_b], writes=[ca[1]])
                b.op("v", lambda v, xs=xs, c2=c2, c=c: v.tensor_scalar(
                    c2[0][:], xs[0][:, 2:2 + TA], cvw[:, 4 * c + 2:4 * c + 3], None, OP.mult),
                    reads=[xs[1], cvw_b], writes=[c2[1]])
                b.op("v", lambda v, xs=xs, ca=ca, c=c: v.scalar_tensor_tensor(
                    ca[0][:], xs[0][:, 1:1 + TA], cvw[:, 4 * c + 1:4 * c + 2], ca[0][:],
                    OP.mult, OP.add), reads=[xs[1], ca[1], cvw_b], writes=[ca[1]])
                b.op("v", lambda v, xs=xs, c2=c2, c=c: v.scalar_tensor_tensor(
                    c2[0][:], xs[0][:, 3:3 + TA], cvw[:, 4 * c + 3:4 * c + 4], c2[0][:],
                    OP.mult, OP.add), reads=[xs[1], c2[1], cvw_b], writes=[c2[1]])
                b.op("v", lambda v, ca=ca, c2=c2: v.tensor_tensor(
                    ca[0][:], ca[0][:], c2[0][:], OP.add),
                    reads=[ca[1], c2[1]], writes=[ca[1]])
                b.op("s", lambda s, ca=ca, c=c, xct=xct: s.activation(
                    xct[:, c, :], ca[0][:], AF.Silu, bias=cvb[:, c:c + 1]),
                    reads=[ca[1], cvb_b], writes=[xct_b])
            else:
                c = j - QI
                b.op("s", lambda s, c=c, pp=pp, szt=szt: s.activation(
                    szt[:, c, :], pp[0][:], AF.Silu), reads=[pp[1]], writes=[szt_b])
        b.dma(lambda y, bi=bi, t0=t0, xct=xct: y.dma_start(
            out=xc_d[bi, :, t0:t0 + TA].rearrange("(c p) t -> p c t", p=128),
            in_=xct[:]), reads=[xct_b])
        b.dma(lambda y, bi=bi, t0=t0, szt=szt: y.dma_start(
            out=sz_d[bi, :, t0:t0 + TA].rearrange("(c p) t -> p c t", p=128),
            in_=szt[:]), reads=[szt_b])

    b.barrier()
    pA1.close()

    pA2 = ExitStack()

    def sb2(name, shape, dt=F32):
        t = pA2.enter_context(nc.sbuf_tensor(name, list(shape), dt))
        return t, Buf()

    def ps2(name, shape, dt=F32):
        t = pA2.enter_context(nc.psum_tensor(name, list(shape), dt))
        return t, Buf()

    x0s2 = [sb2("x0b_0", (128, QM, TA))]
    xcs2 = [sb2(f"xc2_{i}", (128, QI, TA), BF16) for i in range(2)]
    szs2 = [sb2("sz2_0", (128, QI, TA), BF16)]
    xpbf, xpbf_b = sb2("xpbf", (33, TA), BF16)
    rsb, rsb_b = sb2("rsb", (128, TA), BF16)
    dts = [sb2(f"dts{i}", (128, TA), BF16) for i in range(2)]
    scrp = [sb2(f"scrp{i}", (128, TA)) for i in range(4)]
    spE = [sb2(f"spE{i}", (128, TA), BF16) for i in range(2)]
    spR = [sb2(f"spR{i}", (128, TA), BF16) for i in range(2)]
    spR2 = [sb2(f"spR2{i}", (128, TA), BF16) for i in range(2)]
    usc = [sb2(f"usc{i}", (128, TA), BF16) for i in range(2)]
    Bt, Bt_b = sb2("Bt", (128, NST, TA), BF16)
    Ct, Ct_b = sb2("Ct", (128, NST, TA), BF16)
    dAs = [sb2(f"dAs{i}", (128, 2, TA)) for i in range(2)]
    dbx = [sb2(f"dbx{i}", (128, 2, TA), BF16) for i in range(3)]
    hsl = [sb2(f"hsl{i}", (128, 2, TA), BF16) for i in range(3)]
    hcs = [sb2(f"hcs{i}", (128, 2, TA), BF16) for i in range(2)]
    ysc = [sb2(f"ysc{i}", (128, TA)) for i in range(2)]
    ygt, ygt_b = sb2("ygt", (128, QI, TA), BF16)
    mixb, mixb_b = sb2("mixb", (128, QM, TA), BF16)
    sqs2 = [sb2(f"sqs2_{i}", (128, TA), BF16) for i in range(2)]
    rows2 = [sb2("rows2_0", (1, 6 * TA))]
    lbf2 = [sb2("lbf2_0", (1, 4 * TA), BF16)]
    t1q2 = [scrp[0], scrp[1]]
    t2q2 = [scrp[2], scrp[3]]
    x1o = [spR[0], spR[1]]
    woutb = [sb2("woutb0", (128, QI * 128), BF16)]
    pout = [ps2(f"pout{i}", (128, TA)) for i in range(2)]
    pY, pY_b = ps2("pY", (128, TA))
    pst2, pst2_b = ps2("pst2", (1, TA))
    pln2, pln2_b = ps2("pln2", (128, 2 * TA))
    pmisc, pmisc_b = ps2("pmisc", (128, TA))
    pmisc2, pmisc2_b = ps2("pmisc2", (128, TA))

    ln_2, bcast_2, next_2, ln2_begin, ln2_feed, ln2_finish = make_ln(
        TA, (pst2[0:1, 0:TA], pst2_b), (pmisc2[0:1, 0:TA], pmisc2_b),
        [(rows2[0][0], rows2[0][1], lbf2[0][0], lbf2[0][1], sqs2)])

    states_bb = [Buf(), Buf()]
    for ui, (bi, ti) in enumerate(unitsA):
        par = ui % 2
        t0 = ti * TA
        x0, x0_b = x0s2[0]
        xct, xct_b = xcs2[par]
        sluz, sluz_b = szs2[0]
        st_b = states_bb[bi]
        b.dma(lambda y, bi=bi, t0=t0, x0=x0: y.dma_start(
            out=x0[:], in_=x_d[bi, :, t0:t0 + TA].rearrange("(q p) t -> p q t", p=128)),
            writes=[x0_b])
        b.dma(lambda y, bi=bi, t0=t0, xct=xct: y.dma_start(
            out=xct[:], in_=xc_d[bi, :, t0:t0 + TA].rearrange("(c p) t -> p c t", p=128)),
            writes=[xct_b])
        b.dma(lambda y, bi=bi, t0=t0, sluz=sluz: y.dma_start(
            out=sluz[:], in_=sz_d[bi, :, t0:t0 + TA].rearrange("(c p) t -> p c t", p=128)),
            writes=[sluz_b])
        g = GAcc(b, pmisc[0:33, :], pmisc_b, QI)
        for c in range(QI):
            g.add(xpw_bf[:, c, :], xct[:, c, :], [xpw_bf_b, xct_b])
        b.op("v", lambda v: v.tensor_copy(xpbf[:], pmisc[0:33, :]),
             reads=[pmisc_b], writes=[xpbf_b])
        GAcc(b, pmisc[:, :], pmisc_b, 1).add(selbf[:, 0:128], xpbf[:], [xpbf_b, selbf_b])
        b.op("s", lambda s: s.copy(rsb[:], pmisc[:, :]), reads=[pmisc_b], writes=[rsb_b])
        bc_banks = [(pmisc, pmisc_b), (pmisc2, pmisc2_b)]
        for n in range(NST):
            bk, bk_b = bc_banks[0]
            GAcc(b, bk[:, :], bk_b, 1).add(
                selbf[:, (1 + n) * 128:(2 + n) * 128], xpbf[:], [xpbf_b, selbf_b])
            b.op("s", lambda s, n=n, bk=bk: s.copy(Bt[:, n, :], bk[:, :]),
                 reads=[bk_b], writes=[Bt_b])
            bk, bk_b = bc_banks[1]
            GAcc(b, bk[:, :], bk_b, 1).add(
                selbf[:, (17 + n) * 128:(18 + n) * 128], xpbf[:], [xpbf_b, selbf_b])
            b.op("s", lambda s, n=n, bk=bk: s.copy(Ct[:, n, :], bk[:, :]),
                 reads=[bk_b], writes=[Ct_b])

        def softplus_pre(c):
            pre, ab = scrp[c % 2], scrp[2 + c % 2]
            er, rl, ln_ = spE[c % 2], spR[c % 2], spR2[c % 2]
            b.op("v", lambda v, c=c, pre=pre: v.tensor_scalar(
                pre[0][:], rsb[:], dtw[:, c:c + 1], dtb[:, c:c + 1], OP.mult, OP.add),
                reads=[rsb_b, dtw_b, dtb_b], writes=[pre[1]])
            b.op("s", lambda s, pre=pre, rl=rl: s.activation(rl[0][:], pre[0][:], AF.Relu),
                 reads=[pre[1]], writes=[rl[1]])
            b.op("s", lambda s, pre=pre, ab=ab: s.activation(ab[0][:], pre[0][:], AF.Abs),
                 reads=[pre[1]], writes=[ab[1]])
            b.op("s", lambda s, ab=ab, er=er: s.activation(
                er[0][:], ab[0][:], AF.Exp, scale=-1.0), reads=[ab[1]], writes=[er[1]])
            b.op("s", lambda s, er=er, ln_=ln_: s.activation(
                ln_[0][:], er[0][:], AF.Ln, bias=1.0), reads=[er[1]], writes=[ln_[1]])

        def softplus_tail(c, xct=xct, xct_b=xct_b):
            dtc = dts[c % 2]
            rl, ln_ = spR[c % 2], spR2[c % 2]
            b.op("v", lambda v, dtc=dtc, rl=rl, ln_=ln_: v.tensor_tensor(
                dtc[0][:], rl[0][:], ln_[0][:], OP.add),
                reads=[rl[1], ln_[1]], writes=[dtc[1]])
            uc = usc[c % 2]
            b.op("v", lambda v, c=c, dtc=dtc, uc=uc: v.tensor_tensor(
                uc[0][:], dtc[0][:], xct[:, c, :], OP.mult),
                reads=[dtc[1], xct_b], writes=[uc[1]])

        softplus_pre(0)
        softplus_tail(0)
        NGG = NST // 2
        for c in range(QI):
            dtc = dts[c % 2]
            uc = usc[c % 2]
            gy = GAcc(b, pY[:], pY_b, NST)
            for k in range(NGG + 2):
                if k == 1 and c + 1 < QI:
                    softplus_pre(c + 1)
                if k == 6 and c + 1 < QI:
                    softplus_tail(c + 1)
                if k < NGG:
                    gg = k
                    da, dx = dAs[gg % 2], dbx[gg % 3]
                    for i in range(2):
                        n = 2 * gg + i
                        b.op("s", lambda s, c=c, n=n, i=i, da=da, dtc=dtc: s.activation(
                            da[0][:, i, :], dtc[0][:], AF.Exp,
                            scale=Ares[:, c * NST + n:c * NST + n + 1]),
                            reads=[dtc[1], Ares_b], writes=[da[1]])
                    b.op("v", lambda v, gg=gg, dx=dx, uc=uc: v.tensor_tensor(
                        dx[0][:], uc[0][:].unsqueeze(1).broadcast_to([128, 2, TA]),
                        Bt[:, 2 * gg:2 * gg + 2, :], OP.mult),
                        reads=[uc[1], Bt_b], writes=[dx[1]])
                if 1 <= k <= NGG:
                    gg = k - 1
                    da, dx, hh = dAs[gg % 2], dbx[gg % 3], hsl[gg % 3]
                    for i in range(2):
                        n = 2 * gg + i
                        sidx = bi * QI * NST + c * NST + n
                        if ti == 0:
                            init = 0.0
                            rds = [da[1], dx[1]]
                        else:
                            init = states[:, sidx:sidx + 1]
                            rds = [da[1], dx[1], st_b]
                        b.op("v", lambda v, i=i, da=da, dx=dx, hh=hh, init=init:
                             v.tensor_tensor_scan(hh[0][:, i, :], da[0][:, i, :],
                                                  dx[0][:, i, :], init, OP.mult, OP.add),
                             reads=rds, writes=[hh[1]])
                if k >= 2:
                    gg = k - 2
                    hh, hc = hsl[gg % 3], hcs[gg % 2]
                    b.op("v", lambda v, gg=gg, hh=hh, hc=hc: v.tensor_tensor(
                        hc[0][:], hh[0][:], Ct[:, 2 * gg:2 * gg + 2, :], OP.mult),
                        reads=[hh[1], Ct_b], writes=[hc[1]])
                    s0 = bi * QI * NST + c * NST + 2 * gg
                    b.op("v", lambda v, hh=hh, s0=s0: v.tensor_copy(
                        states[:, s0:s0 + 2],
                        hh[0][:, :, TA - 1:TA].rearrange("p n o -> p (n o)")),
                        reads=[hh[1]], writes=[st_b])
                    for i in range(2):
                        gy.add(eye_bf[:], hc[0][:, i, :], [hc[1], eye_bf_b])
            ys = ysc[c % 2]
            b.op("v", lambda v, c=c, ys=ys, xct=xct: v.scalar_tensor_tensor(
                ys[0][:], xct[:, c, :], dvv[:, c:c + 1], pY[:], OP.mult, OP.add),
                reads=[xct_b, dvv_b, pY_b], writes=[ys[1]])
            b.op("v", lambda v, c=c, ys=ys, sluz=sluz: v.tensor_tensor(
                ygt[:, c, :], ys[0][:], sluz[:, c, :], OP.mult),
                reads=[ys[1], sluz_b], writes=[ygt_b])

        ln2_begin(QM)
        for j in range(QM):
            wb = woutb[0]
            b.dma(lambda y, j=j, wb=wb: y.dma_start(
                out=wb[0][:], in_=wbf_out_d[j]), writes=[wb[1]])
            pp = pout[j % 2]
            g = GAcc(b, pp[0][:], pp[1], QI)
            for c in range(QI):
                g.add(wb[0][:, c * 128:(c + 1) * 128], ygt[:, c, :], [wb[1], ygt_b])
            b.op("s", lambda s, j=j, pp=pp: s.copy(mixb[:, j, :], pp[0][:]),
                 reads=[pp[1]], writes=[mixb_b])
            ln2_feed(mixb[:, j, :], [mixb_b])
        ln2_finish(QM, False)
        bcast_2(pln2[:, 0:TA], pln2_b, 0)
        bcast_2(pln2[:, TA:2 * TA], pln2_b, 2)
        next_2()
        for q in range(QM):
            tq, tq2, xo = t1q2[q % 2], t2q2[q % 2], x1o[q % 2]
            b.op("v", lambda v, q=q, tq=tq: v.tensor_tensor(
                tq[0][:], mixb[:, q, :], pln2[:, 0:TA], OP.mult),
                reads=[mixb_b, pln2_b], writes=[tq[1]])
            b.op("v", lambda v, q=q, tq=tq, tq2=tq2, x0=x0: v.tensor_tensor(
                tq2[0][:], tq[0][:], x0[:, q, :], OP.add),
                reads=[tq[1], x0_b], writes=[tq2[1]])
            b.op("v", lambda v, q=q, tq2=tq2, xo=xo: v.tensor_tensor(
                xo[0][:], tq2[0][:], pln2[:, TA:2 * TA], OP.subtract),
                reads=[tq2[1], pln2_b], writes=[xo[1]])
            b.dma(lambda y, bi=bi, t0=t0, q=q, xo=xo: y.dma_start(
                out=x1_d[bi, 128 * q:128 * (q + 1), t0:t0 + TA], in_=xo[0][:]),
                reads=[xo[1]])

    b.barrier()
    pA2.close()

    pB = ExitStack()

    def sbB(name, shape, dt=F32):
        t = pB.enter_context(nc.sbuf_tensor(name, list(shape), dt))
        return t, Buf()

    def psB(name, shape, dt=F32):
        t = pB.enter_context(nc.psum_tensor(name, list(shape), dt))
        return t, Buf()

    x1s = [sbB(f"x1s{i}", (128, QM, TB), BF16) for i in range(4)]
    qtmp, qtmp_b = sbB("qtmp", (128, 2 * TB))
    vqs = [sbB(f"vq{i}", (128, QM, TB), BF16) for i in range(2)]
    gts = [sbB(f"gt{i}", (128, QH, TB), BF16) for i in range(2)]
    fts, fts_b = sbB("fts", (128, QM, 2 * TB), BF16)
    fbA = [sbB("fbA0", (128, 2 * TB), BF16)]

    def qset(tag):
        return (sbB("zsc" + tag, (128, 2 * TB)),
                sbB("qw1" + tag, (128, 2 * TB), BF16),
                sbB("qw2" + tag, (128, 2 * TB), BF16),
                sbB("qdd" + tag, (128, 2 * TB), BF16))

    qsetA, qsetB2 = qset("A"), qset("B")
    m1A = sbB("m1A", (128, TB), BF16)
    m1B = sbB("m1B", (128, TB), BF16)
    colmA = sbB("colmA", (128, 10))
    colmB = sbB("colmB", (128, 10))
    colbfA = sbB("colbfA", (128, 8), BF16)
    colbfB = sbB("colbfB", (128, 8), BF16)
    rows4A = sbB("rows4A", (4, TB), BF16)
    rows4B = sbB("rows4B", (4, TB), BF16)
    alf = [sbB(f"alf{i}", (128, TB)) for i in range(2)]
    gsc = [sbB(f"gsc{i}", (128, TB)) for i in range(2)]
    gsf = [sbB(f"gsf{i}", (128, TB)) for i in range(2)]
    btf, btf_b = sbB("btf", (128, TB))
    t1qB = [sbB(f"t1qB{i}", (128, TB)) for i in range(2)]
    ftq = [sbB(f"ftq{i}", (128, TB)) for i in range(2)]
    x2o = [sbB(f"x2o{i}", (128, TB)) for i in range(2)]
    rowsB1 = sbB("rowsB1", (1, 6 * TB))
    lbfB1 = sbB("lbfB1", (1, 4 * TB), BF16)
    rowsBF = sbB("rowsBF", (1, 6 * TB))
    lbfF = sbB("lbfF", (1, 4 * TB), BF16)
    wq1r, wq1r_b = sbB("wq1r", (128, QM * QH * 128), BF16)
    wq2s = [sbB(f"wq2s{i}", (128, QH * 128), BF16) for i in range(2)]

    pm1a = psB("pm1a", (128, TB))
    pm1b = psB("pm1b", (128, TB))
    pm2a = psB("pm2a", (128, TB))
    pm2b = psB("pm2b", (128, TB))
    pbcA = psB("pbcA", (128, 2 * TB))
    pbcB = psB("pbcB", (128, 2 * TB))
    ptrA = psB("ptrA", (128, 2 * TB))
    ptrB = psB("ptrB", (128, 2 * TB))
    stA = (ptrA[0], ptrA[1])

    for j in range(QH):
        b.dma(lambda y, j=j: y.dma_start(
            out=wq1r[:, j * QM * 128:(j + 1) * QM * 128], in_=wq1_d[j]),
            writes=[wq1r_b])

    def ln_finishB(stat_ap, stat_buf, rowsp, lbfp, fused):
        rows, rows_b = rowsp
        lbf, lbf_b = lbfp
        R = lambda i: rows[:, i * TB:(i + 1) * TB]
        H = lambda i: lbf[:, i * TB:(i + 1) * TB]
        RW = [rows_b]
        nfeat = float(QM * 128)
        b.op("v", lambda v: v.tensor_copy(rows[:, 0:2 * TB], stat_ap),
             reads=[stat_buf], writes=RW)
        b.op("v", lambda v: v.tensor_scalar(R(2), R(0), 1.0 / nfeat, None, OP.mult),
             reads=RW, writes=RW)
        b.op("v", lambda v: v.tensor_tensor(R(4), R(2), R(2), OP.mult),
             reads=RW, writes=RW)
        b.op("v", lambda v: v.scalar_tensor_tensor(R(5), R(1), 1.0 / nfeat, R(4),
                                                   OP.mult, OP.subtract),
             reads=RW, writes=RW)
        if fused:
            b.op("v", lambda v: v.tensor_scalar(R(1), R(5), 1.0 + EPS, EPS * EPS,
                                                OP.mult, OP.add),
                 reads=RW, writes=RW)
        else:
            b.op("v", lambda v: v.tensor_scalar(R(1), R(5), EPS, None, OP.add),
                 reads=RW, writes=RW)
        b.op("v", lambda v: v.reciprocal(R(4), R(1)), reads=RW, writes=RW)
        b.op("s", lambda s: s.activation(R(0), R(4), AF.Sqrt),
             reads=RW, writes=RW)
        b.op("s", lambda s: s.copy(H(0), R(0)), reads=RW, writes=[lbf_b])
        b.op("v", lambda v: v.tensor_tensor(H(1), R(0), H(0), OP.subtract),
             reads=RW + [lbf_b], writes=[lbf_b])
        b.op("v", lambda v: v.tensor_tensor(R(3), R(2), R(0), OP.mult),
             reads=RW, writes=RW)
        b.op("s", lambda s: s.copy(H(2), R(3)), reads=RW, writes=[lbf_b])
        b.op("v", lambda v: v.tensor_tensor(H(3), R(3), H(2), OP.subtract),
             reads=RW + [lbf_b], writes=[lbf_b])

    def bcastB2(lbfp, off, psum_ap, psum_buf):
        lbf, lbf_b = lbfp
        g = GAcc(b, psum_ap, psum_buf, 2)
        g.add(orow_bf[:], lbf[:, off * TB:(off + 1) * TB], [lbf_b, orow_bf_b])
        g.add(orow_bf[:], lbf[:, (off + 1) * TB:(off + 2) * TB], [lbf_b, orow_bf_b])

    def absmaxB(src_ap, src_rds, m1p, colmp, cbfp, r4p, ptrp, swi_pair):
        m1, m1_b = m1p
        colm, colm_b = colmp
        cbf, cbf_b = cbfp
        r4, r4_b = r4p
        ptr_t, ptr_b2 = ptrp
        swi, swi_b = swi_pair
        b.op("v", lambda v: v.tensor_reduce(
            m1[:], src_ap.rearrange("p q t -> p t q"), AX.X, OP.max,
            apply_absolute_value=True), reads=list(src_rds), writes=[m1_b])
        yield
        for g_ in range(TB // 128):
            GAcc(b, ptr_t[:, g_ * 128:(g_ + 1) * 128], ptr_b2, 1).add(
                m1[:, g_ * 128:(g_ + 1) * 128], eye_bf[:], [m1_b, eye_bf_b])
            b.op("v", lambda v, g_=g_: v.tensor_reduce(
                colm[:, g_:g_ + 1], ptr_t[:, g_ * 128:(g_ + 1) * 128], AX.X, OP.max),
                reads=[ptr_b2], writes=[colm_b])
        yield
        b.op("v", lambda v: v.tensor_scalar(colm[:, 2:4], colm[:, 0:2], 1e-5, None,
                                            OP.max), reads=[colm_b], writes=[colm_b])
        b.op("v", lambda v: v.reciprocal(colm[:, 4:6], colm[:, 2:4]),
             reads=[colm_b], writes=[colm_b])
        b.op("v", lambda v: v.tensor_scalar(colm[:, 6:8], colm[:, 4:6], 127.0, None,
                                            OP.mult), reads=[colm_b], writes=[colm_b])
        b.op("v", lambda v: v.tensor_scalar(colm[:, 8:10], colm[:, 2:4], swi[:],
                                            1.0 / 127.0, OP.mult, OP.mult),
             reads=[colm_b, swi_b], writes=[colm_b])
        cb3 = cbf[:].rearrange("p (g k) -> p g k", k=4)
        yield
        b.op("v", lambda v: v.tensor_copy(cb3[:, :, 0:1], colm[:, 6:8].unsqueeze(2)),
             reads=[colm_b], writes=[cbf_b])
        b.op("v", lambda v: v.tensor_tensor(cb3[:, :, 1:2], colm[:, 6:8].unsqueeze(2),
                                            cb3[:, :, 0:1], OP.subtract),
             reads=[colm_b, cbf_b], writes=[cbf_b])
        b.op("v", lambda v: v.tensor_copy(cb3[:, :, 2:3], colm[:, 8:10].unsqueeze(2)),
             reads=[colm_b], writes=[cbf_b])
        b.op("v", lambda v: v.tensor_tensor(cb3[:, :, 3:4], colm[:, 8:10].unsqueeze(2),
                                            cb3[:, :, 2:3], OP.subtract),
             reads=[colm_b, cbf_b], writes=[cbf_b])
        for g_ in range(TB // 128):
            GAcc(b, ptr_t[0:4, 256 + g_ * 128:256 + (g_ + 1) * 128], ptr_b2, 1).add(
                cbf[:, g_ * 4:(g_ + 1) * 4], eye_bf[:], [cbf_b, eye_bf_b])
        b.op("v", lambda v: v.tensor_copy(r4[:, 0:TB], ptr_t[0:4, 256:256 + TB]),
             reads=[ptr_b2], writes=[r4_b])

    def selb4(r4p, off, psum_ap, psum_buf):
        r4, r4_b = r4p
        g = GAcc(b, psum_ap, psum_buf, 2)
        g.add(selbf[0:4, off * 128:(off + 1) * 128], r4[:], [r4_b, selbf_b])
        g.add(selbf[0:4, (off + 1) * 128:(off + 2) * 128], r4[:], [r4_b, selbf_b])

    def quantB(src3, src_rds, s_ap, s_buf, out3, out_buf, qs, smagic=False):
        (zs, zs_b), (q1, q1_b), (q2, q2_b), (qd, qd_b) = qs
        z3 = zs[:].rearrange("p (q t) -> p q t", t=TB)
        q13 = q1[:].rearrange("p (q t) -> p q t", t=TB)
        qd3 = qd[:].rearrange("p (q t) -> p q t", t=TB)
        b.op("v", lambda v: v.tensor_tensor(
            z3, src3, s_ap.unsqueeze(1).broadcast_to([128, 2, TB]), OP.mult),
            reads=list(src_rds) + [s_buf], writes=[zs_b])
        if smagic:
            b.op("s", lambda s: s.activation(qtmp[:], zs[:], AF.Identity,
                                             bias=m32c[:]),
                 reads=[zs_b, m32c_b], writes=[qtmp_b])
            b.op("s", lambda s: s.activation(q1[:], qtmp[:], AF.Identity,
                                             bias=nm32c[:]),
                 reads=[qtmp_b, nm32c_b], writes=[q1_b])
            b.op("s", lambda s: s.activation(qtmp[:], zs[:], AF.Identity,
                                             bias=m64c[:]),
                 reads=[zs_b, m64c_b], writes=[qtmp_b])
            b.op("s", lambda s: s.activation(q2[:], qtmp[:], AF.Identity,
                                             bias=nm64c[:]),
                 reads=[qtmp_b, nm64c_b], writes=[q2_b])
        else:
            b.op("v", lambda v: v.tensor_scalar(q1[:], zs[:], M32, -M32, OP.add,
                                                OP.add), reads=[zs_b], writes=[q1_b])
            b.op("v", lambda v: v.tensor_scalar(q2[:], zs[:], M64, -M64, OP.add,
                                                OP.add), reads=[zs_b], writes=[q2_b])
        b.op("v", lambda v: v.tensor_tensor(qd[:], q2[:], q1[:], OP.subtract),
             reads=[q1_b, q2_b], writes=[qd_b])
        b.op("v", lambda v: v.scalar_tensor_tensor(out3, qd3, ublend[:], q13,
                                                   OP.mult, OP.add),
             reads=[qd_b, q1_b, ublend_b], writes=[out_buf])

    unitsB = [(bb, t) for t in range(NTB) for bb in range(BL)]
    NU = len(unitsB)

    def dma_x1(u):
        bi, ti = unitsB[u]
        t0 = ti * TB
        x1, x1_b = x1s[u % 4]
        b.dma(lambda y, bi=bi, t0=t0, x1=x1: y.dma_start(
            out=x1[:], in_=x1_d[bi, :, t0:t0 + TB].rearrange("(q p) t -> p q t", p=128)),
            writes=[x1_b])

    def gen_P1(u):
        x1, x1_b = x1s[u % 4]
        vq, vq_b = vqs[u % 2]
        al, al_b = alf[u % 2]
        g1 = GAcc(b, stA[0][0:1, 0:2 * TB], stA[1], QM)
        for q in range(QM):
            fb, fb_b = fbA[0]
            b.op("s", lambda s, q=q, fb=fb: s.copy(fb[:, 0:TB], x1[:, q, :]),
                 reads=[x1_b], writes=[fb_b])
            b.op("s", lambda s, q=q, fb=fb: s.activation(
                fb[:, TB:2 * TB], x1[:, q, :], AF.Square),
                reads=[x1_b], writes=[fb_b])
            g1.add(ocol_bf[:], fb[:], [fb_b, ocol_bf_b])
            if q % 2 == 1:
                yield
        ln_finishB(stA[0][0:1, 0:2 * TB], stA[1], rowsB1, lbfB1, True)
        yield
        bcastB2(lbfB1, 0, pbcA[0][:, 0:TB], pbcA[1])
        bcastB2(lbfB1, 2, pbcA[0][:, TB:2 * TB], pbcA[1])
        yield
        zsA, zsA_b = qsetA[0]
        for q in range(QM):
            tq, tq_b = t1qB[q % 2]
            t2 = zsA[:, (q % 2) * TB:(q % 2) * TB + TB]
            b.op("v", lambda v, q=q, tq=tq: v.tensor_tensor(
                tq[:], x1[:, q, :], pbcA[0][:, 0:TB], OP.mult),
                reads=[x1_b, pbcA[1]], writes=[tq_b])
            b.op("v", lambda v, q=q, tq=tq, t2=t2: v.tensor_tensor(
                t2, tq[:], pbcA[0][:, TB:2 * TB], OP.subtract),
                reads=[tq_b, pbcA[1]], writes=[zsA_b])
            b.op("s", lambda s, q=q, t2=t2: s.activation(
                vq[:, q, :], t2, AF.Identity,
                scale=gam[:, q:q + 1], bias=bet[:, q:q + 1]),
                reads=[zsA_b, gam_b, bet_b], writes=[vq_b])
            if q % 2 == 1:
                yield
        if dbg and u == 0:
            tap("v", vq[:].rearrange("p q t -> p (q t)"), (128, QM * TB), [vq_b], BF16)
        yield from absmaxB(vq[:], [vq_b], m1A, colmA, colbfA, rows4A, ptrA,
                           (sw1i_vec, sw1i_vec_b))
        yield
        selb4(rows4A, 0, pbcA[0][:, 0:TB], pbcA[1])
        selb4(rows4A, 2, pbcA[0][:, TB:2 * TB], pbcA[1])
        yield
        for h in range(4):
            quantB(vq[:, 2 * h:2 * h + 2, :], [vq_b], pbcA[0][:, 0:TB], pbcA[1],
                   vq[:, 2 * h:2 * h + 2, :], vq_b, qsetA)
            yield
        b.op("v", lambda v, al=al: v.tensor_copy(al[:], pbcA[0][:, TB:2 * TB]),
             reads=[pbcA[1]], writes=[al_b])

    def gen_P2(u):
        vq, vq_b = vqs[u % 2]
        al, al_b = alf[u % 2]
        gt, gt_b = gts[u % 2]
        b.op("v", lambda v: v.tensor_tensor(
            vq[:], vq[:], al[:].unsqueeze(1).broadcast_to([128, QM, TB]), OP.mult),
            reads=[al_b], writes=[vq_b])
        for j in range(QH):
            pp, pp_b = pm1a if j % 2 == 0 else pm1b
            g = GAcc(b, pp[:], pp_b, QM)
            for k in range(QM):
                g.add(wq1r[:, (j * QM + k) * 128:(j * QM + k + 1) * 128],
                      vq[:, k, :], [wq1r_b, vq_b])
            gs, gs_b = gsc[j % 2]
            b.op("s", lambda s, pp=pp, gs=gs: s.copy(gs[:], pp[:]),
                 reads=[pp_b], writes=[gs_b])
            b.op("s", lambda s, j=j, gs=gs, gt=gt: s.activation(
                gt[:, j, :], gs[:], AF.Gelu, bias=b1v[:, j:j + 1]),
                reads=[gs_b, b1v_b], writes=[gt_b])
            yield
        if dbg and u == 0:
            tap("g", gt[:].rearrange("p q t -> p (q t)"), (128, QH * TB), [gt_b], BF16)

    def gen_P3(u):
        bi, ti = unitsB[u]
        t0 = ti * TB
        x1, x1_b = x1s[u % 4]
        gt, gt_b = gts[u % 2]
        ws0, ws0_b = wq2s[0]
        b.dma(lambda y, ws0=ws0: y.dma_start(out=ws0[:], in_=wq2_d[0]),
              writes=[ws0_b])
        yield from absmaxB(gt[:], [gt_b], m1B, colmB, colbfB, rows4B, ptrB,
                           (sw2i_vec, sw2i_vec_b))
        yield
        selb4(rows4B, 0, pbcB[0][:, 0:TB], pbcB[1])
        selb4(rows4B, 2, pbcB[0][:, TB:2 * TB], pbcB[1])
        b.op("v", lambda v: v.tensor_copy(btf[:], pbcB[0][:, TB:2 * TB]),
             reads=[pbcB[1]], writes=[btf_b])
        yield
        for h in range(16):
            quantB(gt[:, 2 * h:2 * h + 2, :], [gt_b], pbcB[0][:, 0:TB], pbcB[1],
                   gt[:, 2 * h:2 * h + 2, :], gt_b, qsetB2)
            yield
        gf = GAcc(b, ptrB[0][0:1, 0:2 * TB], ptrB[1], QM)
        for j in range(QM):
            if j + 1 < QM:
                wsn, wsn_b = wq2s[(j + 1) % 2]
                b.dma(lambda y, j=j, wsn=wsn: y.dma_start(
                    out=wsn[:], in_=wq2_d[j + 1]), writes=[wsn_b])
            ws, ws_b = wq2s[j % 2]
            pp, pp_b = pm2a if j % 2 == 0 else pm2b
            g = GAcc(b, pp[:], pp_b, QH)
            for k in range(QH):
                g.add(ws[:, k * 128:(k + 1) * 128], gt[:, k, :], [ws_b, gt_b])
            gs, gs_b = gsf[j % 2]
            b.op("v", lambda v, pp=pp, gs=gs: v.tensor_tensor(
                gs[:], pp[:], btf[:], OP.mult),
                reads=[pp_b, btf_b], writes=[gs_b])
            b.op("s", lambda s, j=j, gs=gs: s.activation(
                fts[:, j, 0:TB], gs[:], AF.Identity, bias=b2v[:, j:j + 1]),
                reads=[gs_b, b2v_b], writes=[fts_b])
            b.op("s", lambda s, j=j: s.activation(
                fts[:, j, TB:2 * TB], fts[:, j, 0:TB], AF.Square),
                reads=[fts_b], writes=[fts_b])
            gf.add(ocol_bf[:], fts[:, j, :], [fts_b, ocol_bf_b])
            yield
        ln_finishB(ptrB[0][0:1, 0:2 * TB], ptrB[1], rowsBF, lbfF, False)
        yield
        bcastB2(lbfF, 0, pbcB[0][:, 0:TB], pbcB[1])
        bcastB2(lbfF, 2, pbcB[0][:, TB:2 * TB], pbcB[1])
        yield
        for q in range(QM):
            tq, tq_b = ftq[q % 2]
            t2, t2_b = x2o[q % 2]
            b.op("v", lambda v, q=q, tq=tq: v.tensor_tensor(
                tq[:], fts[:, q, 0:TB], pbcB[0][:, 0:TB], OP.mult),
                reads=[fts_b, pbcB[1]], writes=[tq_b])
            b.op("v", lambda v, q=q, tq=tq, t2=t2: v.tensor_tensor(
                t2[:], tq[:], x1[:, q, :], OP.add),
                reads=[tq_b, x1_b], writes=[t2_b])
            b.op("v", lambda v, q=q, tq=tq, t2=t2: v.tensor_tensor(
                tq[:], t2[:], pbcB[0][:, TB:2 * TB], OP.subtract),
                reads=[t2_b, pbcB[1]], writes=[tq_b])
            b.dma(lambda y, bi=bi, t0=t0, q=q, tq=tq: y.dma_start(
                out=out_d[bi, 128 * q:128 * (q + 1), t0:t0 + TB], in_=tq[:]),
                reads=[tq_b])
            if q % 2 == 1:
                yield

    def step(g, n=1):
        if g is None:
            return None
        try:
            for _ in range(n):
                next(g)
            return g
        except StopIteration:
            return None

    dma_x1(0)
    dma_x1(1)
    for c in range(NU + 2):
        p2 = gen_P2(c - 1) if 0 <= c - 1 < NU else None
        p3 = gen_P3(c - 2) if 0 <= c - 2 < NU else None
        p1 = gen_P1(c) if c < NU else None
        p3 = step(p3)
        p2 = step(p2, 16)
        p3 = step(p3, 3)
        p1_head = 6
        while p3 is not None or p2 is not None or p1 is not None:
            if p3 is not None:
                p3 = step(p3)
            if p2 is not None:
                p2 = step(p2)
            if p1 is not None and (p1_head > 0 or p2 is None):
                p1 = step(p1)
                p1_head -= 1
        if c + 2 < NU:
            dma_x1(c + 2)

    b.barrier()
    pB.close()
    es.close()
    return nc


_NC_CACHE = {}


def kernel(**inputs):
    g = _host_prep(inputs)
    if "nc" not in _NC_CACHE:
        _NC_CACHE["nc"] = build_nc()
    nc = _NC_CACHE["nc"]
    base = {k: v for k, v in g.items() if k != "xT"}
    in_maps = []
    for c in range(8):
        m = dict(base)
        m["x"] = np.ascontiguousarray(g["xT"][2 * c:2 * c + 2])
        in_maps.append(m)
    res = run_bass_kernel_spmd(nc, in_maps, core_ids=list(range(8)))
    outs = [np.asarray(res.results[c]["out"]) for c in range(8)]
    full = np.concatenate(outs, axis=0)
    return np.ascontiguousarray(full.transpose(0, 2, 1)).astype(np.float32)


# version 21
# speedup vs baseline: 1.0059x; 1.0042x over previous
import numpy as np
from contextlib import ExitStack

import concourse.bass as bass
import concourse.mybir as mybir
from concourse.bass_utils import run_bass_kernel_spmd

F32 = mybir.dt.float32
BF16 = mybir.dt.bfloat16
AF = mybir.ActivationFunctionType
OP = mybir.AluOpType
AX = mybir.AxisListType

M32 = float(np.float32(1.5 * 2 ** 23))
M64 = float(np.float32(1.5 * 2 ** 52))
EPS = 1e-5

L, DM, DI, NST, HID = 2048, 1024, 2048, 16, 4096
QM, QI, QH = DM // 128, DI // 128, HID // 128
BL = 2
TA, TB = 512, 256
NTA, NTB = L // TA, L // TB


class Buf:
    __slots__ = ("w", "r")

    def __init__(self):
        self.w = {}
        self.r = {}


def _merge(d, tok):
    sk, v = tok
    if d.get(sk, 0) < v:
        d[sk] = v


NQPOOL = 32


class Bld:
    def __init__(self, nc, es):
        self.nc = nc
        self.sem = {}
        self.cnt = {}
        for name in ("v", "s", "t"):
            self.sem[name] = es.enter_context(nc.semaphore("sem_" + name))
            self.cnt[name] = 0
        for i in range(NQPOOL):
            self.sem[f"q{i}"] = es.enter_context(nc.semaphore(f"sem_q{i}"))
            self.cnt[f"q{i}"] = 0
        self.qi = 0
        self.eng = dict(v=nc.vector, s=nc.scalar, t=nc.tensor, y=nc.sync)
        self.wm = {e: {} for e in ("v", "s", "t", "y")}

    def _wait(self, ekey, bufs_r, bufs_w):
        need = {}
        for b in bufs_r:
            for sk, v in b.w.items():
                if v > need.get(sk, 0):
                    need[sk] = v
        for b in bufs_w:
            for d in (b.w, b.r):
                for sk, v in d.items():
                    if v > need.get(sk, 0):
                        need[sk] = v
        eng = self.eng[ekey]
        wm = self.wm[ekey]
        for sk, val in need.items():
            if wm.get(sk, 0) < val:
                eng.wait_ge(self.sem[sk], val)
                wm[sk] = val

    def op(self, ekey, fn, reads=(), writes=()):
        self._wait(ekey, reads, writes)
        instr = fn(self.eng[ekey])
        self.cnt[ekey] += 1
        instr.then_inc(self.sem[ekey], 1)
        tok = (ekey, self.cnt[ekey])
        for b in writes:
            _merge(b.w, tok)
        for b in reads:
            _merge(b.r, tok)
        return tok

    def dma(self, fn, reads=(), writes=()):
        sk = f"q{self.qi}"
        self.qi = (self.qi + 1) % NQPOOL
        prior = Buf()
        prior.w[sk] = self.cnt[sk]
        self._wait("y", list(reads) + [prior], writes)
        fn(self.eng["y"]).then_inc(self.sem[sk], 16)
        self.cnt[sk] += 16
        tok = (sk, self.cnt[sk])
        for b in writes:
            _merge(b.w, tok)
        for b in reads:
            _merge(b.r, tok)
        return tok

    def barrier(self):
        fake = Buf()
        for k in self.cnt:
            fake.w[k] = self.cnt[k]
        for e in ("v", "s", "t", "y"):
            self._wait(e, [fake], [])


class GAcc:

    def __init__(self, bld, psum_ap, psum_buf, n):
        self.b, self.ap, self.buf, self.n = bld, psum_ap, psum_buf, n
        self.i = 0
        self.reads = []

    def add(self, lt, rh, reads):
        reads = list(reads)
        first, last = self.i == 0, self.i == self.n - 1
        self.b._wait("t", reads, [self.buf] if first else [])
        ins = self.b.eng["t"].matmul(self.ap, lt, rh, start=first, stop=last)
        self.i += 1
        self.b.cnt["t"] += 1
        ins.then_inc(self.b.sem["t"], 1)
        tok = ("t", self.b.cnt["t"])
        for r in reads:
            _merge(r.r, tok)
        if last:
            _merge(self.buf.w, tok)
            return tok
        return None


def _host_prep(inputs):
    g = {}
    x = np.asarray(inputs["L0_attn_x"], dtype=np.float32)
    g["xT"] = np.ascontiguousarray(x.transpose(0, 2, 1))

    def tiles(wT, nk, nm):
        K, M = wT.shape
        a = wT.reshape(nk, 128, nm, 128)
        return np.ascontiguousarray(a.transpose(2, 0, 1, 3))

    w_in_T = np.ascontiguousarray(np.asarray(inputs["L0_attn_in_proj_w"], np.float32).T)
    w_out_T = np.ascontiguousarray(np.asarray(inputs["L0_attn_out_proj_w"], np.float32).T)
    w_xp_T = np.ascontiguousarray(np.asarray(inputs["L0_attn_x_proj_w"], np.float32).T)
    w1_T = np.ascontiguousarray(np.asarray(inputs["L0_ffn_w1"], np.float32).T)
    w2_T = np.ascontiguousarray(np.asarray(inputs["L0_ffn_w2"], np.float32).T)

    g["w_in"] = tiles(w_in_T, QM, QM * 4)
    g["w_out"] = tiles(w_out_T, QI, QM)
    g["w1"] = tiles(w1_T, QM, QH)
    g["w2"] = tiles(w2_T, QH, QM)
    g["w_xp"] = np.ascontiguousarray(w_xp_T.reshape(QI, 128, 33).transpose(1, 0, 2))

    def chunkvec(v, q):
        return np.ascontiguousarray(np.asarray(v, np.float32).reshape(q, 128).T)

    g["conv_w"] = np.ascontiguousarray(
        np.asarray(inputs["L0_attn_conv_w"], np.float32).reshape(DI, 4)
        .reshape(QI, 128, 4).transpose(1, 0, 2).reshape(128, QI * 4))
    g["conv_b"] = chunkvec(inputs["L0_attn_conv_b"], QI)
    g["dt_w"] = chunkvec(np.asarray(inputs["L0_attn_dt_proj_w"], np.float32).reshape(DI), QI)
    g["dt_b"] = chunkvec(inputs["L0_attn_dt_proj_b"], QI)
    g["a_log"] = np.ascontiguousarray(
        np.asarray(inputs["L0_attn_A_log"], np.float32)
        .reshape(QI, 128, NST).transpose(1, 0, 2).reshape(128, QI * NST))
    g["dvec"] = chunkvec(inputs["L0_attn_D"], QI)
    g["b1"] = chunkvec(inputs["L0_ffn_b1"], QH)
    g["b2"] = chunkvec(inputs["L0_ffn_b2"], QM)
    g["gamma"] = chunkvec(inputs["L0_ffn_gamma"], QM)
    g["beta"] = chunkvec(inputs["L0_ffn_beta"], QM)
    g["ones_row"] = np.ones((1, 128), np.float32)
    g["konst"] = np.array([[EPS, -0.5, 1.0 + EPS, EPS * EPS]], np.float32)
    sel = np.zeros((33, 33 * 128), np.float32)
    for r in range(33):
        sel[r, r * 128:(r + 1) * 128] = 1.0
    g["selmat"] = sel
    g["ones_col"] = np.ones((128, 1), np.float32)
    g["eye"] = np.eye(128, dtype=np.float32)
    return g


def build_nc(dbg=False):
    nc = bass.Bass("TRN2", target_bir_lowering=False)
    es = ExitStack()
    b = Bld(nc, es)

    dr = {}
    for nm, sh in [("x", (BL, DM, L)), ("w_in", (QM * 4, QM, 128, 128)),
                   ("w_out", (QM, QI, 128, 128)), ("w1", (QH, QM, 128, 128)),
                   ("w2", (QM, QH, 128, 128)), ("w_xp", (128, QI, 33)),
                   ("conv_w", (128, QI * 4)), ("conv_b", (128, QI)),
                   ("dt_w", (128, QI)), ("dt_b", (128, QI)),
                   ("a_log", (128, QI * NST)), ("dvec", (128, QI)),
                   ("b1", (128, QH)), ("b2", (128, QM)),
                   ("gamma", (128, QM)), ("beta", (128, QM)),
                   ("ones_row", (1, 128)), ("ones_col", (128, 1)),
                   ("eye", (128, 128)), ("konst", (1, 4)),
                   ("selmat", (33, 33 * 128))]:
        dr[nm] = nc.dram_tensor(nm, list(sh), F32, kind="ExternalInput").ap()
    x_d = dr["x"]
    out_d = nc.dram_tensor("out", [BL, DM, L], F32, kind="ExternalOutput").ap()

    wbf_in_d = nc.dram_tensor("wbf_in", [QM * 4, 128, QM * 128], BF16, kind="Internal").ap()
    wbf_out_d = nc.dram_tensor("wbf_out", [QM, 128, QI * 128], BF16, kind="Internal").ap()
    wq1_d = nc.dram_tensor("wq1", [QH, 128, QM * 128], BF16, kind="Internal").ap()
    wq2_d = nc.dram_tensor("wq2", [QM, 128, QH * 128], BF16, kind="Internal").ap()
    x1_d = nc.dram_tensor("x1h", [BL, DM, L], BF16, kind="ExternalOutput").ap()
    xc_d = nc.dram_tensor("xch", [BL, DI, L], BF16, kind="ExternalOutput").ap()
    sz_d = nc.dram_tensor("szh", [BL, DI, L], BF16, kind="ExternalOutput").ap()

    def tap(name, sb_ap, shape, reads, dt=F32):
        if not dbg:
            return
        d = nc.dram_tensor("dbg_" + name, list(shape), dt, kind="ExternalOutput").ap()
        b.dma(lambda y: y.dma_start(out=d, in_=sb_ap), reads=list(reads), writes=[Buf()])

    def sb(name, shape, dt=F32):
        t = es.enter_context(nc.sbuf_tensor(name, list(shape), dt))
        return t, Buf()

    cvw, cvw_b = sb("cvw", (128, QI * 4))
    cvb, cvb_b = sb("cvb", (128, QI))
    dtw, dtw_b = sb("dtw", (128, QI))
    dtb, dtb_b = sb("dtb", (128, QI))
    alog, alog_b = sb("alog", (128, QI * NST))
    Ares, Ares_b = sb("Ares", (128, QI * NST))
    dvv, dvv_b = sb("dvv", (128, QI))
    b1v, b1v_b = sb("b1v", (128, QH))
    b2v, b2v_b = sb("b2v", (128, QM))
    gam, gam_b = sb("gam", (128, QM))
    bet, bet_b = sb("bet", (128, QM))
    nbet, nbet_b = sb("nbet", (128, QM))
    orow, orow_b = sb("orow", (1, 128))
    konst, konst_b = sb("konst_sb", (1, 4))
    selbf, selbf_b = sb("selbf", (33, 33 * 128), BF16)
    ocol, ocol_b = sb("ocol", (128, 1))
    eyef, eyef_b = sb("eyef", (128, 128))
    orow_bf, orow_bf_b = sb("orow_bf", (1, 128), BF16)
    ocol_bf, ocol_bf_b = sb("ocol_bf", (128, 1), BF16)
    eye_bf, eye_bf_b = sb("eye_bf", (128, 128), BF16)
    xpw_bf, xpw_bf_b = sb("xpw_bf", (128, QI, 33), BF16)
    ublend, ublend_b = sb("ublend", (128, 1))
    ub_scr, ub_scr_b = sb("ub_scr", (128, 1))
    sw1_vec, sw1_vec_b = sb("sw1_vec", (128, 1))
    sw1i_vec, sw1i_vec_b = sb("sw1i_vec", (128, 1))
    sw2_vec, sw2_vec_b = sb("sw2_vec", (128, 1))
    sw2i_vec, sw2i_vec_b = sb("sw2i_vec", (128, 1))
    states, states_b = sb("states", (128, BL * QI * NST))
    tails, tails_b = sb("tails", (128, BL * QI * 3), BF16)

    p0 = ExitStack()

    def sb0(name, shape, dt=F32):
        t = p0.enter_context(nc.sbuf_tensor(name, list(shape), dt))
        return t, Buf()

    for nm, (tt, bb) in [("conv_w", (cvw, cvw_b)), ("conv_b", (cvb, cvb_b)),
                         ("dt_w", (dtw, dtw_b)), ("dt_b", (dtb, dtb_b)),
                         ("a_log", (alog, alog_b)), ("dvec", (dvv, dvv_b)),
                         ("b1", (b1v, b1v_b)), ("b2", (b2v, b2v_b)),
                         ("gamma", (gam, gam_b)), ("beta", (bet, bet_b)),
                         ("ones_row", (orow, orow_b)), ("ones_col", (ocol, ocol_b)),
                         ("eye", (eyef, eyef_b)), ("konst", (konst, konst_b))]:
        b.dma(lambda y, d=dr[nm], t=tt: y.dma_start(out=t[:], in_=d), writes=[bb])

    b.op("v", lambda v: v.tensor_copy(orow_bf[:], orow[:]), reads=[orow_b], writes=[orow_bf_b])
    b.op("v", lambda v: v.tensor_copy(ocol_bf[:], ocol[:]), reads=[ocol_b], writes=[ocol_bf_b])
    b.op("v", lambda v: v.tensor_copy(eye_bf[:], eyef[:]), reads=[eyef_b], writes=[eye_bf_b])
    self_f, self_f_b = sb0("self_f", (33, 33 * 128))
    b.dma(lambda y: y.dma_start(out=self_f[:], in_=dr["selmat"]), writes=[self_f_b])
    b.op("v", lambda v: v.tensor_copy(selbf[:], self_f[:]), reads=[self_f_b], writes=[selbf_b])

    ascr, ascr_b = sb0("ascr", (128, QI * NST))
    b.op("s", lambda s: s.activation(ascr[:], alog[:], AF.Exp), reads=[alog_b], writes=[ascr_b])
    b.op("v", lambda v: v.tensor_scalar(Ares[:], ascr[:], -1.0, None, OP.mult),
         reads=[ascr_b], writes=[Ares_b])

    b.op("v", lambda v: v.tensor_scalar(nbet[:], bet[:], -1.0, None, OP.mult),
         reads=[bet_b], writes=[nbet_b])
    b.op("v", lambda v: v.tensor_scalar(ub_scr[:], ocol[:], M64, None, OP.add),
         reads=[ocol_b], writes=[ub_scr_b])
    b.op("v", lambda v: v.tensor_scalar(ublend[:], ub_scr[:], -M64, None, OP.add),
         reads=[ub_scr_b], writes=[ublend_b])

    xpw_f, xpw_f_b = sb0("xpw_f", (128, QI * 33))
    b.dma(lambda y: y.dma_start(out=xpw_f[:], in_=dr["w_xp"].rearrange("p c m -> p (c m)")),
          writes=[xpw_f_b])
    b.op("s", lambda s: s.copy(xpw_bf[:].rearrange("p c m -> p (c m)"), xpw_f[:]),
         reads=[xpw_f_b], writes=[xpw_bf_b])

    def cast_stream(src_d, dst_d, nj, nk, tag):
        fsl = [sb0(f"{tag}_f{i}", (128, nk * 128)) for i in range(3)]
        bsl = [sb0(f"{tag}_b{i}", (128, nk * 128), BF16) for i in range(3)]
        for j in range(nj):
            (ft, fb), (bt, bb2) = fsl[j % 3], bsl[j % 3]
            b.dma(lambda y, j=j, ft=ft: y.dma_start(
                out=ft[:].rearrange("p (k m) -> p k m", m=128),
                in_=src_d[j].rearrange("k p m -> p k m")), writes=[fb])
            b.op("s", lambda s, ft=ft, bt=bt: s.copy(bt[:], ft[:]), reads=[fb], writes=[bb2])
            b.dma(lambda y, j=j, bt=bt: y.dma_start(out=dst_d[j], in_=bt[:]),
                  reads=[bb2])

    cast_stream(dr["w_in"], wbf_in_d, QM * 4, QM, "ci")
    cast_stream(dr["w_out"], wbf_out_d, QM, QI, "co")

    def quant_weights(src_d, dst_d, nj, nk, sw_pair, swi_pair, tag):
        sw_vec, sw_vec_b = sw_pair
        swi_vec, swi_vec_b = swi_pair
        n_el = float(nj * nk * 128 * 128)
        npg = 3 if nk <= 8 else 2
        fsl = [sb0(f"{tag}qf{i}", (128, nk * 128)) for i in range(npg)]
        red, red_b = sb0(f"{tag}red", (128, nj))
        for j in range(nj):
            ft, fb = fsl[j % npg]
            b.dma(lambda y, j=j, ft=ft: y.dma_start(
                out=ft[:].rearrange("p (k m) -> p k m", m=128),
                in_=src_d[j].rearrange("k p m -> p k m")), writes=[fb])
            b.op("v", lambda v, ft=ft, j=j: v.tensor_reduce(
                red[:, j:j + 1], ft[:], AX.X, OP.add, apply_absolute_value=True),
                reads=[fb], writes=[red_b])
        redt, redt_b = sb0(f"{tag}redt", (128, 1))
        red_bf, red_bf_b = sb0(f"{tag}redbf", (128, 1), BF16)
        b.op("v", lambda v: v.tensor_reduce(redt[:], red[:], AX.X, OP.add),
             reads=[red_b], writes=[redt_b])
        b.op("v", lambda v: v.tensor_copy(red_bf[:], redt[:]), reads=[redt_b], writes=[red_bf_b])
        pml = p0.enter_context(nc.psum_tensor(f"{tag}pm", [128, 512], F32))
        pml_b = Buf()
        GAcc(b, pml[0:1, 0:1], pml_b, 1).add(ocol_bf[:], red_bf[:], [red_bf_b, ocol_bf_b])
        sc1, sc1_b = sb0(f"{tag}sc1", (1, 2))
        b.op("v", lambda v: v.tensor_scalar(sc1[:, 0:1], pml[0:1, 0:1], 1.0 / n_el, 1e-5,
                                            OP.mult, OP.max), reads=[pml_b], writes=[sc1_b])
        b.op("v", lambda v: v.reciprocal(sc1[:, 1:2], sc1[:, 0:1]), reads=[sc1_b], writes=[sc1_b])
        sc_bf, sc_bf_b = sb0(f"{tag}scbf", (1, 2), BF16)
        b.op("v", lambda v: v.tensor_copy(sc_bf[:], sc1[:]), reads=[sc1_b], writes=[sc_bf_b])
        for col, (vec, vec_b) in [(0, (swi_vec, swi_vec_b)), (1, (sw_vec, sw_vec_b))]:
            GAcc(b, pml[:, 4 + col:5 + col], pml_b, 1).add(
                orow_bf[:], sc_bf[:, col:col + 1], [sc_bf_b, orow_bf_b])
            b.op("v", lambda v, col=col, vec=vec: v.tensor_copy(
                vec[:], pml[:, 4 + col:5 + col]), reads=[pml_b], writes=[vec_b])
        zs, zs_b = sb0(f"{tag}zs", (128, nk * 128))
        q1, q1_b = sb0(f"{tag}q1", (128, nk * 128), BF16)
        q2, q2_b = sb0(f"{tag}q2", (128, nk * 128), BF16)
        qd, qd_b = sb0(f"{tag}qd", (128, nk * 128), BF16)
        qb, qb_b = sb0(f"{tag}qb", (128, nk * 128), BF16)
        qc, qc_b = sb0(f"{tag}qc", (128, nk * 128), BF16)
        for j in range(nj):
            ft, fb = fsl[j % 2]
            b.dma(lambda y, j=j, ft=ft: y.dma_start(
                out=ft[:].rearrange("p (k m) -> p k m", m=128),
                in_=src_d[j].rearrange("k p m -> p k m")), writes=[fb])
            b.op("v", lambda v, ft=ft: v.tensor_scalar(zs[:], ft[:], sw_vec[:], None, OP.mult),
                 reads=[fb, sw_vec_b], writes=[zs_b])
            b.op("v", lambda v: v.tensor_scalar(q1[:], zs[:], M32, -M32, OP.add, OP.add),
                 reads=[zs_b], writes=[q1_b])
            b.op("v", lambda v: v.tensor_scalar(q2[:], zs[:], M64, -M64, OP.add, OP.add),
                 reads=[zs_b], writes=[q2_b])
            b.op("v", lambda v: v.tensor_tensor(qd[:], q2[:], q1[:], OP.subtract),
                 reads=[q1_b, q2_b], writes=[qd_b])
            b.op("v", lambda v: v.scalar_tensor_tensor(qb[:], qd[:], ublend[:], q1[:],
                                                       OP.mult, OP.add),
                 reads=[qd_b, q1_b, ublend_b], writes=[qb_b])
            b.op("v", lambda v: v.tensor_scalar(qc[:], qb[:], -1.0, 1.0, OP.max, OP.min),
                 reads=[qb_b], writes=[qc_b])
            b.dma(lambda y, j=j: y.dma_start(out=dst_d[j], in_=qc[:]), reads=[qc_b])

    quant_weights(dr["w1"], wq1_d, QH, QM, (sw1_vec, sw1_vec_b), (sw1i_vec, sw1i_vec_b), "a")
    quant_weights(dr["w2"], wq2_d, QM, QH, (sw2_vec, sw2_vec_b), (sw2i_vec, sw2i_vec_b), "b")

    b.op("v", lambda v: v.memset(states[:], 0.0), writes=[states_b])
    b.op("v", lambda v: v.memset(tails[:], 0.0), writes=[tails_b])

    b.barrier()
    p0.close()

    def make_ln(T, pstat1, pstat2, row_sets):
        state = {"i": 0}

        def ln_begin(nq):
            rows, rows_b, lbf, lbf_b, sqp = row_sets[state["i"] % len(row_sets)]
            ap1, buf1 = pstat1
            ap2, buf2 = pstat2
            state["g1"] = GAcc(b, ap1, buf1, nq)
            state["g2"] = GAcc(b, ap2, buf2, nq)
            state["qn"] = 0

        def ln_feed(ap, rds):
            rows, rows_b, lbf, lbf_b, sqp = row_sets[state["i"] % len(row_sets)]
            state["g1"].add(ocol_bf[:], ap, list(rds) + [ocol_bf_b])
            sq = sqp[state["qn"] % 2]
            state["qn"] += 1
            b.op("s", lambda s, ap=ap, sq=sq: s.activation(sq[0][:], ap, AF.Square),
                 reads=rds, writes=[sq[1]])
            state["g2"].add(ocol_bf[:], sq[0][:], [sq[1], ocol_bf_b])

        def ln_finish(nq, fuse_double):
            rows, rows_b, lbf, lbf_b, sqp = row_sets[state["i"] % len(row_sets)]
            R = lambda i: rows[:, i * T:(i + 1) * T]
            H = lambda i: lbf[:, i * T:(i + 1) * T]
            ap1, buf1 = pstat1
            ap2, buf2 = pstat2
            RW = [rows_b]
            b.op("v", lambda v: v.tensor_copy(R(0), ap1), reads=[buf1], writes=RW)
            b.op("v", lambda v: v.tensor_copy(R(1), ap2), reads=[buf2], writes=RW)
            nfeat = float(nq * 128)
            b.op("v", lambda v: v.tensor_scalar(R(2), R(0), 1.0 / nfeat, None, OP.mult),
                 reads=RW, writes=RW)
            b.op("v", lambda v: v.tensor_tensor(R(4), R(2), R(2), OP.mult),
                 reads=RW, writes=RW)
            b.op("v", lambda v: v.scalar_tensor_tensor(R(5), R(1), 1.0 / nfeat, R(4),
                                                       OP.mult, OP.subtract),
                 reads=RW, writes=RW)
            sc1 = (1.0 + EPS) if fuse_double else 1.0
            sc2 = (EPS * EPS) if fuse_double else EPS
            b.op("v", lambda v: v.tensor_scalar(R(1), R(5), sc1, sc2, OP.mult, OP.add),
                 reads=RW, writes=RW)
            b.op("s", lambda s: s.activation(R(3), R(1), AF.Ln), reads=RW, writes=RW)
            b.op("s", lambda s: s.activation(R(0), R(3), AF.Exp, scale=konst[0:1, 1:2]),
                 reads=RW + [konst_b], writes=RW)
            b.op("s", lambda s: s.copy(H(0), R(0)), reads=RW, writes=[lbf_b])
            b.op("v", lambda v: v.tensor_tensor(H(1), R(0), H(0), OP.subtract),
                 reads=RW + [lbf_b], writes=[lbf_b])
            b.op("v", lambda v: v.tensor_tensor(R(3), R(2), R(0), OP.mult),
                 reads=RW, writes=RW)
            b.op("s", lambda s: s.copy(H(2), R(3)), reads=RW, writes=[lbf_b])
            b.op("v", lambda v: v.tensor_tensor(H(3), R(3), H(2), OP.subtract),
                 reads=RW + [lbf_b], writes=[lbf_b])
            state["cur"] = (lbf, lbf_b)

        def ln_stats(src, nq, fuse_double):
            ln_begin(nq)
            for q in range(nq):
                ap, rds = src(q)
                ln_feed(ap, rds)
            ln_finish(nq, fuse_double)

        def bcast(psum_ap, psum_buf, off):
            lbf, lbf_b = state["cur"]
            H = lambda i: lbf[:, i * T:(i + 1) * T]
            g = GAcc(b, psum_ap, psum_buf, 2)
            g.add(orow_bf[:], H(off), [lbf_b, orow_bf_b])
            g.add(orow_bf[:], H(off + 1), [lbf_b, orow_bf_b])

        def next_site():
            state["i"] += 1

        return ln_stats, bcast, next_site, ln_begin, ln_feed, ln_finish

    pA1 = ExitStack()

    def sb1(name, shape, dt=F32):
        t = pA1.enter_context(nc.sbuf_tensor(name, list(shape), dt))
        return t, Buf()

    def ps1(name, shape, dt=F32):
        t = pA1.enter_context(nc.psum_tensor(name, list(shape), dt))
        return t, Buf()

    x0s = [sb1(f"x0_{i}", (128, QM, TA)) for i in range(2)]
    u1s = [sb1(f"u1_{i}", (128, QM, TA), BF16) for i in range(2)]
    xssm = [sb1(f"xssm{i}", (128, TA + 3), BF16) for i in range(2)]
    cvs = [sb1(f"cvs{i}", (128, TA)) for i in range(4)]
    xcts = [sb1(f"xct{i}", (128, QI, TA), BF16) for i in range(2)]
    szts = [sb1(f"szt{i}", (128, QI, TA), BF16) for i in range(2)]
    sqs1 = [sb1(f"sqs1_{i}", (128, TA), BF16) for i in range(2)]
    cbs1 = [sb1(f"cbs1_{i}", (128, TA), BF16) for i in range(2)]
    rows1 = [sb1(f"rows1_{i}", (1, 6 * TA)) for i in range(2)]
    lbf1 = [sb1(f"lbf1_{i}", (1, 4 * TA), BF16) for i in range(2)]
    t1q = [sb1(f"t1q{i}", (128, TA)) for i in range(2)]
    winb = [sb1(f"winb{i}", (128, QM * 128), BF16) for i in range(3)]
    pin = [ps1(f"pin{i}", (128, TA)) for i in range(3)]
    pst1, pst1_b = ps1("pst1", (1, 2 * TA))
    pln1, pln1_b = ps1("pln1", (128, 2 * TA))

    ln_1, bcast_1, next_1, _, _, _ = make_ln(
        TA, (pst1[0:1, 0:TA], pst1_b), (pst1[0:1, TA:2 * TA], pst1_b),
        [(rows1[0][0], rows1[0][1], lbf1[0][0], lbf1[0][1], sqs1),
         (rows1[1][0], rows1[1][1], lbf1[1][0], lbf1[1][1], sqs1)])

    tails_bb = [Buf(), Buf()]
    unitsA = [(bb, t) for t in range(NTA) for bb in range(BL)]
    for ui, (bi, ti) in enumerate(unitsA):
        par = ui % 2
        t0 = ti * TA
        x0, x0_b = x0s[par]
        u1t, u1_b = u1s[par]
        xct, xct_b = xcts[par]
        szt, szt_b = szts[par]
        tl_b = tails_bb[bi]
        b.dma(lambda y, bi=bi, t0=t0, x0=x0: y.dma_start(
            out=x0[:], in_=x_d[bi, :, t0:t0 + TA].rearrange("(q p) t -> p q t", p=128)),
            writes=[x0_b])

        def src_x0(q, x0=x0, x0_b=x0_b):
            cb = cbs1[q % 2]
            b.op("s", lambda s, q=q, cb=cb, x0=x0: s.copy(cb[0][:], x0[:, q, :]),
                 reads=[x0_b], writes=[cb[1]])
            return cb[0][:], [cb[1]]

        ln_1(src_x0, QM, False)
        bcast_1(pln1[:, 0:TA], pln1_b, 0)
        bcast_1(pln1[:, TA:2 * TA], pln1_b, 2)
        next_1()
        for q in range(QM):
            tq = t1q[q % 2]
            b.op("v", lambda v, q=q, tq=tq, x0=x0: v.tensor_tensor(
                tq[0][:], x0[:, q, :], pln1[:, 0:TA], OP.mult),
                reads=[x0_b, pln1_b], writes=[tq[1]])
            b.op("v", lambda v, q=q, tq=tq, u1t=u1t: v.tensor_tensor(
                u1t[:, q, :], tq[0][:], pln1[:, TA:2 * TA], OP.subtract),
                reads=[tq[1], pln1_b], writes=[u1_b])

        for j in range(QM * 4):
            wb = winb[j % 3]
            b.dma(lambda y, j=j, wb=wb: y.dma_start(
                out=wb[0][:], in_=wbf_in_d[j]), writes=[wb[1]])
            pp = pin[j % 3]
            g = GAcc(b, pp[0][:], pp[1], QM)
            for k in range(QM):
                g.add(wb[0][:, k * 128:(k + 1) * 128], u1t[:, k, :], [wb[1], u1_b])
            if j < QI:
                c = j
                xs = xssm[c % 2]
                tix = bi * QI * 3 + c * 3
                b.op("v", lambda v, xs=xs, pp=pp: v.tensor_copy(
                    xs[0][:, 3:3 + TA], pp[0][:]), reads=[pp[1]], writes=[xs[1]])
                b.op("v", lambda v, xs=xs, tix=tix: v.tensor_copy(
                    xs[0][:, 0:3], tails[:, tix:tix + 3]), reads=[tl_b], writes=[xs[1]])
                b.op("v", lambda v, xs=xs, tix=tix: v.tensor_copy(
                    tails[:, tix:tix + 3], xs[0][:, TA:TA + 3]),
                    reads=[xs[1]], writes=[tl_b])
                ca, c2 = cvs[c % 2], cvs[2 + c % 2]
                b.op("v", lambda v, xs=xs, ca=ca, c=c: v.tensor_scalar(
                    ca[0][:], xs[0][:, 0:TA], cvw[:, 4 * c:4 * c + 1], None, OP.mult),
                    reads=[xs[1], cvw_b], writes=[ca[1]])
                b.op("v", lambda v, xs=xs, c2=c2, c=c: v.tensor_scalar(
                    c2[0][:], xs[0][:, 2:2 + TA], cvw[:, 4 * c + 2:4 * c + 3], None, OP.mult),
                    reads=[xs[1], cvw_b], writes=[c2[1]])
                b.op("v", lambda v, xs=xs, ca=ca, c=c: v.scalar_tensor_tensor(
                    ca[0][:], xs[0][:, 1:1 + TA], cvw[:, 4 * c + 1:4 * c + 2], ca[0][:],
                    OP.mult, OP.add), reads=[xs[1], ca[1], cvw_b], writes=[ca[1]])
                b.op("v", lambda v, xs=xs, c2=c2, c=c: v.scalar_tensor_tensor(
                    c2[0][:], xs[0][:, 3:3 + TA], cvw[:, 4 * c + 3:4 * c + 4], c2[0][:],
                    OP.mult, OP.add), reads=[xs[1], c2[1], cvw_b], writes=[c2[1]])
                b.op("v", lambda v, ca=ca, c2=c2: v.tensor_tensor(
                    ca[0][:], ca[0][:], c2[0][:], OP.add),
                    reads=[ca[1], c2[1]], writes=[ca[1]])
                b.op("s", lambda s, ca=ca, c=c, xct=xct: s.activation(
                    xct[:, c, :], ca[0][:], AF.Silu, bias=cvb[:, c:c + 1]),
                    reads=[ca[1], cvb_b], writes=[xct_b])
            else:
                c = j - QI
                b.op("s", lambda s, c=c, pp=pp, szt=szt: s.activation(
                    szt[:, c, :], pp[0][:], AF.Silu), reads=[pp[1]], writes=[szt_b])
        b.dma(lambda y, bi=bi, t0=t0, xct=xct: y.dma_start(
            out=xc_d[bi, :, t0:t0 + TA].rearrange("(c p) t -> p c t", p=128),
            in_=xct[:]), reads=[xct_b])
        b.dma(lambda y, bi=bi, t0=t0, szt=szt: y.dma_start(
            out=sz_d[bi, :, t0:t0 + TA].rearrange("(c p) t -> p c t", p=128),
            in_=szt[:]), reads=[szt_b])

    b.barrier()
    pA1.close()

    pA2 = ExitStack()

    def sb2(name, shape, dt=F32):
        t = pA2.enter_context(nc.sbuf_tensor(name, list(shape), dt))
        return t, Buf()

    def ps2(name, shape, dt=F32):
        t = pA2.enter_context(nc.psum_tensor(name, list(shape), dt))
        return t, Buf()

    x0s2 = [sb2("x0b_0", (128, QM, TA))]
    xcs2 = [sb2(f"xc2_{i}", (128, QI, TA), BF16) for i in range(2)]
    szs2 = [sb2("sz2_0", (128, QI, TA), BF16)]
    xpbf, xpbf_b = sb2("xpbf", (33, TA), BF16)
    rsb, rsb_b = sb2("rsb", (128, TA), BF16)
    dts = [sb2(f"dts{i}", (128, TA), BF16) for i in range(2)]
    scrp = [sb2(f"scrp{i}", (128, TA)) for i in range(4)]
    spE = [sb2(f"spE{i}", (128, TA), BF16) for i in range(2)]
    spR = [sb2(f"spR{i}", (128, TA), BF16) for i in range(2)]
    spR2 = [sb2(f"spR2{i}", (128, TA), BF16) for i in range(2)]
    usc = [sb2(f"usc{i}", (128, TA), BF16) for i in range(2)]
    Bt, Bt_b = sb2("Bt", (128, NST, TA), BF16)
    Ct, Ct_b = sb2("Ct", (128, NST, TA), BF16)
    dAs = [sb2(f"dAs{i}", (128, 2, TA)) for i in range(2)]
    dbx = [sb2(f"dbx{i}", (128, 2, TA), BF16) for i in range(3)]
    hsl = [sb2(f"hsl{i}", (128, 2, TA), BF16) for i in range(3)]
    hcs = [sb2(f"hcs{i}", (128, 2, TA), BF16) for i in range(2)]
    ysc = [sb2(f"ysc{i}", (128, TA)) for i in range(2)]
    ygt, ygt_b = sb2("ygt", (128, QI, TA), BF16)
    mixb, mixb_b = sb2("mixb", (128, QM, TA), BF16)
    sqs2 = [sb2(f"sqs2_{i}", (128, TA), BF16) for i in range(2)]
    rows2 = [sb2("rows2_0", (1, 6 * TA))]
    lbf2 = [sb2("lbf2_0", (1, 4 * TA), BF16)]
    t1q2 = [scrp[0], scrp[1]]
    t2q2 = [scrp[2], scrp[3]]
    x1o = [spR[0], spR[1]]
    woutb = [sb2("woutb0", (128, QI * 128), BF16)]
    pout = [ps2(f"pout{i}", (128, TA)) for i in range(2)]
    pY, pY_b = ps2("pY", (128, TA))
    pst2, pst2_b = ps2("pst2", (1, TA))
    pln2, pln2_b = ps2("pln2", (128, 2 * TA))
    pmisc, pmisc_b = ps2("pmisc", (128, TA))
    pmisc2, pmisc2_b = ps2("pmisc2", (128, TA))

    ln_2, bcast_2, next_2, ln2_begin, ln2_feed, ln2_finish = make_ln(
        TA, (pst2[0:1, 0:TA], pst2_b), (pmisc2[0:1, 0:TA], pmisc2_b),
        [(rows2[0][0], rows2[0][1], lbf2[0][0], lbf2[0][1], sqs2)])

    states_bb = [Buf(), Buf()]
    for ui, (bi, ti) in enumerate(unitsA):
        par = ui % 2
        t0 = ti * TA
        x0, x0_b = x0s2[0]
        xct, xct_b = xcs2[par]
        sluz, sluz_b = szs2[0]
        st_b = states_bb[bi]
        b.dma(lambda y, bi=bi, t0=t0, x0=x0: y.dma_start(
            out=x0[:], in_=x_d[bi, :, t0:t0 + TA].rearrange("(q p) t -> p q t", p=128)),
            writes=[x0_b])
        b.dma(lambda y, bi=bi, t0=t0, xct=xct: y.dma_start(
            out=xct[:], in_=xc_d[bi, :, t0:t0 + TA].rearrange("(c p) t -> p c t", p=128)),
            writes=[xct_b])
        b.dma(lambda y, bi=bi, t0=t0, sluz=sluz: y.dma_start(
            out=sluz[:], in_=sz_d[bi, :, t0:t0 + TA].rearrange("(c p) t -> p c t", p=128)),
            writes=[sluz_b])
        g = GAcc(b, pmisc[0:33, :], pmisc_b, QI)
        for c in range(QI):
            g.add(xpw_bf[:, c, :], xct[:, c, :], [xpw_bf_b, xct_b])
        b.op("v", lambda v: v.tensor_copy(xpbf[:], pmisc[0:33, :]),
             reads=[pmisc_b], writes=[xpbf_b])
        GAcc(b, pmisc[:, :], pmisc_b, 1).add(selbf[:, 0:128], xpbf[:], [xpbf_b, selbf_b])
        b.op("s", lambda s: s.copy(rsb[:], pmisc[:, :]), reads=[pmisc_b], writes=[rsb_b])
        bc_banks = [(pmisc, pmisc_b), (pmisc2, pmisc2_b)]
        for n in range(NST):
            bk, bk_b = bc_banks[0]
            GAcc(b, bk[:, :], bk_b, 1).add(
                selbf[:, (1 + n) * 128:(2 + n) * 128], xpbf[:], [xpbf_b, selbf_b])
            b.op("s", lambda s, n=n, bk=bk: s.copy(Bt[:, n, :], bk[:, :]),
                 reads=[bk_b], writes=[Bt_b])
            bk, bk_b = bc_banks[1]
            GAcc(b, bk[:, :], bk_b, 1).add(
                selbf[:, (17 + n) * 128:(18 + n) * 128], xpbf[:], [xpbf_b, selbf_b])
            b.op("s", lambda s, n=n, bk=bk: s.copy(Ct[:, n, :], bk[:, :]),
                 reads=[bk_b], writes=[Ct_b])

        def softplus_pre(c):
            pre, ab = scrp[c % 2], scrp[2 + c % 2]
            er, rl, ln_ = spE[c % 2], spR[c % 2], spR2[c % 2]
            b.op("v", lambda v, c=c, pre=pre: v.tensor_scalar(
                pre[0][:], rsb[:], dtw[:, c:c + 1], dtb[:, c:c + 1], OP.mult, OP.add),
                reads=[rsb_b, dtw_b, dtb_b], writes=[pre[1]])
            b.op("s", lambda s, pre=pre, rl=rl: s.activation(rl[0][:], pre[0][:], AF.Relu),
                 reads=[pre[1]], writes=[rl[1]])
            b.op("s", lambda s, pre=pre, ab=ab: s.activation(ab[0][:], pre[0][:], AF.Abs),
                 reads=[pre[1]], writes=[ab[1]])
            b.op("s", lambda s, ab=ab, er=er: s.activation(
                er[0][:], ab[0][:], AF.Exp, scale=-1.0), reads=[ab[1]], writes=[er[1]])
            b.op("s", lambda s, er=er, ln_=ln_: s.activation(
                ln_[0][:], er[0][:], AF.Ln, bias=1.0), reads=[er[1]], writes=[ln_[1]])

        def softplus_tail(c, xct=xct, xct_b=xct_b):
            dtc = dts[c % 2]
            rl, ln_ = spR[c % 2], spR2[c % 2]
            b.op("v", lambda v, dtc=dtc, rl=rl, ln_=ln_: v.tensor_tensor(
                dtc[0][:], rl[0][:], ln_[0][:], OP.add),
                reads=[rl[1], ln_[1]], writes=[dtc[1]])
            uc = usc[c % 2]
            b.op("v", lambda v, c=c, dtc=dtc, uc=uc: v.tensor_tensor(
                uc[0][:], dtc[0][:], xct[:, c, :], OP.mult),
                reads=[dtc[1], xct_b], writes=[uc[1]])

        softplus_pre(0)
        softplus_tail(0)
        NGG = NST // 2
        for c in range(QI):
            dtc = dts[c % 2]
            uc = usc[c % 2]
            gy = GAcc(b, pY[:], pY_b, NST)
            for k in range(NGG + 2):
                if k == 1 and c + 1 < QI:
                    softplus_pre(c + 1)
                if k == 6 and c + 1 < QI:
                    softplus_tail(c + 1)
                if k < NGG:
                    gg = k
                    da, dx = dAs[gg % 2], dbx[gg % 3]
                    for i in range(2):
                        n = 2 * gg + i
                        b.op("s", lambda s, c=c, n=n, i=i, da=da, dtc=dtc: s.activation(
                            da[0][:, i, :], dtc[0][:], AF.Exp,
                            scale=Ares[:, c * NST + n:c * NST + n + 1]),
                            reads=[dtc[1], Ares_b], writes=[da[1]])
                    b.op("v", lambda v, gg=gg, dx=dx, uc=uc: v.tensor_tensor(
                        dx[0][:], uc[0][:].unsqueeze(1).broadcast_to([128, 2, TA]),
                        Bt[:, 2 * gg:2 * gg + 2, :], OP.mult),
                        reads=[uc[1], Bt_b], writes=[dx[1]])
                if 1 <= k <= NGG:
                    gg = k - 1
                    da, dx, hh = dAs[gg % 2], dbx[gg % 3], hsl[gg % 3]
                    for i in range(2):
                        n = 2 * gg + i
                        sidx = bi * QI * NST + c * NST + n
                        if ti == 0:
                            init = 0.0
                            rds = [da[1], dx[1]]
                        else:
                            init = states[:, sidx:sidx + 1]
                            rds = [da[1], dx[1], st_b]
                        b.op("v", lambda v, i=i, da=da, dx=dx, hh=hh, init=init:
                             v.tensor_tensor_scan(hh[0][:, i, :], da[0][:, i, :],
                                                  dx[0][:, i, :], init, OP.mult, OP.add),
                             reads=rds, writes=[hh[1]])
                if k >= 2:
                    gg = k - 2
                    hh, hc = hsl[gg % 3], hcs[gg % 2]
                    b.op("v", lambda v, gg=gg, hh=hh, hc=hc: v.tensor_tensor(
                        hc[0][:], hh[0][:], Ct[:, 2 * gg:2 * gg + 2, :], OP.mult),
                        reads=[hh[1], Ct_b], writes=[hc[1]])
                    s0 = bi * QI * NST + c * NST + 2 * gg
                    b.op("v", lambda v, hh=hh, s0=s0: v.tensor_copy(
                        states[:, s0:s0 + 2],
                        hh[0][:, :, TA - 1:TA].rearrange("p n o -> p (n o)")),
                        reads=[hh[1]], writes=[st_b])
                    for i in range(2):
                        gy.add(eye_bf[:], hc[0][:, i, :], [hc[1], eye_bf_b])
            ys = ysc[c % 2]
            b.op("v", lambda v, c=c, ys=ys, xct=xct: v.scalar_tensor_tensor(
                ys[0][:], xct[:, c, :], dvv[:, c:c + 1], pY[:], OP.mult, OP.add),
                reads=[xct_b, dvv_b, pY_b], writes=[ys[1]])
            b.op("v", lambda v, c=c, ys=ys, sluz=sluz: v.tensor_tensor(
                ygt[:, c, :], ys[0][:], sluz[:, c, :], OP.mult),
                reads=[ys[1], sluz_b], writes=[ygt_b])

        ln2_begin(QM)
        for j in range(QM):
            wb = woutb[0]
            b.dma(lambda y, j=j, wb=wb: y.dma_start(
                out=wb[0][:], in_=wbf_out_d[j]), writes=[wb[1]])
            pp = pout[j % 2]
            g = GAcc(b, pp[0][:], pp[1], QI)
            for c in range(QI):
                g.add(wb[0][:, c * 128:(c + 1) * 128], ygt[:, c, :], [wb[1], ygt_b])
            b.op("s", lambda s, j=j, pp=pp: s.copy(mixb[:, j, :], pp[0][:]),
                 reads=[pp[1]], writes=[mixb_b])
            ln2_feed(mixb[:, j, :], [mixb_b])
        ln2_finish(QM, False)
        bcast_2(pln2[:, 0:TA], pln2_b, 0)
        bcast_2(pln2[:, TA:2 * TA], pln2_b, 2)
        next_2()
        for q in range(QM):
            tq, tq2, xo = t1q2[q % 2], t2q2[q % 2], x1o[q % 2]
            b.op("v", lambda v, q=q, tq=tq: v.tensor_tensor(
                tq[0][:], mixb[:, q, :], pln2[:, 0:TA], OP.mult),
                reads=[mixb_b, pln2_b], writes=[tq[1]])
            b.op("v", lambda v, q=q, tq=tq, tq2=tq2, x0=x0: v.tensor_tensor(
                tq2[0][:], tq[0][:], x0[:, q, :], OP.add),
                reads=[tq[1], x0_b], writes=[tq2[1]])
            b.op("v", lambda v, q=q, tq2=tq2, xo=xo: v.tensor_tensor(
                xo[0][:], tq2[0][:], pln2[:, TA:2 * TA], OP.subtract),
                reads=[tq2[1], pln2_b], writes=[xo[1]])
            b.dma(lambda y, bi=bi, t0=t0, q=q, xo=xo: y.dma_start(
                out=x1_d[bi, 128 * q:128 * (q + 1), t0:t0 + TA], in_=xo[0][:]),
                reads=[xo[1]])

    b.barrier()
    pA2.close()

    pB = ExitStack()

    def sbB(name, shape, dt=F32):
        t = pB.enter_context(nc.sbuf_tensor(name, list(shape), dt))
        return t, Buf()

    def psB(name, shape, dt=F32):
        t = pB.enter_context(nc.psum_tensor(name, list(shape), dt))
        return t, Buf()

    x1s = [sbB(f"x1s{i}", (128, QM, TB), BF16) for i in range(4)]
    vqs = [sbB(f"vq{i}", (128, QM, TB), BF16) for i in range(2)]
    gts = [sbB(f"gt{i}", (128, QH, TB), BF16) for i in range(2)]
    fts, fts_b = sbB("fts", (128, QM, 2 * TB), BF16)
    fbA = [sbB("fbA0", (128, 2 * TB), BF16)]

    def qset(tag, w):
        return (sbB("zsc" + tag, (128, w * TB)),
                sbB("qw1" + tag, (128, w * TB), BF16),
                sbB("qw2" + tag, (128, w * TB), BF16),
                sbB("qdd" + tag, (128, w * TB), BF16))

    qsetA, qsetB2 = qset("A", 2), qset("B", 4)
    m1A = sbB("m1A", (128, TB), BF16)
    m1B = sbB("m1B", (128, TB), BF16)
    colmA = sbB("colmA", (128, 10))
    colmB = sbB("colmB", (128, 10))
    colbfA = sbB("colbfA", (128, 8), BF16)
    colbfB = sbB("colbfB", (128, 8), BF16)
    rows4A = sbB("rows4A", (4, TB), BF16)
    rows4B = sbB("rows4B", (4, TB), BF16)
    alf = [sbB(f"alf{i}", (128, TB)) for i in range(2)]
    gsc = [sbB(f"gsc{i}", (128, TB)) for i in range(2)]
    gsf = [sbB(f"gsf{i}", (128, TB)) for i in range(2)]
    btf, btf_b = sbB("btf", (128, TB))
    t1qB = [sbB(f"t1qB{i}", (128, TB)) for i in range(2)]
    ftq = [sbB(f"ftq{i}", (128, TB)) for i in range(2)]
    x2o = [sbB(f"x2o{i}", (128, TB)) for i in range(2)]
    rowsB1 = sbB("rowsB1", (1, 6 * TB))
    lbfB1 = sbB("lbfB1", (1, 4 * TB), BF16)
    rowsBF = sbB("rowsBF", (1, 6 * TB))
    lbfF = sbB("lbfF", (1, 4 * TB), BF16)
    wq1r, wq1r_b = sbB("wq1r", (128, QM * QH * 128), BF16)
    wq2s = [sbB(f"wq2s{i}", (128, QH * 128), BF16) for i in range(2)]

    pm1a = psB("pm1a", (128, TB))
    pm1b = psB("pm1b", (128, TB))
    pm2a = psB("pm2a", (128, TB))
    pm2b = psB("pm2b", (128, TB))
    pbcA = psB("pbcA", (128, 2 * TB))
    pbcB = psB("pbcB", (128, 2 * TB))
    ptrA = psB("ptrA", (128, 2 * TB))
    ptrB = psB("ptrB", (128, 2 * TB))
    stA = (ptrA[0], ptrA[1])

    for j in range(QH):
        b.dma(lambda y, j=j: y.dma_start(
            out=wq1r[:, j * QM * 128:(j + 1) * QM * 128], in_=wq1_d[j]),
            writes=[wq1r_b])

    def ln_finishB(stat_ap, stat_buf, rowsp, lbfp, fused):
        rows, rows_b = rowsp
        lbf, lbf_b = lbfp
        R = lambda i: rows[:, i * TB:(i + 1) * TB]
        H = lambda i: lbf[:, i * TB:(i + 1) * TB]
        RW = [rows_b]
        nfeat = float(QM * 128)
        b.op("v", lambda v: v.tensor_copy(rows[:, 0:2 * TB], stat_ap),
             reads=[stat_buf], writes=RW)
        b.op("v", lambda v: v.tensor_scalar(R(2), R(0), 1.0 / nfeat, None, OP.mult),
             reads=RW, writes=RW)
        b.op("v", lambda v: v.tensor_tensor(R(4), R(2), R(2), OP.mult),
             reads=RW, writes=RW)
        b.op("v", lambda v: v.scalar_tensor_tensor(R(5), R(1), 1.0 / nfeat, R(4),
                                                   OP.mult, OP.subtract),
             reads=RW, writes=RW)
        if fused:
            b.op("v", lambda v: v.tensor_scalar(R(1), R(5), 1.0 + EPS, EPS * EPS,
                                                OP.mult, OP.add),
                 reads=RW, writes=RW)
        else:
            b.op("v", lambda v: v.tensor_scalar(R(1), R(5), EPS, None, OP.add),
                 reads=RW, writes=RW)
        b.op("v", lambda v: v.reciprocal(R(4), R(1)), reads=RW, writes=RW)
        b.op("s", lambda s: s.activation(R(0), R(4), AF.Sqrt),
             reads=RW, writes=RW)
        b.op("s", lambda s: s.copy(H(0), R(0)), reads=RW, writes=[lbf_b])
        b.op("v", lambda v: v.tensor_tensor(H(1), R(0), H(0), OP.subtract),
             reads=RW + [lbf_b], writes=[lbf_b])
        b.op("v", lambda v: v.tensor_tensor(R(3), R(2), R(0), OP.mult),
             reads=RW, writes=RW)
        b.op("s", lambda s: s.copy(H(2), R(3)), reads=RW, writes=[lbf_b])
        b.op("v", lambda v: v.tensor_tensor(H(3), R(3), H(2), OP.subtract),
             reads=RW + [lbf_b], writes=[lbf_b])

    def bcastB2(lbfp, off, psum_ap, psum_buf):
        lbf, lbf_b = lbfp
        g = GAcc(b, psum_ap, psum_buf, 2)
        g.add(orow_bf[:], lbf[:, off * TB:(off + 1) * TB], [lbf_b, orow_bf_b])
        g.add(orow_bf[:], lbf[:, (off + 1) * TB:(off + 2) * TB], [lbf_b, orow_bf_b])

    def absmaxB(src_ap, src_rds, m1p, colmp, cbfp, r4p, ptrp, swi_pair):
        m1, m1_b = m1p
        colm, colm_b = colmp
        cbf, cbf_b = cbfp
        r4, r4_b = r4p
        ptr_t, ptr_b2 = ptrp
        swi, swi_b = swi_pair
        b.op("v", lambda v: v.tensor_reduce(
            m1[:], src_ap.rearrange("p q t -> p t q"), AX.X, OP.max,
            apply_absolute_value=True), reads=list(src_rds), writes=[m1_b])
        yield
        for g_ in range(TB // 128):
            GAcc(b, ptr_t[:, g_ * 128:(g_ + 1) * 128], ptr_b2, 1).add(
                m1[:, g_ * 128:(g_ + 1) * 128], eye_bf[:], [m1_b, eye_bf_b])
            b.op("v", lambda v, g_=g_: v.tensor_reduce(
                colm[:, g_:g_ + 1], ptr_t[:, g_ * 128:(g_ + 1) * 128], AX.X, OP.max),
                reads=[ptr_b2], writes=[colm_b])
        yield
        b.op("v", lambda v: v.tensor_scalar(colm[:, 2:4], colm[:, 0:2], 1e-5, None,
                                            OP.max), reads=[colm_b], writes=[colm_b])
        b.op("v", lambda v: v.reciprocal(colm[:, 4:6], colm[:, 2:4]),
             reads=[colm_b], writes=[colm_b])
        b.op("v", lambda v: v.tensor_scalar(colm[:, 6:8], colm[:, 4:6], 127.0, None,
                                            OP.mult), reads=[colm_b], writes=[colm_b])
        b.op("v", lambda v: v.tensor_scalar(colm[:, 8:10], colm[:, 2:4], swi[:],
                                            1.0 / 127.0, OP.mult, OP.mult),
             reads=[colm_b, swi_b], writes=[colm_b])
        cb3 = cbf[:].rearrange("p (g k) -> p g k", k=4)
        yield
        b.op("v", lambda v: v.tensor_copy(cb3[:, :, 0:1], colm[:, 6:8].unsqueeze(2)),
             reads=[colm_b], writes=[cbf_b])
        b.op("v", lambda v: v.tensor_tensor(cb3[:, :, 1:2], colm[:, 6:8].unsqueeze(2),
                                            cb3[:, :, 0:1], OP.subtract),
             reads=[colm_b, cbf_b], writes=[cbf_b])
        b.op("v", lambda v: v.tensor_copy(cb3[:, :, 2:3], colm[:, 8:10].unsqueeze(2)),
             reads=[colm_b], writes=[cbf_b])
        b.op("v", lambda v: v.tensor_tensor(cb3[:, :, 3:4], colm[:, 8:10].unsqueeze(2),
                                            cb3[:, :, 2:3], OP.subtract),
             reads=[colm_b, cbf_b], writes=[cbf_b])
        for g_ in range(TB // 128):
            GAcc(b, ptr_t[0:4, 256 + g_ * 128:256 + (g_ + 1) * 128], ptr_b2, 1).add(
                cbf[:, g_ * 4:(g_ + 1) * 4], eye_bf[:], [cbf_b, eye_bf_b])
        b.op("v", lambda v: v.tensor_copy(r4[:, 0:TB], ptr_t[0:4, 256:256 + TB]),
             reads=[ptr_b2], writes=[r4_b])

    def selb4(r4p, off, psum_ap, psum_buf):
        r4, r4_b = r4p
        g = GAcc(b, psum_ap, psum_buf, 2)
        g.add(selbf[0:4, off * 128:(off + 1) * 128], r4[:], [r4_b, selbf_b])
        g.add(selbf[0:4, (off + 1) * 128:(off + 2) * 128], r4[:], [r4_b, selbf_b])

    def quantB(src3, src_rds, s_ap, s_buf, out3, out_buf, qs):
        (zs, zs_b), (q1, q1_b), (q2, q2_b), (qd, qd_b) = qs
        w = src3.shape[1]
        z3 = zs[:].rearrange("p (q t) -> p q t", t=TB)
        q13 = q1[:].rearrange("p (q t) -> p q t", t=TB)
        qd3 = qd[:].rearrange("p (q t) -> p q t", t=TB)
        b.op("v", lambda v: v.tensor_tensor(
            z3, src3, s_ap.unsqueeze(1).broadcast_to([128, w, TB]), OP.mult),
            reads=list(src_rds) + [s_buf], writes=[zs_b])
        b.op("v", lambda v: v.tensor_scalar(q1[:], zs[:], M32, -M32, OP.add,
                                            OP.add), reads=[zs_b], writes=[q1_b])
        b.op("v", lambda v: v.tensor_scalar(q2[:], zs[:], M64, -M64, OP.add,
                                            OP.add), reads=[zs_b], writes=[q2_b])
        b.op("v", lambda v: v.tensor_tensor(qd[:], q2[:], q1[:], OP.subtract),
             reads=[q1_b, q2_b], writes=[qd_b])
        b.op("v", lambda v: v.scalar_tensor_tensor(out3, qd3, ublend[:], q13,
                                                   OP.mult, OP.add),
             reads=[qd_b, q1_b, ublend_b], writes=[out_buf])

    unitsB = [(bb, t) for t in range(NTB) for bb in range(BL)]
    NU = len(unitsB)

    def dma_x1(u):
        bi, ti = unitsB[u]
        t0 = ti * TB
        x1, x1_b = x1s[u % 4]
        b.dma(lambda y, bi=bi, t0=t0, x1=x1: y.dma_start(
            out=x1[:], in_=x1_d[bi, :, t0:t0 + TB].rearrange("(q p) t -> p q t", p=128)),
            writes=[x1_b])

    def gen_P1(u):
        x1, x1_b = x1s[u % 4]
        vq, vq_b = vqs[u % 2]
        al, al_b = alf[u % 2]
        g1 = GAcc(b, stA[0][0:1, 0:2 * TB], stA[1], QM)
        for q in range(QM):
            fb, fb_b = fbA[0]
            b.op("s", lambda s, q=q, fb=fb: s.copy(fb[:, 0:TB], x1[:, q, :]),
                 reads=[x1_b], writes=[fb_b])
            b.op("s", lambda s, q=q, fb=fb: s.activation(
                fb[:, TB:2 * TB], x1[:, q, :], AF.Square),
                reads=[x1_b], writes=[fb_b])
            g1.add(ocol_bf[:], fb[:], [fb_b, ocol_bf_b])
            if q % 2 == 1:
                yield
        ln_finishB(stA[0][0:1, 0:2 * TB], stA[1], rowsB1, lbfB1, True)
        yield
        bcastB2(lbfB1, 0, pbcA[0][:, 0:TB], pbcA[1])
        bcastB2(lbfB1, 2, pbcA[0][:, TB:2 * TB], pbcA[1])
        yield
        zsA, zsA_b = qsetA[0]
        for q in range(QM):
            tq, tq_b = t1qB[q % 2]
            t2 = zsA[:, (q % 2) * TB:(q % 2) * TB + TB]
            b.op("v", lambda v, q=q, tq=tq: v.tensor_tensor(
                tq[:], x1[:, q, :], pbcA[0][:, 0:TB], OP.mult),
                reads=[x1_b, pbcA[1]], writes=[tq_b])
            b.op("v", lambda v, q=q, tq=tq, t2=t2: v.tensor_tensor(
                t2, tq[:], pbcA[0][:, TB:2 * TB], OP.subtract),
                reads=[tq_b, pbcA[1]], writes=[zsA_b])
            b.op("s", lambda s, q=q, t2=t2: s.activation(
                vq[:, q, :], t2, AF.Identity,
                scale=gam[:, q:q + 1], bias=bet[:, q:q + 1]),
                reads=[zsA_b, gam_b, bet_b], writes=[vq_b])
            if q % 2 == 1:
                yield
        if dbg and u == 0:
            tap("v", vq[:].rearrange("p q t -> p (q t)"), (128, QM * TB), [vq_b], BF16)
        yield from absmaxB(vq[:], [vq_b], m1A, colmA, colbfA, rows4A, ptrA,
                           (sw1i_vec, sw1i_vec_b))
        yield
        selb4(rows4A, 0, pbcA[0][:, 0:TB], pbcA[1])
        selb4(rows4A, 2, pbcA[0][:, TB:2 * TB], pbcA[1])
        yield
        for h in range(4):
            quantB(vq[:, 2 * h:2 * h + 2, :], [vq_b], pbcA[0][:, 0:TB], pbcA[1],
                   vq[:, 2 * h:2 * h + 2, :], vq_b, qsetA)
            yield
        b.op("v", lambda v, al=al: v.tensor_copy(al[:], pbcA[0][:, TB:2 * TB]),
             reads=[pbcA[1]], writes=[al_b])

    def gen_P2(u):
        vq, vq_b = vqs[u % 2]
        al, al_b = alf[u % 2]
        gt, gt_b = gts[u % 2]
        b.op("v", lambda v: v.tensor_tensor(
            vq[:], vq[:], al[:].unsqueeze(1).broadcast_to([128, QM, TB]), OP.mult),
            reads=[al_b], writes=[vq_b])
        for j in range(QH):
            pp, pp_b = pm1a if j % 2 == 0 else pm1b
            g = GAcc(b, pp[:], pp_b, QM)
            for k in range(QM):
                g.add(wq1r[:, (j * QM + k) * 128:(j * QM + k + 1) * 128],
                      vq[:, k, :], [wq1r_b, vq_b])
            gs, gs_b = gsc[j % 2]
            b.op("s", lambda s, pp=pp, gs=gs: s.copy(gs[:], pp[:]),
                 reads=[pp_b], writes=[gs_b])
            b.op("s", lambda s, j=j, gs=gs, gt=gt: s.activation(
                gt[:, j, :], gs[:], AF.Gelu, bias=b1v[:, j:j + 1]),
                reads=[gs_b, b1v_b], writes=[gt_b])
            yield
        if dbg and u == 0:
            tap("g", gt[:].rearrange("p q t -> p (q t)"), (128, QH * TB), [gt_b], BF16)

    def gen_P3(u):
        bi, ti = unitsB[u]
        t0 = ti * TB
        x1, x1_b = x1s[u % 4]
        gt, gt_b = gts[u % 2]
        ws0, ws0_b = wq2s[0]
        b.dma(lambda y, ws0=ws0: y.dma_start(out=ws0[:], in_=wq2_d[0]),
              writes=[ws0_b])
        yield from absmaxB(gt[:], [gt_b], m1B, colmB, colbfB, rows4B, ptrB,
                           (sw2i_vec, sw2i_vec_b))
        yield
        selb4(rows4B, 0, pbcB[0][:, 0:TB], pbcB[1])
        selb4(rows4B, 2, pbcB[0][:, TB:2 * TB], pbcB[1])
        b.op("v", lambda v: v.tensor_copy(btf[:], pbcB[0][:, TB:2 * TB]),
             reads=[pbcB[1]], writes=[btf_b])
        yield
        for h in range(8):
            quantB(gt[:, 4 * h:4 * h + 4, :], [gt_b], pbcB[0][:, 0:TB], pbcB[1],
                   gt[:, 4 * h:4 * h + 4, :], gt_b, qsetB2)
            yield
        gf = GAcc(b, ptrB[0][0:1, 0:2 * TB], ptrB[1], QM)
        for j in range(QM):
            if j + 1 < QM:
                wsn, wsn_b = wq2s[(j + 1) % 2]
                b.dma(lambda y, j=j, wsn=wsn: y.dma_start(
                    out=wsn[:], in_=wq2_d[j + 1]), writes=[wsn_b])
            ws, ws_b = wq2s[j % 2]
            pp, pp_b = pm2a if j % 2 == 0 else pm2b
            g = GAcc(b, pp[:], pp_b, QH)
            for k in range(QH):
                g.add(ws[:, k * 128:(k + 1) * 128], gt[:, k, :], [ws_b, gt_b])
            gs, gs_b = gsf[j % 2]
            b.op("v", lambda v, pp=pp, gs=gs: v.tensor_tensor(
                gs[:], pp[:], btf[:], OP.mult),
                reads=[pp_b, btf_b], writes=[gs_b])
            b.op("s", lambda s, j=j, gs=gs: s.activation(
                fts[:, j, 0:TB], gs[:], AF.Identity, bias=b2v[:, j:j + 1]),
                reads=[gs_b, b2v_b], writes=[fts_b])
            b.op("s", lambda s, j=j: s.activation(
                fts[:, j, TB:2 * TB], fts[:, j, 0:TB], AF.Square),
                reads=[fts_b], writes=[fts_b])
            gf.add(ocol_bf[:], fts[:, j, :], [fts_b, ocol_bf_b])
            yield
        ln_finishB(ptrB[0][0:1, 0:2 * TB], ptrB[1], rowsBF, lbfF, False)
        yield
        bcastB2(lbfF, 0, pbcB[0][:, 0:TB], pbcB[1])
        bcastB2(lbfF, 2, pbcB[0][:, TB:2 * TB], pbcB[1])
        yield
        for q in range(QM):
            tq, tq_b = ftq[q % 2]
            t2, t2_b = x2o[q % 2]
            b.op("v", lambda v, q=q, tq=tq: v.tensor_tensor(
                tq[:], fts[:, q, 0:TB], pbcB[0][:, 0:TB], OP.mult),
                reads=[fts_b, pbcB[1]], writes=[tq_b])
            b.op("v", lambda v, q=q, tq=tq, t2=t2: v.tensor_tensor(
                t2[:], tq[:], x1[:, q, :], OP.add),
                reads=[tq_b, x1_b], writes=[t2_b])
            b.op("v", lambda v, q=q, tq=tq, t2=t2: v.tensor_tensor(
                tq[:], t2[:], pbcB[0][:, TB:2 * TB], OP.subtract),
                reads=[t2_b, pbcB[1]], writes=[tq_b])
            b.dma(lambda y, bi=bi, t0=t0, q=q, tq=tq: y.dma_start(
                out=out_d[bi, 128 * q:128 * (q + 1), t0:t0 + TB], in_=tq[:]),
                reads=[tq_b])
            if q % 2 == 1:
                yield

    def step(g, n=1):
        if g is None:
            return None
        try:
            for _ in range(n):
                next(g)
            return g
        except StopIteration:
            return None

    dma_x1(0)
    dma_x1(1)
    for c in range(NU + 2):
        p2 = gen_P2(c - 1) if 0 <= c - 1 < NU else None
        p3 = gen_P3(c - 2) if 0 <= c - 2 < NU else None
        p1 = gen_P1(c) if c < NU else None
        p3 = step(p3)
        p2 = step(p2, 12)
        p3 = step(p3, 3)
        p1_head = 6
        while p3 is not None or p2 is not None or p1 is not None:
            if p3 is not None:
                p3 = step(p3)
            if p2 is not None:
                p2 = step(p2)
            if p1 is not None and (p1_head > 0 or p2 is None):
                p1 = step(p1)
                p1_head -= 1
        if c + 2 < NU:
            dma_x1(c + 2)

    b.barrier()
    pB.close()
    es.close()
    return nc


_NC_CACHE = {}


def kernel(**inputs):
    g = _host_prep(inputs)
    if "nc" not in _NC_CACHE:
        _NC_CACHE["nc"] = build_nc()
    nc = _NC_CACHE["nc"]
    base = {k: v for k, v in g.items() if k != "xT"}
    in_maps = []
    for c in range(8):
        m = dict(base)
        m["x"] = np.ascontiguousarray(g["xT"][2 * c:2 * c + 2])
        in_maps.append(m)
    res = run_bass_kernel_spmd(nc, in_maps, core_ids=list(range(8)))
    outs = [np.asarray(res.results[c]["out"]) for c in range(8)]
    full = np.concatenate(outs, axis=0)
    return np.ascontiguousarray(full.transpose(0, 2, 1)).astype(np.float32)
